# version 05
# baseline (speedup 1.0000x reference)
import contextlib
import numpy as np
import ml_dtypes

import concourse.bass as bass
import concourse.mybir as mybir
from concourse.bass_utils import run_bass_kernel_spmd
from concourse.alu_op_type import AluOpType

F32 = mybir.dt.float32
BF16 = mybir.dt.bfloat16
I32 = mybir.dt.int32
AF = mybir.ActivationFunctionType
NB = np.dtype(ml_dtypes.bfloat16)

D = 4096
T = 1024
S = 4096
NT = 8
DC = 32
H = 32
KVH = 8
RSQ = float(1.0 / np.sqrt(np.float32(128.0)))

ENGS = ("tensor", "vector", "scalar", "sync", "gpsimd")


class Prog:

    def __init__(self):
        self.prog = {e: [] for e in ENGS}
        self.cnt = {}
        self.ni = {e: 0 for e in ENGS}

    def ev(self, sem, by):
        self.cnt[sem] = self.cnt.get(sem, 0) + by
        return (sem, self.cnt[sem])

    def op(self, eng, make, waits=(), inc=None, by=1):
        waits = tuple(w for w in waits if w is not None)
        e = self.ev(inc, by) if inc else None
        self.ni[eng] += 1 + len(waits)

        def f(engobj, sems, _w=waits, _m=make, _e=e, _by=by):
            for (s, v) in _w:
                engobj.wait_ge(sems[s], v)
            ins = _m(engobj)
            if _e is not None:
                ins.then_inc(sems[_e[0]], _by)
        self.prog[eng].append(f)
        return e

    def wait(self, eng, *evs):
        evs = tuple(e for e in evs if e is not None)
        if not evs:
            return
        self.ni[eng] += len(evs)

        def f(engobj, sems, _w=evs):
            for (s, v) in _w:
                engobj.wait_ge(sems[s], v)
        self.prog[eng].append(f)


def build_nc(debug=False):
    nc = bass.Bass("TRN2", target_bir_lowering=False)

    x_loc = nc.dram_tensor("x_loc", [T, D], F32, kind="ExternalInput")
    wqT = nc.dram_tensor("wqT", [D, D], F32, kind="ExternalInput")
    wkvT = nc.dram_tensor("wkvT", [D, 2048], F32, kind="ExternalInput")
    woT = nc.dram_tensor("woT", [D, D], F32, kind="ExternalInput")
    codesT = nc.dram_tensor("codesT", [D, D], I32, kind="ExternalInput")
    scexpT = nc.dram_tensor("scexpT", [D, D], F32, kind="ExternalInput")
    qsc_row = nc.dram_tensor("qsc_row", [1, D], F32, kind="ExternalInput")
    kvsc_row = nc.dram_tensor("kvsc_row", [1, 2048], F32, kind="ExternalInput")
    osc_row = nc.dram_tensor("osc_row", [1, D], F32, kind="ExternalInput")
    bias_row = nc.dram_tensor("bias_row", [1, D], F32, kind="ExternalInput")
    identity = nc.dram_tensor("identity", [128, 128], BF16, kind="ExternalInput")
    identity_f = nc.dram_tensor("identity_f", [128, 128], F32, kind="ExternalInput")
    onesc_d = nc.dram_tensor("onesc_d", [128, 1], BF16, kind="ExternalInput")
    onesr_d = nc.dram_tensor("onesr_d", [1, 128], F32, kind="ExternalInput")

    out_d = nc.dram_tensor("out", [T, D], F32, kind="ExternalOutput")

    qT_dram = nc.dram_tensor("qT_dram", [H, 128, T], BF16)
    KP_COLS = 1040
    kp_in = nc.dram_tensor("kp_in", [T + 1, KP_COLS], BF16)
    kp_out = nc.dram_tensor("kp_out", [4 * (T + 1), KP_COLS], BF16)
    vp_in = nc.dram_tensor("vp_in", [T, 1024], BF16)
    vp_out = nc.dram_tensor("vp_out", [S, 1024], BF16)
    ffn_dram = nc.dram_tensor("ffn_dram", [T, D], F32)

    dbg = {}
    if debug:
        for nm, sh, dt in (("xnT", [128, DC * T], BF16), ("attnT", [128, H * T], BF16),
                           ("negc", [128, 32], F32), ("kn2", [1, 8], F32),
                           ("qn2", [1, 32], F32), ("cvrow", [1, 1024], F32),
                           ("cqk", [128, 32], F32)):
            dbg[nm] = nc.dram_tensor("dbg_" + nm, sh, dt, kind="ExternalOutput")

    P = Prog()
    es = contextlib.ExitStack()
    with es:
        t_ = lambda *a: es.enter_context(nc.sbuf_tensor(*a))
        arenaA = t_("arenaA", [128, DC * 1024], BF16)
        wbuf = t_("wbuf", [128, 32768], BF16)
        scrF = t_("scrF", [128, 4096], F32)
        scrB = t_("scrB", [128, 4096], BF16)
        absB = t_("absB", [128, 4096], BF16)
        stageB = t_("stageB", [128, 4096], BF16)
        stageF = t_("stageF", [128, 2048], F32)
        kn2a = t_("kn2a", [128, 8], F32)
        kn2b = t_("kn2b", [128, 8], F32)
        kn2c = t_("kn2c", [128, 8], F32)
        coF = t_("coF", [128, 1024], F32)
        sb_id = t_("sb_id", [128, 128], BF16)
        sb_idf = t_("sb_idf", [128, 128], F32)
        sb_onesc = t_("sb_onesc", [128, 1], BF16)
        sb_onesr = t_("sb_onesr", [1, 128], F32)
        epsv = t_("epsv", [128, 1], F32)
        lstat = [t_("lstatA", [128, 8], F32), t_("lstatB", [128, 8], F32)]
        lsq = [t_("lsqA", [128, 4], F32), t_("lsqB", [128, 4], F32)]
        rowS = t_("rowS", [1, 512], F32)
        rowC = t_("rowC", [1, 512], F32)
        rowSc = t_("rowSc", [1, 512], F32)
        cvrow = t_("cvrow", [1, 1024], F32)
        zrow = t_("zrow", [1, 512], F32)
        rzrow = t_("rzrow", [1, 512], F32)
        zrow2 = t_("zrow2", [1, 512], F32)
        rzrow2 = t_("rzrow2", [1, 512], F32)
        qn2A = t_("qn2A", [1, 32], F32)
        qn2B = t_("qn2B", [1, 32], F32)
        qn2T = t_("qn2T", [1, 32], F32)
        kngb = t_("kngb", [4, 8], BF16)
        knrow8 = t_("knrow8", [1, 8], F32)
        knloc = t_("knloc", [8, 128], F32)
        knlocm = t_("knlocm", [8, 1], F32)
        knlocb = t_("knlocb", [1, 8], BF16)
        negC = t_("negC", [128, 32], F32)
        ckcols = t_("ckcols", [128, 8], F32)
        cqcols = t_("cqcols", [128, 32], F32)
        cqk = t_("cqk", [128, 32], F32)

        pA = es.enter_context(nc.psum_tensor("pA", [128, 1024], F32))
        pB = es.enter_context(nc.psum_tensor("pB", [128, 1024], F32))
        pC = es.enter_context(nc.psum_tensor("pC", [128, 512], F32))
        pD = es.enter_context(nc.psum_tensor("pD", [128, 512], F32))
        pE = es.enter_context(nc.psum_tensor("pE", [128, 512], F32))
        pF = es.enter_context(nc.psum_tensor("pF", [128, 512], F32))

        semnames = ["pe", "act", "dve", "ds", "dg", "cc"]
        sems = {n: es.enter_context(nc.semaphore(n)) for n in semnames}
        block = es.enter_context(nc.Block())

        xnT = arenaA.ap().rearrange("p (c t) -> p c t", c=DC)
        attnT = arenaA.ap().rearrange("p (c t) -> p c t", c=H)
        wring = wbuf.ap().rearrange("p (r c n) -> p r c n", r=2, c=DC)
        rawv = scrB.ap().rearrange("p (r n) -> p r n", r=2)
        absv = absB.ap().rearrange("p (r n) -> p r n", r=2)
        kjT = wbuf.ap()[:, 0:8192].rearrange("p (r s) -> p r s", r=2)
        vjv = wbuf.ap()[:, 8192:16384].rearrange("p (r s c) -> p r s c", r=2, s=32)
        qthv = wbuf.ap()[:, 16384:18432].rearrange("p (r n) -> p r n", r=2)
        ptv = wbuf.ap()[:, 18432:22528].rearrange("p (r n) -> p r n", r=4)
        qstv = stageB.ap().rearrange("p (r n) -> p r n", r=8)
        xbuf = scrF.ap()
        xeE2 = scrF.ap().rearrange("p (r n) -> p r n", r=8)
        sctF = stageF.ap().rearrange("p (r n) -> p r n", r=4)
        q2v = stageF.ap().bitcast(BF16)[:, 0:512]
        xnb = stageB.ap()
        ptsv = stageB.ap()[:, 0:2048].rearrange("p (r n) -> p r n", r=2)
        paks = [pA.ap()[:, 0:512], pA.ap()[:, 512:1024],
                pB.ap()[:, 0:512], pB.ap()[:, 512:1024]]
        trb = [p.ap().bitcast(BF16) for p in (pC, pD, pE, pF)]

        for dst, src in ((sb_id, identity), (sb_idf, identity_f),
                         (sb_onesc, onesc_d), (sb_onesr, onesr_d)):
            P.op("sync", lambda e, d=dst, s=src: e.dma_start(out=d.ap()[:, :], in_=s.ap()[:, :]),
                 inc="ds", by=16)
        evA = ("ds", P.cnt["ds"])
        ev_eps = P.op("vector", lambda e: e.memset(epsv.ap()[:, :], 1e-5), inc="dve")
        P.op("vector", lambda e: e.memset(qn2A.ap()[:, :], -3.0e38), inc="dve")
        ev_qn2i = P.op("vector", lambda e: e.memset(qn2B.ap()[:, :], -3.0e38), inc="dve")

        ev_xn = [None] * NT
        ev_junk = [None] * NT
        ev_evac = {}
        ev_tp = {}
        for tt in range(NT):
            r = tt % 2
            ev_x = P.op(
                "sync",
                lambda e, tt=tt: e.dma_start(
                    out=xbuf[:, :], in_=x_loc.ap()[tt * 128:(tt + 1) * 128, :]),
                waits=[ev_xn[tt - 1] if tt >= 1 else None,
                       ev_junk[tt - 1] if tt >= 1 else None], inc="ds", by=16)
            ls, lq = lstat[r], lsq[r]
            e1 = P.op("vector", lambda e, ls=ls: e.tensor_reduce(
                out=ls.ap()[:, 0:1], in_=xbuf[:, 0:2048], axis=mybir.AxisListType.X,
                op=AluOpType.add), waits=[ev_x], inc="dve")
            e2 = P.op("vector", lambda e, ls=ls: e.tensor_reduce(
                out=ls.ap()[:, 1:2], in_=xbuf[:, 2048:4096], axis=mybir.AxisListType.X,
                op=AluOpType.add), inc="dve")
            ea1 = P.op("scalar", lambda e, lq=lq: e.activation(
                out=absv[:, 0, :], in_=xbuf[:, 0:2048], func=AF.Square,
                accum_out=lq.ap()[:, 0:1]), waits=[ev_x], inc="act")
            ev_junk[tt] = ea2 = P.op("scalar", lambda e, lq=lq: e.activation(
                out=absv[:, 1, :], in_=xbuf[:, 2048:4096], func=AF.Square,
                accum_out=lq.ap()[:, 1:2]), inc="act")
            e3 = P.op("vector", lambda e, ls=ls: e.tensor_tensor(
                out=ls.ap()[:, 2:3], in0=ls.ap()[:, 0:1], in1=ls.ap()[:, 1:2],
                op=AluOpType.add), waits=[e1, e2], inc="dve")
            e4 = P.op("vector", lambda e, ls=ls: e.tensor_scalar(
                out=ls.ap()[:, 3:4], in0=ls.ap()[:, 2:3], scalar1=1.0 / D, scalar2=None,
                op0=AluOpType.mult), waits=[e3], inc="dve")
            e5 = P.op("vector", lambda e, ls=ls: e.tensor_tensor(
                out=ls.ap()[:, 4:5], in0=ls.ap()[:, 3:4], in1=ls.ap()[:, 3:4],
                op=AluOpType.mult), waits=[e4], inc="dve")
            e6 = P.op("vector", lambda e, lq=lq: e.tensor_tensor(
                out=lq.ap()[:, 2:3], in0=lq.ap()[:, 0:1], in1=lq.ap()[:, 1:2],
                op=AluOpType.add), waits=[ea1, ea2, e5], inc="dve")
            e7 = P.op("vector", lambda e, ls=ls, lq=lq: e.tensor_scalar(
                out=ls.ap()[:, 5:6], in0=lq.ap()[:, 2:3], scalar1=1.0 / D,
                scalar2=ls.ap()[:, 4:5], op0=AluOpType.mult, op1=AluOpType.subtract),
                waits=[e6], inc="dve")
            e8 = P.op("scalar", lambda e, ls=ls: e.activation(
                out=ls.ap()[:, 6:7], in_=ls.ap()[:, 5:6], func=AF.Sqrt,
                bias=epsv.ap()[:, :]), waits=[e7, ev_eps], inc="act")
            e9 = P.op("vector", lambda e, ls=ls: e.reciprocal(
                out=ls.ap()[:, 7:8], in_=ls.ap()[:, 6:7]), waits=[e8], inc="dve")
            P.op("vector", lambda e, ls=ls: e.tensor_scalar(
                out=xnb[:, 0:2048], in0=xbuf[:, 0:2048], scalar1=ls.ap()[:, 3:4],
                scalar2=ls.ap()[:, 7:8], op0=AluOpType.subtract, op1=AluOpType.mult),
                waits=[e9, ev_tp.get((tt - 1, 3))], inc="dve")
            ev_xn[tt] = P.op("vector", lambda e, ls=ls: e.tensor_scalar(
                out=xnb[:, 2048:4096], in0=xbuf[:, 2048:4096], scalar1=ls.ap()[:, 3:4],
                scalar2=ls.ap()[:, 7:8], op0=AluOpType.subtract, op1=AluOpType.mult),
                inc="dve")
            for g in range(4):
                for i in range(8):
                    dc = 8 * g + i
                    ee = P.op("tensor", lambda e, g=g, i=i, dc=dc: e.transpose(
                        trb[g][:, i * 128:(i + 1) * 128],
                        xnb[:, dc * 128:(dc + 1) * 128], sb_id.ap()[:, :]),
                        waits=([ev_xn[tt], ev_evac.get((tt - 1, g)),
                                evA if (tt == 0 and g == 0) else None] if i == 0 else []),
                        inc=("pe" if i == 7 else None))
                    if i == 7:
                        ev_tp[(tt, g)] = ee
                ev_evac[(tt, g)] = P.op("vector", lambda e, tt=tt, g=g: e.tensor_copy(
                    xnT[:, 8 * g:8 * g + 8, tt * 128:(tt + 1) * 128],
                    trb[g].rearrange("p (c t) -> p c t", c=8)),
                    waits=[ev_tp[(tt, g)]], inc="dve")
        ev_B1_last = ev_evac[(NT - 1, 3)]
        ev_B1_junk = ev_junk[NT - 1]

        if debug:
            P.op("sync", lambda e: e.dma_start(out=dbg["xnT"].ap()[:, :], in_=arenaA.ap()[:, :]),
                 waits=[ev_B1_last], inc="ds", by=16)

        st = {
            "raw_free": {}, "abs_free": {}, "graw": 0, "gabs": 0,
            "rowSc_free": None, "pE_free": None,
        }
        st["raw_free"][-1] = st["raw_free"][-2] = None
        st["abs_free"][-1] = st["abs_free"][-2] = None

        def seed_ring_barrier(ev):
            g = st["graw"]
            st["raw_free"][g - 1] = st["raw_free"][g - 2] = ev
            g = st["gabs"]
            st["abs_free"][g - 1] = st["abs_free"][g - 2] = ev

        def wprep_load(src, col0, slot, wslot_free_ev):
            pd = {"abs": [None] * 8, "absga": [None] * 8, "sign": None, "slot": slot}
            for g in range(8):
                rr = g % 2
                G = st["graw"]
                st["graw"] += 1
                ev_dma = P.op("gpsimd", lambda e, g=g, rr=rr: e.dma_start(
                    out=rawv[:, rr, :].rearrange("p (c n) -> p c n", c=4),
                    in_=src.ap().rearrange("(c p) n -> p c n", p=128)[
                        :, 4 * g:4 * g + 4, col0:col0 + 512]),
                    waits=[st["raw_free"].get(G - 2)], inc="dg", by=16)
                GA = st["gabs"]
                st["gabs"] += 1
                pd["absga"][g] = GA
                pd["abs"][g] = P.op("vector", lambda e, rr=rr: e.scalar_tensor_tensor(
                    out=absv[:, rr, :], in0=rawv[:, rr, :], scalar=-1.0,
                    in1=rawv[:, rr, :], op0=AluOpType.mult, op1=AluOpType.max),
                    waits=[ev_dma, st["abs_free"].get(GA - 2)], inc="dve")
                pd["sign"] = P.op("scalar", lambda e, g=g, rr=rr, slot=slot: e.activation(
                    out=wring[:, slot, 4 * g:4 * g + 4, :],
                    in_=rawv[:, rr, :].rearrange("p (c n) -> p c n", c=4), func=AF.Sign),
                    waits=[ev_dma, pd["abs"][g],
                           wslot_free_ev if g == 0 else None], inc="act")
                st["raw_free"][G] = pd["sign"]
            return pd

        def wprep_smm(pd, g):
            rr = g % 2
            ev_s = None
            for i in range(4):
                ev_s = P.op("tensor", lambda e, rr=rr, i=i, g=g: e.matmul(
                    pE.ap()[0:1, 0:512], sb_onesc.ap()[:, :],
                    absv[:, rr, i * 512:(i + 1) * 512],
                    start=(g == 0 and i == 0), stop=(g == 7 and i == 3)),
                    waits=([pd["abs"][g]] if i == 0 else []) +
                          ([st["pE_free"]] if (g == 0 and i == 0) else []),
                    inc="pe")
            st["abs_free"][pd["absga"][g]] = ev_s
            pd["smm"] = ev_s
            return ev_s

        def wprep_finish(pd, scale_src, scale_off):
            ev_sc = P.op("sync", lambda e: e.dma_start(
                out=rowSc.ap()[:, :],
                in_=scale_src.ap()[0:1, scale_off:scale_off + 512]),
                waits=[st["rowSc_free"]], inc="ds", by=16)
            ec1 = P.op("vector", lambda e: e.tensor_copy(rowS.ap()[:, :], pE.ap()[0:1, 0:512]),
                       waits=[pd["smm"]], inc="dve")
            st["pE_free"] = ec1
            ec2 = P.op("vector", lambda e: e.tensor_scalar(
                out=rowS.ap()[:, :], in0=rowS.ap()[:, :], scalar1=1.0 / D, scalar2=1e-8,
                op0=AluOpType.mult, op1=AluOpType.max), waits=[ec1], inc="dve")
            ev_crow = P.op("vector", lambda e: e.tensor_tensor(
                out=rowC.ap()[:, :], in0=rowS.ap()[:, :], in1=rowSc.ap()[:, :],
                op=AluOpType.mult), waits=[ec2, ev_sc, st.get("rowC_free")], inc="dve")
            st["rowSc_free"] = ev_crow
            return ev_crow

        ev_wslot_free = {}
        ev_kst_free = {}
        kst_dma = {}
        ev_kp_dma = []
        ev_vp_dma = []
        ev_pF_read = ev_B1_last
        ev_kn_prev = None
        ev_ck = {}
        ev_cv = {}
        seed_ring_barrier(ev_B1_junk)

        kvpd = {}
        kvpd[0] = wprep_load(wkvT, 0, 0, None)
        for g in range(8):
            wprep_smm(kvpd[0], g)
        crow = wprep_finish(kvpd[0], kvsc_row, 0)
        kv_crow = {0: crow}

        def emit_kv_c(kc, ev_crow):
            nonlocal_state = {}
            if kc < 2:
                evt = None
                for i in range(4):
                    evt = P.op("tensor", lambda e, i=i: e.transpose(
                        pF.ap()[:, i:i + 1], rowC.ap()[0:1, i * 128:(i + 1) * 128],
                        sb_idf.ap()[0:1, 0:1]),
                        waits=([ev_crow, st["pF_read"]] if i == 0 else []),
                        inc=("pe" if i == 3 else None))
                ev_ck[kc] = P.op("vector", lambda e, kc=kc: e.tensor_copy(
                    ckcols.ap()[:, 4 * kc:4 * kc + 4], pF.ap()[:, 0:4]),
                    waits=[evt], inc="dve")
                st["pF_read"] = ev_ck[kc]
                st["rowC_free"] = ev_ck[kc]
            else:
                ev_cv[kc] = P.op("vector", lambda e, kc=kc: e.tensor_copy(
                    cvrow.ap()[0:1, (kc - 2) * 512:(kc - 1) * 512], rowC.ap()[:, :]),
                    waits=[ev_crow], inc="dve")
                st["rowC_free"] = ev_cv[kc]

        st["pF_read"] = ev_pF_read
        emit_kv_c(0, crow)
        for kc in range(4):
            slot = kc % 2
            ev_sign = kvpd[kc]["sign"]
            if kc + 1 < 4:
                kvpd[kc + 1] = wprep_load(wkvT, (kc + 1) * 512, (kc + 1) % 2,
                                          ev_wslot_free.get((kc + 1) % 2))
            elif True:
                kvpd[4] = wprep_load(wqT, 0, 0, ev_wslot_free.get(0))
            mm = None
            for tt in range(NT):
                idx = kc * NT + tt
                pk = paks[idx % 4]
                for dc in range(DC):
                    mm = P.op("tensor", lambda e, dc=dc, tt=tt, slot=slot, pk=pk: e.matmul(
                        pk, xnT[:, dc, tt * 128:(tt + 1) * 128], wring[:, slot, dc, :],
                        start=(dc == 0), stop=(dc == DC - 1)),
                        waits=([ev_sign, ev_B1_last, ev_kst_free.get(idx - 4)]
                               if dc == 0 else []),
                        inc=("pe" if dc == DC - 1 else None))
                wprep_smm(kvpd[kc + 1], tt)
                kr = idx % 4
                ev_ev = P.op("vector", lambda e, kr=kr, pk=pk: e.tensor_copy(
                    qstv[:, kr, :], pk), waits=[mm, kst_dma.get(idx - 4)], inc="dve")
                ev_kst_free[idx] = ev_ev
                if kc < 2:
                    ev_d = P.op("sync", lambda e, kr=kr, kc=kc, tt=tt: e.dma_start(
                        out=kp_in.ap()[tt * 128:(tt + 1) * 128, kc * 512:kc * 512 + 512],
                        in_=qstv[:, kr, :]), waits=[ev_ev], inc="ds", by=16)
                    ev_kp_dma.append(ev_d)
                    kst_dma[idx] = ev_d
                    eq = P.op("vector", lambda e, kr=kr: e.tensor_tensor(
                        out=q2v, in0=qstv[:, kr, :], in1=qstv[:, kr, :],
                        op=AluOpType.mult), waits=[ev_ev], inc="dve")
                    er = None
                    for i in range(4):
                        er = P.op("vector", lambda e, i=i, kc=kc: e.tensor_reduce(
                            out=kn2a.ap()[:, 4 * kc + i:4 * kc + i + 1],
                            in_=q2v[:, i * 128:(i + 1) * 128],
                            axis=mybir.AxisListType.X, op=AluOpType.add),
                            waits=([eq] if i == 0 else []), inc="dve")
                    cols = (4 * kc, 4 * kc + 4)
                    if tt == 0:
                        ev_kn_prev = P.op("vector", lambda e, cols=cols: e.tensor_copy(
                            kn2b.ap()[:, cols[0]:cols[1]], kn2a.ap()[:, cols[0]:cols[1]]),
                            waits=[er], inc="dve")
                    else:
                        src_t = kn2b if tt % 2 == 1 else kn2c
                        dst_t = kn2c if tt % 2 == 1 else kn2b
                        ev_kn_prev = P.op("vector",
                                          lambda e, cols=cols, s=src_t, d=dst_t: e.tensor_tensor(
                            out=d.ap()[:, cols[0]:cols[1]], in0=s.ap()[:, cols[0]:cols[1]],
                            in1=kn2a.ap()[:, cols[0]:cols[1]], op=AluOpType.max),
                            waits=[er, ev_kn_prev], inc="dve")
                else:
                    ev_d = P.op("sync", lambda e, kr=kr, kc=kc, tt=tt: e.dma_start(
                        out=vp_in.ap()[tt * 128:(tt + 1) * 128,
                                       (kc - 2) * 512:(kc - 2) * 512 + 512],
                        in_=qstv[:, kr, :]), waits=[ev_ev], inc="ds", by=16)
                    ev_vp_dma.append(ev_d)
                    kst_dma[idx] = ev_d
            ev_wslot_free[slot] = mm
            if kc + 1 < 4:
                crow = wprep_finish(kvpd[kc + 1], kvsc_row, (kc + 1) * 512)
                emit_kv_c(kc + 1, crow)
            else:
                q_crow0 = wprep_finish(kvpd[4], qsc_row, 0)
        ev_pF_read = st["pF_read"]

        knfin = kn2c
        et1 = P.op("tensor", lambda e: e.transpose(
            pF.ap()[0:8, 0:128], knfin.ap()[:, 0:8], sb_idf.ap()[:, :]),
            waits=[ev_kn_prev, ev_pF_read], inc="pe")
        et2 = P.op("vector", lambda e: e.tensor_copy(knloc.ap()[:, :], pF.ap()[0:8, 0:128]),
                   waits=[et1], inc="dve")
        et3 = P.op("vector", lambda e: e.tensor_reduce(
            out=knlocm.ap()[:, :], in_=knloc.ap()[:, :], axis=mybir.AxisListType.X,
            op=AluOpType.max), waits=[et2], inc="dve")
        et4 = P.op("tensor", lambda e: e.transpose(
            pF.ap()[0:1, 0:8], knlocm.ap()[:, :], sb_idf.ap()[0:8, 0:8]),
            waits=[et3], inc="pe")
        et5 = P.op("vector", lambda e: e.tensor_copy(knlocb.ap()[:, :], pF.ap()[0:1, 0:8]),
                   waits=[et4], inc="dve")
        ev_pF_read = et5
        ev_knrow_dma = P.op("sync", lambda e: e.dma_start(
            out=kp_in.ap()[T:T + 1, 0:8], in_=knlocb.ap()[:, :]),
            waits=[et5], inc="ds", by=16)

        P.wait("gpsimd", ev_kp_dma[-1], ev_knrow_dma)
        ev_ccK = P.op("gpsimd", lambda e: e.collective_compute(
            "AllGather", mybir.AluOpType.bypass,
            replica_groups=[[0, 1, 2, 3], [4, 5, 6, 7]],
            ins=[kp_in.ap().opt()], outs=[kp_out.ap().opt()]), inc="cc")
        P.wait("gpsimd", ev_vp_dma[-1])
        ev_ccV = P.op("gpsimd", lambda e: e.collective_compute(
            "AllGather", mybir.AluOpType.bypass,
            replica_groups=[[0, 1, 2, 3], [4, 5, 6, 7]],
            ins=[vp_in.ap().opt()], outs=[vp_out.ap().opt()]), inc="cc")

        ev_slot_dma = {}
        c_act_evac = {}
        ev_erm_prev = None
        ev_qT_dma_last = None
        qpd = {0: kvpd[4]}
        ecqk_map = {}

        def emit_q_c(qc, ev_crow):
            evt = None
            for i in range(4):
                evt = P.op("tensor", lambda e, i=i: e.transpose(
                    pF.ap()[:, i:i + 1], rowC.ap()[0:1, i * 128:(i + 1) * 128],
                    sb_idf.ap()[0:1, 0:1]),
                    waits=([ev_crow, st["pF_read"], ev_erm_prev] if i == 0 else []),
                    inc=("pe" if i == 3 else None))
            ecq = P.op("vector", lambda e, qc=qc: e.tensor_copy(
                cqcols.ap()[:, 4 * qc:4 * qc + 4], pF.ap()[:, 0:4]), waits=[evt], inc="dve")
            st["pF_read"] = ecq
            st["rowC_free"] = ecq
            return P.op("vector", lambda e, qc=qc: e.tensor_scalar(
                out=cqk.ap()[:, 4 * qc:4 * qc + 4], in0=cqcols.ap()[:, 4 * qc:4 * qc + 4],
                scalar1=ckcols.ap()[:, qc:qc + 1], scalar2=None, op0=AluOpType.mult),
                waits=[ecq, ev_ck.get(0), ev_ck.get(1)], inc="dve")

        st["pF_read"] = ev_pF_read
        ecqk_map[0] = emit_q_c(0, q_crow0)
        for qc in range(8):
            slot = qc % 2
            ev_sign = qpd[qc]["sign"]
            ecqk = ecqk_map[qc]
            if qc + 1 < 8:
                qpd[qc + 1] = wprep_load(wqT, (qc + 1) * 512, (qc + 1) % 2,
                                         ev_wslot_free.get((qc + 1) % 2))
            mm = None
            for ms in range(4):
                m = 4 * qc + ms
                for th in range(2):
                    idx = m * 2 + th
                    pk = paks[idx % 4]
                    for dc in range(DC):
                        mm = P.op("tensor", lambda e, dc=dc, slot=slot, ms=ms, th=th, pk=pk:
                                  e.matmul(pk, wring[:, slot, dc, ms * 128:(ms + 1) * 128],
                                           xnT[:, dc, th * 512:(th + 1) * 512],
                                           start=(dc == 0), stop=(dc == DC - 1)),
                                  waits=([ev_sign, ev_slot_dma.get(idx - 4),
                                          ev_kst_free.get(28 + idx) if idx < 4 else None]
                                         if dc == 0 else []),
                                  inc=("pe" if dc == DC - 1 else None))
                    if qc + 1 < 8:
                        wprep_smm(qpd[qc + 1], ms * 2 + th)
                    esq = P.op("scalar", lambda e, m=m, pk=pk, ecqk=ecqk: e.activation(
                        out=q2v, in_=pk, func=AF.Square,
                        scale=cqk.ap()[:, m:m + 1]),
                        waits=[mm, ecqk,
                               et3 if idx == 0 else None,
                               ev_erm_prev if idx > 0 else None], inc="act")
                    kr = idx % 8
                    eev = P.op("scalar", lambda e, kr=kr, m=m, pk=pk: e.activation(
                        out=qstv[:, kr, :], in_=pk, func=AF.Copy,
                        scale=cqk.ap()[:, m:m + 1]),
                        waits=[ev_slot_dma.get(idx - 8),
                               ev_vp_dma[-1] if idx == 0 else None,
                               kst_dma.get(4 * NT - 4 + idx)], inc="act")
                    ed = P.op("sync", lambda e, kr=kr, m=m, th=th: e.dma_start(
                        out=qT_dram.ap()[m, :, th * 512:(th + 1) * 512], in_=qstv[:, kr, :]),
                        waits=[eev], inc="ds", by=16)
                    ev_slot_dma[idx] = ed
                    ev_qT_dma_last = ed
                    c_act_evac[idx] = eev
                    eqm = P.op("tensor", lambda e: e.matmul(
                        pF.ap()[0:1, 0:512], sb_onesc.ap()[:, :], q2v,
                        start=True, stop=True),
                        waits=[esq, st["pF_read"] if idx == 0 else None,
                               ev_erm_prev if idx > 0 else None], inc="pe")
                    dst = qn2A if th == 0 else qn2B
                    ev_erm_prev = P.op("vector", lambda e, m=m, dst=dst: e.tensor_reduce(
                        out=dst.ap()[0:1, m:m + 1], in_=pF.ap()[0:1, 0:512],
                        axis=mybir.AxisListType.X, op=AluOpType.max),
                        waits=[eqm, ev_qn2i], inc="dve")
            ev_wslot_free[slot] = mm
            if qc + 1 < 8:
                crow = wprep_finish(qpd[qc + 1], qsc_row, (qc + 1) * 512)
                ecqk_map[qc + 1] = emit_q_c(qc + 1, crow)
        ev_pF_read = ev_erm_prev
        ev_C_qn2 = ev_erm_prev
        ev_pF_read = ev_erm_prev

        ev_fst_free = {}
        fst_dma = {}
        ev_bf_prev = None
        ev_ffn_last = None
        for fc in range(8):
            slot = fc % 2
            ev_stt = None
            for g in range(8):
                rr = g % 2
                G = st["graw"]
                st["graw"] += 1
                evc = P.op("gpsimd", lambda e, g=g, rr=rr, fc=fc: e.dma_start(
                    out=rawv[:, rr, :].rearrange("p (c n) -> p c n", c=4),
                    in_=codesT.ap().rearrange("(c p) n -> p c n", p=128)[
                        :, 4 * g:4 * g + 4, fc * 512:fc * 512 + 512]),
                    waits=[st["raw_free"].get(G - 2)], inc="dg", by=16)
                GA = st["gabs"]
                st["gabs"] += 1
                evs = P.op("gpsimd", lambda e, g=g, rr=rr, fc=fc: e.dma_start(
                    out=absv[:, rr, :].rearrange("p (c n) -> p c n", c=4),
                    in_=scexpT.ap().rearrange("(c p) n -> p c n", p=128)[
                        :, 4 * g:4 * g + 4, fc * 512:fc * 512 + 512]),
                    waits=[st["abs_free"].get(GA - 2)], inc="dg", by=16)
                ev_stt = P.op("vector", lambda e, g=g, rr=rr, slot=slot: e.scalar_tensor_tensor(
                    out=wring[:, slot, 4 * g:4 * g + 4, :],
                    in0=rawv[:, rr, :].rearrange("p (c n) -> p c n", c=4), scalar=128.0,
                    in1=absv[:, rr, :].rearrange("p (c n) -> p c n", c=4),
                    op0=AluOpType.subtract, op1=AluOpType.mult),
                    waits=[evc, evs, ev_wslot_free.get(slot) if g == 0 else None],
                    inc="dve")
                st["raw_free"][G] = ev_stt
                st["abs_free"][GA] = ev_stt
            ev_bs = P.op("sync", lambda e, fc=fc: e.dma_start(
                out=rowSc.ap()[:, :], in_=bias_row.ap()[0:1, fc * 512:fc * 512 + 512]),
                waits=[st["rowSc_free"]], inc="ds", by=16)
            ev_bm = P.op("tensor", lambda e: e.matmul(
                pF.ap()[:, 0:512], sb_onesr.ap()[:, :], rowSc.ap()[:, :],
                start=True, stop=True),
                waits=[ev_bs, ev_bf_prev, ev_pF_read if fc == 0 else None], inc="pe")
            st["rowSc_free"] = ev_bm
            ev_bf_prev = P.op("vector", lambda e, slot=slot: e.tensor_copy(
                coF.ap()[:, slot * 512:slot * 512 + 512], pF.ap()[:, 0:512]),
                waits=[ev_bm], inc="dve")
            mm = None
            for tt in range(NT):
                pk = paks[tt % 4]
                for dc in range(DC):
                    mm = P.op("tensor", lambda e, dc=dc, tt=tt, slot=slot, pk=pk: e.matmul(
                        pk, xnT[:, dc, tt * 128:(tt + 1) * 128], wring[:, slot, dc, :],
                        start=(dc == 0), stop=(dc == DC - 1)),
                        waits=([ev_stt, ev_fst_free.get((fc, tt - 4)),
                                c_act_evac.get(60 + tt) if (fc == 0 and tt < 4) else None]
                               if dc == 0 else []),
                        inc=("pe" if dc == DC - 1 else None))
                kr = tt % 4
                eev = P.op("vector", lambda e, kr=kr, slot=slot, pk=pk: e.scalar_tensor_tensor(
                    out=sctF[:, kr, :], in0=pk, scalar=0.0,
                    in1=coF.ap()[:, slot * 512:slot * 512 + 512],
                    op0=AluOpType.add, op1=AluOpType.add),
                    waits=[mm, fst_dma.get((fc, tt - 4)),
                           fst_dma.get((fc - 1, tt + 4))], inc="dve")
                ed = P.op("sync", lambda e, kr=kr, fc=fc, tt=tt: e.dma_start(
                    out=ffn_dram.ap()[tt * 128:(tt + 1) * 128, fc * 512:fc * 512 + 512],
                    in_=sctF[:, kr, :]), waits=[eev], inc="ds", by=16)
                ev_fst_free[(fc, tt)] = ed
                fst_dma[(fc, tt)] = ed
                ev_ffn_last = ed
            ev_wslot_free[slot] = mm
        ev_E1_done = ev_ffn_last
        ev_pF_read = ev_bf_prev

        eg1 = P.op("sync", lambda e: e.dma_start(
            out=kngb.ap()[:, :],
            in_=kp_out.ap().rearrange("(r t) n -> r t n", r=4)[:, T, 0:8]),
            waits=[ev_ccK], inc="ds", by=16)
        eg2 = P.op("tensor", lambda e: e.transpose(
            trb[3][0:8, 0:4], kngb.ap()[:, :], sb_id.ap()[0:4, 0:4]),
            waits=[eg1, ev_pF_read], inc="pe")
        eg3 = P.op("vector", lambda e: e.tensor_copy(
            knloc.ap()[0:8, 0:4], trb[3][0:8, 0:4]), waits=[eg2], inc="dve")
        eg4 = P.op("vector", lambda e: e.tensor_reduce(
            out=knlocm.ap()[:, :], in_=knloc.ap()[0:8, 0:4], axis=mybir.AxisListType.X,
            op=AluOpType.max), waits=[eg3], inc="dve")
        eg5 = P.op("tensor", lambda e: e.transpose(
            pF.ap()[0:1, 0:8], knlocm.ap()[:, :], sb_idf.ap()[0:8, 0:8]),
            waits=[eg4, eg3], inc="pe")
        eg6 = P.op("vector", lambda e: e.tensor_copy(knrow8.ap()[:, :], pF.ap()[0:1, 0:8]),
                   waits=[eg5], inc="dve")
        en1 = P.op("vector", lambda e: e.tensor_tensor(
            out=qn2T.ap()[:, :], in0=qn2A.ap()[:, :], in1=qn2B.ap()[:, :],
            op=AluOpType.max), waits=[ev_C_qn2], inc="dve")
        en2 = None
        for r in range(4):
            en2 = P.op("vector", lambda e, r=r: e.tensor_tensor(
                out=qn2T.ap().rearrange("o (j r) -> o j r", r=4)[:, :, r],
                in0=qn2T.ap().rearrange("o (j r) -> o j r", r=4)[:, :, r],
                in1=knrow8.ap()[:, :], op=AluOpType.mult),
                waits=([en1, eg6] if r == 0 else []), inc="dve")
        en3 = P.op("vector", lambda e: e.tensor_scalar(
            out=qn2T.ap()[:, :], in0=qn2T.ap()[:, :], scalar1=1e-30, scalar2=None,
            op0=AluOpType.max), waits=[en2], inc="dve")
        en4 = P.op("scalar", lambda e: e.activation(
            out=qn2T.ap()[:, :], in_=qn2T.ap()[:, :], func=AF.Sqrt),
            waits=[en3], inc="act")
        en5 = P.op("vector", lambda e: e.tensor_scalar(
            out=qn2T.ap()[:, :], in0=qn2T.ap()[:, :], scalar1=-RSQ, scalar2=None,
            op0=AluOpType.mult), waits=[en4], inc="dve")
        en6 = P.op("tensor", lambda e: e.matmul(
            pF.ap()[:, 0:32], sb_onesr.ap()[:, :], qn2T.ap()[:, :], start=True, stop=True),
            waits=[en5, eg6], inc="pe")
        ev_negC = P.op("vector", lambda e: e.tensor_copy(negC.ap()[:, :], pF.ap()[:, 0:32]),
                       waits=[en6], inc="dve")
        ev_pF_read = ev_negC
        if debug:
            P.op("sync", lambda e: e.dma_start(out=dbg["negc"].ap()[:, :], in_=negC.ap()[:, :]),
                 waits=[ev_negC], inc="ds", by=16)
            P.op("sync", lambda e: e.dma_start(out=dbg["kn2"].ap()[:, :], in_=knrow8.ap()[:, :]),
                 waits=[eg6], inc="ds", by=16)
            P.op("sync", lambda e: e.dma_start(out=dbg["qn2"].ap()[:, :], in_=qn2T.ap()[:, :]),
                 waits=[en5], inc="ds", by=16)
            P.op("sync", lambda e: e.dma_start(out=dbg["cvrow"].ap()[:, :], in_=cvrow.ap()[:, :]),
                 waits=[ev_cv.get(2), ev_cv.get(3)], inc="ds", by=16)
            P.op("sync", lambda e: e.dma_start(out=dbg["cqk"].ap()[:, :], in_=cqk.ap()[:, :]),
                 waits=[ev_C_qn2], inc="ds", by=16)

        ev_kj = {}
        ev_vj = {}
        ev_qh = {}
        ev_kj_free = {}
        ev_vj_free = {}
        ev_qh_free = {}
        ev_pt_free = {}
        st["pts_free"] = {}
        ev_exp = {}
        ev_tt = {0: None, 1: None}
        ev_ez = {0: None, 1: None}
        ev_eo2 = {0: None, 1: None}
        gsc = 0
        for j in range(KVH):
            rj = j % 2
            for r4 in range(4):
                ev_kj[j] = P.op("sync", lambda e, j=j, rj=rj, r4=r4: e.dma_start_transpose(
                    out=kjT[:, rj, r4 * 1024:(r4 + 1) * 1024],
                    in_=kp_out.ap().rearrange("(r t) n -> r t n", r=4)[
                        r4, 0:1024, j * 128:(j + 1) * 128]),
                    waits=[ev_ccK if (j == 0 and r4 == 0) else None,
                           ev_E1_done if (j < 2 and r4 == 0) else None,
                           ev_kj_free.get(j - 2) if r4 == 0 else None],
                    inc="ds", by=16)
            ev_vj[j] = P.op("sync", lambda e, j=j, rj=rj: e.dma_start(
                out=vjv[:, rj, :, :],
                in_=vp_out.ap().rearrange("(s p) o -> p s o", p=128)[
                    :, :, j * 128:(j + 1) * 128]),
                waits=[ev_ccV if j == 0 else None,
                       ev_E1_done if j < 2 else None,
                       ev_vj_free.get(j - 2)], inc="ds", by=16)
            for h4 in range(4):
                h = 4 * j + h4
                rh = h % 2
                ev_qh[h] = P.op("sync", lambda e, h=h, rh=rh: e.dma_start(
                    out=qthv[:, rh, :], in_=qT_dram.ap()[h, :, :]),
                    waits=[ev_qh_free.get(h - 2),
                           ev_E1_done if h < 2 else None,
                           ev_qT_dma_last if h == 0 else None], inc="ds", by=16)
                def emit_scores(sc):
                    par = sc % 2
                    pX = pA if par == 0 else pB
                    P.op("tensor", lambda e, sc=sc, pX=pX, rj=rj, rh=rh: e.matmul(
                        pX.ap()[:, 0:512], kjT[:, rj, sc * 128:(sc + 1) * 128],
                        qthv[:, rh, 0:512], start=True, stop=True),
                        waits=([ev_exp.get(gsc + sc - 2),
                                ev_qh[h] if sc == 0 else None,
                                ev_kj[j] if (h4 == 0 and sc == 0) else None,
                                ev_negC if (j == 0 and h4 == 0 and sc == 0) else None,
                                ev_eo2[par] if sc < 2 else None]))
                    emm1 = P.op("tensor", lambda e, sc=sc, pX=pX, rj=rj, rh=rh: e.matmul(
                        pX.ap()[:, 512:1024], kjT[:, rj, sc * 128:(sc + 1) * 128],
                        qthv[:, rh, 512:1024], start=True, stop=True), inc="pe")
                    ev_exp[gsc + sc] = P.op("scalar", lambda e, sc=sc, pX=pX, h=h:
                                            e.activation(
                        out=ptv[:, sc % 4, :], in_=pX.ap()[:, :], func=AF.Exp,
                        scale=RSQ, bias=negC.ap()[:, h:h + 1]),
                        waits=[emm1] + list(ev_pt_free.get(gsc + sc - 4) or ()), inc="act")

                def emit_avz(sc):
                    P.op("tensor", lambda e, sc=sc, rj=rj: e.matmul(
                        pC.ap()[:, :], vjv[:, rj, sc, :], ptv[:, sc % 4, 0:512],
                        start=(sc == 0), stop=(sc == DC - 1)),
                        waits=[ev_exp[gsc + sc],
                               ev_tt[0] if sc == 0 else None,
                               ev_vj[j] if (h4 == 0 and sc == 0) else None])
                    ev_av = P.op("tensor", lambda e, sc=sc, rj=rj: e.matmul(
                        pD.ap()[:, :], vjv[:, rj, sc, :], ptv[:, sc % 4, 512:1024],
                        start=(sc == 0), stop=(sc == DC - 1)),
                        waits=[ev_tt[1] if sc == 0 else None], inc="pe")
                    if sc % 2 == 1:
                        pr = (sc // 2) % 2
                        ev_pts = P.op("vector", lambda e, sc=sc, pr=pr: e.tensor_tensor(
                            out=ptsv[:, pr, :], in0=ptv[:, (sc - 1) % 4, :],
                            in1=ptv[:, sc % 4, :], op=AluOpType.add),
                            waits=[ev_exp[gsc + sc], st["pts_free"].get((gsc + sc) // 2 - 2)],
                            inc="dve")
                        ev_z0 = P.op("tensor", lambda e, sc=sc, pr=pr: e.matmul(
                            pE.ap()[0:1, 0:512], sb_onesc.ap()[:, :], ptsv[:, pr, 0:512],
                            start=(sc == 1), stop=(sc == DC - 1)),
                            waits=[ev_pts, ev_ez[0] if sc == 1 else None])
                        ev_z1 = P.op("tensor", lambda e, sc=sc, pr=pr: e.matmul(
                            pF.ap()[0:1, 0:512], sb_onesc.ap()[:, :], ptsv[:, pr, 512:1024],
                            start=(sc == 1), stop=(sc == DC - 1)),
                            waits=[ev_ez[1] if sc == 1 else None], inc="pe")
                        st["pts_free"][(gsc + sc) // 2] = ev_z1
                        ev_pt_free[gsc + sc - 1] = ev_pt_free[gsc + sc] = (
                            ev_av, ev_pts)
                    return ev_av

                for sc in range(DC):
                    emit_scores(sc)
                    if sc >= 1:
                        emit_avz(sc - 1)
                emit_avz(DC - 1)
                ev_avz = st["pts_free"][(gsc + DC - 1) // 2]
                gsc += DC
                for thh, (pZ, zr, rzr, pO, pOut, sl) in enumerate(
                        ((pE, zrow, rzrow, pC, pA, 0), (pF, zrow2, rzrow2, pD, pB, 1))):
                    ez1 = P.op("vector", lambda e, pZ=pZ, zr=zr: e.tensor_copy(
                        zr.ap()[:, :], pZ.ap()[0:1, 0:512]), waits=[ev_avz], inc="dve")
                    ev_ez[thh] = ez1
                    el1 = P.op("scalar", lambda e, zr=zr, rzr=rzr: e.activation(
                        out=rzr.ap()[:, :], in_=zr.ap()[:, :], func=AF.Ln),
                        waits=[ez1], inc="act")
                    el2 = P.op("scalar", lambda e, zr=zr, rzr=rzr: e.activation(
                        out=zr.ap()[:, :], in_=rzr.ap()[:, :], func=AF.Exp, scale=-1.0),
                        waits=[el1], inc="act")
                    eo1 = P.op("tensor", lambda e, j=j, zr=zr, pOut=pOut: e.matmul(
                        pOut.ap()[:, 0:512], cvrow.ap()[0:1, j * 128:(j + 1) * 128],
                        zr.ap()[:, :], start=True, stop=True),
                        waits=[el2, ev_exp[gsc - 2 + thh],
                               ev_cv.get(2) if (j == 0 and h4 == 0) else None,
                               ev_cv.get(3) if (j == 0 and h4 == 0) else None], inc="pe")
                    eo2 = P.op("vector", lambda e, pOut=pOut, sl=sl: e.tensor_copy(
                        sctF[:, sl, :], pOut.ap()[:, 0:512]), waits=[eo1], inc="dve")
                    ev_eo2[thh] = eo2
                    ev_tt[thh] = P.op("vector", lambda e, h=h, thh=thh, pO=pO, sl=sl:
                                      e.tensor_tensor(
                        out=attnT[:, h, thh * 512:(thh + 1) * 512], in0=pO.ap()[:, :],
                        in1=sctF[:, sl, :], op=AluOpType.mult), waits=[eo2], inc="dve")
                ev_qh_free[h] = ev_tt[1]
            ev_kj_free[j] = ev_tt[1]
            ev_vj_free[j] = ev_tt[1]
        ev_D_done = ev_tt[1]
        if debug:
            P.op("sync", lambda e: e.dma_start(out=dbg["attnT"].ap()[:, :], in_=arenaA.ap()[:, :]),
                 waits=[ev_D_done], inc="ds", by=16)

        seed_ring_barrier(ev_D_done)
        ev_x2 = {}
        ev_f2 = {}
        ev_x2_free = {}
        ev_out_dma = {}
        ev_cof_prev = None
        opd = {0: wprep_load(woT, 0, 0, ev_wslot_free.get(0))}
        for g in range(8):
            wprep_smm(opd[0], g)
        o_crow = {0: wprep_finish(opd[0], osc_row, 0)}

        def emit_cof(oc, ev_crow):
            nonlocal ev_cof_prev
            ecm = P.op("tensor", lambda e: e.matmul(
                pF.ap()[:, 0:512], sb_onesr.ap()[:, :], rowC.ap()[:, :],
                start=True, stop=True),
                waits=[ev_crow, ev_cof_prev, ev_pF_read if oc == 0 else None], inc="pe")
            ecf = P.op("vector", lambda e, oc=oc: e.tensor_copy(
                coF.ap()[:, (oc % 2) * 512:(oc % 2) * 512 + 512], pF.ap()[:, 0:512]),
                waits=[ecm], inc="dve")
            ev_cof_prev = ecf
            st["rowC_free"] = ecm
            return ecf

        cof_map = {0: emit_cof(0, o_crow[0])}
        for oc in range(8):
            slot = oc % 2
            ev_sign = opd[oc]["sign"]
            ecf = cof_map[oc]
            if oc + 1 < 8:
                opd[oc + 1] = wprep_load(woT, (oc + 1) * 512, (oc + 1) % 2,
                                         ev_wslot_free.get((oc + 1) % 2))
            mm = None
            for tt in range(NT):
                idx = oc * NT + tt
                rx = idx % 2
                ev_x2[idx] = P.op("sync", lambda e, rx=rx, oc=oc, tt=tt: e.dma_start(
                    out=xeE2[:, rx, :],
                    in_=x_loc.ap()[tt * 128:(tt + 1) * 128, oc * 512:oc * 512 + 512]),
                    waits=[ev_out_dma.get(idx - 2)], inc="ds", by=16)
                ev_f2[idx] = P.op("sync", lambda e, rx=rx, oc=oc, tt=tt: e.dma_start(
                    out=xeE2[:, 2 + rx, :],
                    in_=ffn_dram.ap()[tt * 128:(tt + 1) * 128, oc * 512:oc * 512 + 512]),
                    waits=[ev_out_dma.get(idx - 2),
                           ev_E1_done if idx == 0 else None], inc="ds", by=16)
                pk = paks[tt % 4]
                for ac in range(H):
                    mm = P.op("tensor", lambda e, ac=ac, tt=tt, slot=slot, pk=pk: e.matmul(
                        pk, attnT[:, ac, tt * 128:(tt + 1) * 128], wring[:, slot, ac, :],
                        start=(ac == 0), stop=(ac == H - 1)),
                        waits=([ev_sign, ev_D_done if idx == 0 else None,
                                ev_out_dma.get(idx - 4)] if ac == 0 else []),
                        inc=("pe" if ac == H - 1 else None))
                if oc + 1 < 8:
                    wprep_smm(opd[oc + 1], tt)
                kr = tt % 2
                ecb1 = P.op("vector", lambda e, kr=kr, oc=oc, pk=pk: e.tensor_tensor(
                    out=sctF[:, 2 * kr, :], in0=pk,
                    in1=coF.ap()[:, (oc % 2) * 512:(oc % 2) * 512 + 512],
                    op=AluOpType.mult), waits=[mm, ecf], inc="dve")
                ecb2 = P.op("vector", lambda e, kr=kr, rx=rx: e.tensor_tensor(
                    out=sctF[:, 2 * kr + 1, :], in0=sctF[:, 2 * kr, :], in1=xeE2[:, rx, :],
                    op=AluOpType.add), waits=[ecb1, ev_x2[idx]], inc="dve")
                ev_x2_free[idx] = ecb2
                ecb3 = P.op("vector", lambda e, kr=kr, rx=rx: e.tensor_tensor(
                    out=xeE2[:, 4 + rx, :], in0=sctF[:, 2 * kr + 1, :],
                    in1=xeE2[:, 2 + rx, :], op=AluOpType.add),
                    waits=[ecb2, ev_f2[idx], ev_out_dma.get(idx - 2)], inc="dve")
                last_out = P.op("sync", lambda e, rx=rx, oc=oc, tt=tt: e.dma_start(
                    out=out_d.ap()[tt * 128:(tt + 1) * 128, oc * 512:oc * 512 + 512],
                    in_=xeE2[:, 4 + rx, :]), waits=[ecb3], inc="ds", by=16)
                ev_out_dma[idx] = last_out
            ev_wslot_free[slot] = mm
            if oc + 1 < 8:
                crow = wprep_finish(opd[oc + 1], osc_row, (oc + 1) * 512)
                cof_map[oc + 1] = emit_cof(oc + 1, crow)

        P.wait("sync", ("ds", P.cnt["ds"]))
        P.wait("gpsimd", ("dg", P.cnt["dg"]))

        @block.tensor
        def _(t):
            for f in P.prog["tensor"]:
                f(t, sems)

        @block.vector
        def _(v):
            for f in P.prog["vector"]:
                f(v, sems)

        @block.scalar
        def _(s):
            for f in P.prog["scalar"]:
                f(s, sems)

        @block.sync
        def _(s):
            for f in P.prog["sync"]:
                f(s, sems)

        @block.gpsimd
        def _(g):
            for f in P.prog["gpsimd"]:
                f(g, sems)

    return nc, P


def host_inputs(inputs):
    x = np.ascontiguousarray(np.asarray(inputs["L0_attn_hidden_states"], np.float32))
    wq = np.asarray(inputs["L0_attn_q_weight"], np.float32)
    wk = np.asarray(inputs["L0_attn_k_weight"], np.float32)
    wv = np.asarray(inputs["L0_attn_v_weight"], np.float32)
    wo = np.asarray(inputs["L0_attn_o_weight"], np.float32)
    qs = np.asarray(inputs["L0_attn_q_scale"], np.float32)
    ks = np.asarray(inputs["L0_attn_k_scale"], np.float32)
    vs = np.asarray(inputs["L0_attn_v_scale"], np.float32)
    osc = np.asarray(inputs["L0_attn_o_scale"], np.float32)
    q_scales = np.asarray(inputs["L0_ffn_q_scales"], np.float32)
    q_vals = np.asarray(inputs["L0_ffn_q_vals"], np.int32)
    fb = np.asarray(inputs["L0_ffn_bias"], np.float32)

    com = dict(
        wqT=np.ascontiguousarray(wq.T),
        wkvT=np.ascontiguousarray(np.concatenate([wk.T, wv.T], axis=1)),
        woT=np.ascontiguousarray(wo.T),
        codesT=np.ascontiguousarray(q_vals.reshape(4096, 4096).T),
        scexpT=np.ascontiguousarray(np.repeat(q_scales.T, 32, axis=0)),
        qsc_row=np.ascontiguousarray(qs[None, :]),
        kvsc_row=np.ascontiguousarray(np.concatenate([ks, vs])[None, :]),
        osc_row=np.ascontiguousarray(osc[None, :]),
        bias_row=np.ascontiguousarray(fb[None, :]),
        identity=np.eye(128, dtype=NB),
        identity_f=np.eye(128, dtype=np.float32),
        onesc_d=np.ones((128, 1), NB),
        onesr_d=np.ones((1, 128), np.float32),
    )
    in_maps = []
    for c in range(8):
        b, q = c // 4, c % 4
        m = dict(com)
        m["x_loc"] = np.ascontiguousarray(x[b, q * T:(q + 1) * T, :])
        in_maps.append(m)
    return in_maps


def kernel(**inputs):
    in_maps = host_inputs(inputs)
    nc, _ = build_nc(debug=False)
    res = run_bass_kernel_spmd(nc, in_maps, core_ids=list(range(8)))
    out = np.empty((2, 4096, 4096), np.float32)
    for c in range(8):
        b, q = c // 4, c % 4
        out[b, q * T:(q + 1) * T, :] = res.results[c]["out"]
    return out


if __name__ == "__main__":
    nc, P = build_nc(debug=False)
    print("instructions (incl waits):", P.ni, "total:", sum(P.ni.values()))
    print("sem totals:", P.cnt)


# version 07
# speedup vs baseline: 1.0719x; 1.0719x over previous
import contextlib
import numpy as np
import ml_dtypes

import concourse.bass as bass
import concourse.mybir as mybir
from concourse.bass_utils import run_bass_kernel_spmd
from concourse.alu_op_type import AluOpType

F32 = mybir.dt.float32
BF16 = mybir.dt.bfloat16
I32 = mybir.dt.int32
AF = mybir.ActivationFunctionType
NB = np.dtype(ml_dtypes.bfloat16)

D = 4096
T = 1024
S = 4096
NT = 8
DC = 32
H = 32
KVH = 8
RSQ = float(1.0 / np.sqrt(np.float32(128.0)))

ENGS = ("tensor", "vector", "scalar", "sync", "gpsimd")


class Prog:

    def __init__(self):
        self.prog = {e: [] for e in ENGS}
        self.cnt = {}
        self.ni = {e: 0 for e in ENGS}

    def ev(self, sem, by):
        self.cnt[sem] = self.cnt.get(sem, 0) + by
        return (sem, self.cnt[sem])

    def op(self, eng, make, waits=(), inc=None, by=1):
        waits = tuple(w for w in waits if w is not None)
        e = self.ev(inc, by) if inc else None
        self.ni[eng] += 1 + len(waits)

        def f(engobj, sems, _w=waits, _m=make, _e=e, _by=by):
            for (s, v) in _w:
                engobj.wait_ge(sems[s], v)
            ins = _m(engobj)
            if _e is not None:
                ins.then_inc(sems[_e[0]], _by)
        self.prog[eng].append(f)
        return e

    def wait(self, eng, *evs):
        evs = tuple(e for e in evs if e is not None)
        if not evs:
            return
        self.ni[eng] += len(evs)

        def f(engobj, sems, _w=evs):
            for (s, v) in _w:
                engobj.wait_ge(sems[s], v)
        self.prog[eng].append(f)


def build_nc(debug=False):
    nc = bass.Bass("TRN2", target_bir_lowering=False)

    x_loc = nc.dram_tensor("x_loc", [T, D], F32, kind="ExternalInput")
    wqT = nc.dram_tensor("wqT", [D, D], F32, kind="ExternalInput")
    wkvT = nc.dram_tensor("wkvT", [D, 2048], F32, kind="ExternalInput")
    woT = nc.dram_tensor("woT", [D, D], F32, kind="ExternalInput")
    codesT = nc.dram_tensor("codesT", [D, D], I32, kind="ExternalInput")
    scexpT = nc.dram_tensor("scexpT", [D, D], F32, kind="ExternalInput")
    qsc_row = nc.dram_tensor("qsc_row", [1, D], F32, kind="ExternalInput")
    kvsc_row = nc.dram_tensor("kvsc_row", [1, 2048], F32, kind="ExternalInput")
    osc_row = nc.dram_tensor("osc_row", [1, D], F32, kind="ExternalInput")
    bias_row = nc.dram_tensor("bias_row", [1, D], F32, kind="ExternalInput")
    identity = nc.dram_tensor("identity", [128, 128], BF16, kind="ExternalInput")
    identity_f = nc.dram_tensor("identity_f", [128, 128], F32, kind="ExternalInput")
    onesc_d = nc.dram_tensor("onesc_d", [128, 1], BF16, kind="ExternalInput")
    onesr_d = nc.dram_tensor("onesr_d", [1, 128], F32, kind="ExternalInput")

    out_d = nc.dram_tensor("out", [T, D], F32, kind="ExternalOutput")

    qT_dram = nc.dram_tensor("qT_dram", [H, 128, T], BF16)
    KP_COLS = 1040
    kp_in = nc.dram_tensor("kp_in", [T + 1, KP_COLS], BF16)
    kp_out = nc.dram_tensor("kp_out", [4 * (T + 1), KP_COLS], BF16)
    vp_in = nc.dram_tensor("vp_in", [T, 1024], BF16)
    vp_out = nc.dram_tensor("vp_out", [S, 1024], BF16)
    ffn_dram = nc.dram_tensor("ffn_dram", [T, D], F32)

    dbg = {}
    if debug:
        for nm, sh, dt in (("xnT", [128, DC * T], BF16), ("attnT", [128, H * T], BF16),
                           ("negc", [128, 32], F32), ("kn2", [1, 8], F32),
                           ("qn2", [1, 32], F32), ("cvrow", [1, 1024], F32),
                           ("cqk", [128, 32], F32)):
            dbg[nm] = nc.dram_tensor("dbg_" + nm, sh, dt, kind="ExternalOutput")

    P = Prog()
    es = contextlib.ExitStack()
    with es:
        t_ = lambda *a: es.enter_context(nc.sbuf_tensor(*a))
        arenaA = t_("arenaA", [128, DC * 1024], BF16)
        wbuf = t_("wbuf", [128, 32768], BF16)
        scrF = t_("scrF", [128, 4096], F32)
        scrB = t_("scrB", [128, 4096], BF16)
        absB = t_("absB", [128, 4096], BF16)
        stageB = t_("stageB", [128, 4096], BF16)
        stageF = t_("stageF", [128, 2048], F32)
        kn2a = t_("kn2a", [128, 8], F32)
        kn2b = t_("kn2b", [128, 8], F32)
        kn2c = t_("kn2c", [128, 8], F32)
        coF = t_("coF", [128, 1024], F32)
        sb_id = t_("sb_id", [128, 128], BF16)
        sb_idf = t_("sb_idf", [128, 128], F32)
        sb_onesc = t_("sb_onesc", [128, 1], BF16)
        sb_onesr = t_("sb_onesr", [1, 128], F32)
        epsv = t_("epsv", [128, 1], F32)
        lstat = [t_("lstatA", [128, 8], F32), t_("lstatB", [128, 8], F32)]
        lsq = [t_("lsqA", [128, 4], F32), t_("lsqB", [128, 4], F32)]
        rowS = t_("rowS", [1, 512], F32)
        rowC = t_("rowC", [1, 512], F32)
        rowSc = t_("rowSc", [1, 512], F32)
        cvrow = t_("cvrow", [1, 1024], F32)
        zrow = t_("zrow", [1, 512], F32)
        rzrow = t_("rzrow", [1, 512], F32)
        zrow2 = t_("zrow2", [1, 512], F32)
        rzrow2 = t_("rzrow2", [1, 512], F32)
        qn2A = t_("qn2A", [1, 32], F32)
        qn2B = t_("qn2B", [1, 32], F32)
        qn2T = t_("qn2T", [1, 32], F32)
        kngb = t_("kngb", [4, 8], BF16)
        knrow8 = t_("knrow8", [1, 8], F32)
        knloc = t_("knloc", [8, 128], F32)
        knlocm = t_("knlocm", [8, 1], F32)
        knlocb = t_("knlocb", [1, 8], BF16)
        negC = t_("negC", [128, 32], F32)
        ckcols = t_("ckcols", [128, 8], F32)
        cqcols = t_("cqcols", [128, 32], F32)
        cqk = t_("cqk", [128, 32], F32)

        pA = es.enter_context(nc.psum_tensor("pA", [128, 1024], F32))
        pB = es.enter_context(nc.psum_tensor("pB", [128, 1024], F32))
        pC = es.enter_context(nc.psum_tensor("pC", [128, 512], F32))
        pD = es.enter_context(nc.psum_tensor("pD", [128, 512], F32))
        pE = es.enter_context(nc.psum_tensor("pE", [128, 512], F32))
        pF = es.enter_context(nc.psum_tensor("pF", [128, 512], F32))

        semnames = ["pe", "act", "dve", "ds", "dg", "cc"]
        sems = {n: es.enter_context(nc.semaphore(n)) for n in semnames}
        block = es.enter_context(nc.Block())

        xnT = arenaA.ap().rearrange("p (c t) -> p c t", c=DC)
        attnT = arenaA.ap().rearrange("p (c t) -> p c t", c=H)
        wring = wbuf.ap().rearrange("p (r c n) -> p r c n", r=2, c=DC)
        rawv = scrB.ap().rearrange("p (r n) -> p r n", r=2)
        absv = absB.ap().rearrange("p (r n) -> p r n", r=2)
        kjT = wbuf.ap()[:, 0:8192].rearrange("p (r s) -> p r s", r=2)
        vjv = wbuf.ap()[:, 8192:16384].rearrange("p (r s c) -> p r s c", r=2, s=32)
        qthv = wbuf.ap()[:, 16384:18432].rearrange("p (r n) -> p r n", r=2)
        ptv = wbuf.ap()[:, 18432:22528].rearrange("p (r n) -> p r n", r=4)
        qstv = stageB.ap().rearrange("p (r n) -> p r n", r=8)
        xbuf = scrF.ap()
        xeE2 = scrF.ap().rearrange("p (r n) -> p r n", r=8)
        scxv = scrF.ap().rearrange("p (r n) -> p r n", r=2)
        sctF = stageF.ap().rearrange("p (r n) -> p r n", r=4)
        q2v = stageF.ap().bitcast(BF16)[:, 0:512]
        xnb = stageB.ap()
        ptsv = stageB.ap()[:, 0:2048].rearrange("p (r n) -> p r n", r=2)
        paks = [pA.ap()[:, 0:512], pA.ap()[:, 512:1024],
                pB.ap()[:, 0:512], pB.ap()[:, 512:1024]]
        trb = [p.ap().bitcast(BF16) for p in (pC, pD, pE, pF)]

        for dst, src in ((sb_id, identity), (sb_idf, identity_f),
                         (sb_onesc, onesc_d), (sb_onesr, onesr_d)):
            P.op("sync", lambda e, d=dst, s=src: e.dma_start(out=d.ap()[:, :], in_=s.ap()[:, :]),
                 inc="ds", by=16)
        evA = ("ds", P.cnt["ds"])
        ev_eps = P.op("vector", lambda e: e.memset(epsv.ap()[:, :], 1e-5), inc="dve")
        P.op("vector", lambda e: e.memset(qn2A.ap()[:, :], -3.0e38), inc="dve")
        ev_qn2i = P.op("vector", lambda e: e.memset(qn2B.ap()[:, :], -3.0e38), inc="dve")

        ev_xn = [None] * NT
        ev_junk = [None] * NT
        ev_evac = {}
        ev_tp = {}
        for tt in range(NT):
            r = tt % 2
            ev_x = P.op(
                "sync",
                lambda e, tt=tt: e.dma_start(
                    out=xbuf[:, :], in_=x_loc.ap()[tt * 128:(tt + 1) * 128, :]),
                waits=[ev_xn[tt - 1] if tt >= 1 else None,
                       ev_junk[tt - 1] if tt >= 1 else None], inc="ds", by=16)
            ls, lq = lstat[r], lsq[r]
            e1 = P.op("vector", lambda e, ls=ls: e.tensor_reduce(
                out=ls.ap()[:, 0:1], in_=xbuf[:, 0:2048], axis=mybir.AxisListType.X,
                op=AluOpType.add), waits=[ev_x], inc="dve")
            e2 = P.op("vector", lambda e, ls=ls: e.tensor_reduce(
                out=ls.ap()[:, 1:2], in_=xbuf[:, 2048:4096], axis=mybir.AxisListType.X,
                op=AluOpType.add), inc="dve")
            ea1 = P.op("scalar", lambda e, lq=lq: e.activation(
                out=absv[:, 0, :], in_=xbuf[:, 0:2048], func=AF.Square,
                accum_out=lq.ap()[:, 0:1]), waits=[ev_x], inc="act")
            ev_junk[tt] = ea2 = P.op("scalar", lambda e, lq=lq: e.activation(
                out=absv[:, 1, :], in_=xbuf[:, 2048:4096], func=AF.Square,
                accum_out=lq.ap()[:, 1:2]), inc="act")
            e3 = P.op("vector", lambda e, ls=ls: e.tensor_tensor(
                out=ls.ap()[:, 2:3], in0=ls.ap()[:, 0:1], in1=ls.ap()[:, 1:2],
                op=AluOpType.add), waits=[e1, e2], inc="dve")
            e4 = P.op("vector", lambda e, ls=ls: e.tensor_scalar(
                out=ls.ap()[:, 3:4], in0=ls.ap()[:, 2:3], scalar1=1.0 / D, scalar2=None,
                op0=AluOpType.mult), waits=[e3], inc="dve")
            e5 = P.op("vector", lambda e, ls=ls: e.tensor_tensor(
                out=ls.ap()[:, 4:5], in0=ls.ap()[:, 3:4], in1=ls.ap()[:, 3:4],
                op=AluOpType.mult), waits=[e4], inc="dve")
            e6 = P.op("vector", lambda e, lq=lq: e.tensor_tensor(
                out=lq.ap()[:, 2:3], in0=lq.ap()[:, 0:1], in1=lq.ap()[:, 1:2],
                op=AluOpType.add), waits=[ea1, ea2, e5], inc="dve")
            e7 = P.op("vector", lambda e, ls=ls, lq=lq: e.tensor_scalar(
                out=ls.ap()[:, 5:6], in0=lq.ap()[:, 2:3], scalar1=1.0 / D,
                scalar2=ls.ap()[:, 4:5], op0=AluOpType.mult, op1=AluOpType.subtract),
                waits=[e6], inc="dve")
            e8 = P.op("scalar", lambda e, ls=ls: e.activation(
                out=ls.ap()[:, 6:7], in_=ls.ap()[:, 5:6], func=AF.Sqrt,
                bias=epsv.ap()[:, :]), waits=[e7, ev_eps], inc="act")
            e9 = P.op("vector", lambda e, ls=ls: e.reciprocal(
                out=ls.ap()[:, 7:8], in_=ls.ap()[:, 6:7]), waits=[e8], inc="dve")
            P.op("vector", lambda e, ls=ls: e.tensor_scalar(
                out=xnb[:, 0:2048], in0=xbuf[:, 0:2048], scalar1=ls.ap()[:, 3:4],
                scalar2=ls.ap()[:, 7:8], op0=AluOpType.subtract, op1=AluOpType.mult),
                waits=[e9, ev_tp.get((tt - 1, 3))], inc="dve")
            ev_xn[tt] = P.op("vector", lambda e, ls=ls: e.tensor_scalar(
                out=xnb[:, 2048:4096], in0=xbuf[:, 2048:4096], scalar1=ls.ap()[:, 3:4],
                scalar2=ls.ap()[:, 7:8], op0=AluOpType.subtract, op1=AluOpType.mult),
                inc="dve")
            for g in range(4):
                for i in range(8):
                    dc = 8 * g + i
                    ee = P.op("tensor", lambda e, g=g, i=i, dc=dc: e.transpose(
                        trb[g][:, i * 128:(i + 1) * 128],
                        xnb[:, dc * 128:(dc + 1) * 128], sb_id.ap()[:, :]),
                        waits=([ev_xn[tt], ev_evac.get((tt - 1, g)),
                                evA if (tt == 0 and g == 0) else None] if i == 0 else []),
                        inc=("pe" if i == 7 else None))
                    if i == 7:
                        ev_tp[(tt, g)] = ee
                ev_evac[(tt, g)] = P.op("vector", lambda e, tt=tt, g=g: e.tensor_copy(
                    xnT[:, 8 * g:8 * g + 8, tt * 128:(tt + 1) * 128],
                    trb[g].rearrange("p (c t) -> p c t", c=8)),
                    waits=[ev_tp[(tt, g)]], inc="dve")
        ev_B1_last = ev_evac[(NT - 1, 3)]
        ev_B1_junk = ev_junk[NT - 1]

        if debug:
            P.op("sync", lambda e: e.dma_start(out=dbg["xnT"].ap()[:, :], in_=arenaA.ap()[:, :]),
                 waits=[ev_B1_last], inc="ds", by=16)

        st = {
            "raw_free": {}, "abs_free": {}, "graw": 0, "gabs": 0,
            "rowSc_free": None, "pE_free": None,
        }
        st["raw_free"][-1] = st["raw_free"][-2] = None
        st["abs_free"][-1] = st["abs_free"][-2] = None

        def seed_ring_barrier(ev):
            g = st["graw"]
            st["raw_free"][g - 1] = st["raw_free"][g - 2] = ev
            g = st["gabs"]
            st["abs_free"][g - 1] = st["abs_free"][g - 2] = ev

        def wprep_load(src, col0, slot, wslot_free_ev):
            pd = {"abs": [None] * 8, "absga": [None] * 8, "sign": None, "slot": slot}
            for g in range(8):
                rr = g % 2
                G = st["graw"]
                st["graw"] += 1
                ev_dma = P.op("gpsimd", lambda e, g=g, rr=rr: e.dma_start(
                    out=rawv[:, rr, :].rearrange("p (c n) -> p c n", c=4),
                    in_=src.ap().rearrange("(c p) n -> p c n", p=128)[
                        :, 4 * g:4 * g + 4, col0:col0 + 512]),
                    waits=[st["raw_free"].get(G - 2)], inc="dg", by=16)
                GA = st["gabs"]
                st["gabs"] += 1
                pd["absga"][g] = GA
                pd["abs"][g] = P.op("vector", lambda e, rr=rr: e.scalar_tensor_tensor(
                    out=absv[:, rr, :], in0=rawv[:, rr, :], scalar=-1.0,
                    in1=rawv[:, rr, :], op0=AluOpType.mult, op1=AluOpType.max),
                    waits=[ev_dma, st["abs_free"].get(GA - 2)], inc="dve")
                pd["sign"] = P.op("scalar", lambda e, g=g, rr=rr, slot=slot: e.activation(
                    out=wring[:, slot, 4 * g:4 * g + 4, :],
                    in_=rawv[:, rr, :].rearrange("p (c n) -> p c n", c=4), func=AF.Sign),
                    waits=[ev_dma, pd["abs"][g],
                           wslot_free_ev if g == 0 else None], inc="act")
                st["raw_free"][G] = pd["sign"]
            return pd

        def wprep_smm(pd, g):
            rr = g % 2
            ev_s = None
            for i in range(4):
                ev_s = P.op("tensor", lambda e, rr=rr, i=i, g=g: e.matmul(
                    pE.ap()[0:1, 0:512], sb_onesc.ap()[:, :],
                    absv[:, rr, i * 512:(i + 1) * 512],
                    start=(g == 0 and i == 0), stop=(g == 7 and i == 3)),
                    waits=([pd["abs"][g]] if i == 0 else []) +
                          ([st["pE_free"]] if (g == 0 and i == 0) else []),
                    inc="pe")
            st["abs_free"][pd["absga"][g]] = ev_s
            pd["smm"] = ev_s
            return ev_s

        def wprep_finish(pd, scale_src, scale_off):
            ev_sc = P.op("sync", lambda e: e.dma_start(
                out=rowSc.ap()[:, :],
                in_=scale_src.ap()[0:1, scale_off:scale_off + 512]),
                waits=[st["rowSc_free"]], inc="ds", by=16)
            ec1 = P.op("vector", lambda e: e.tensor_copy(rowS.ap()[:, :], pE.ap()[0:1, 0:512]),
                       waits=[pd["smm"]], inc="dve")
            st["pE_free"] = ec1
            ec2 = P.op("vector", lambda e: e.tensor_scalar(
                out=rowS.ap()[:, :], in0=rowS.ap()[:, :], scalar1=1.0 / D, scalar2=1e-8,
                op0=AluOpType.mult, op1=AluOpType.max), waits=[ec1], inc="dve")
            ev_crow = P.op("vector", lambda e: e.tensor_tensor(
                out=rowC.ap()[:, :], in0=rowS.ap()[:, :], in1=rowSc.ap()[:, :],
                op=AluOpType.mult), waits=[ec2, ev_sc, st.get("rowC_free")], inc="dve")
            st["rowSc_free"] = ev_crow
            return ev_crow

        ev_wslot_free = {}
        ev_kst_free = {}
        kst_dma = {}
        ev_kp_dma = []
        ev_vp_dma = []
        ev_pF_read = ev_B1_last
        ev_kn_prev = None
        ev_ck = {}
        ev_cv = {}
        seed_ring_barrier(ev_B1_junk)

        kvpd = {}
        kvpd[0] = wprep_load(wkvT, 0, 0, None)
        for g in range(8):
            wprep_smm(kvpd[0], g)
        crow = wprep_finish(kvpd[0], kvsc_row, 0)
        kv_crow = {0: crow}

        def emit_kv_c(kc, ev_crow):
            nonlocal_state = {}
            if kc < 2:
                evt = None
                for i in range(4):
                    evt = P.op("tensor", lambda e, i=i: e.transpose(
                        pF.ap()[:, i:i + 1], rowC.ap()[0:1, i * 128:(i + 1) * 128],
                        sb_idf.ap()[0:1, 0:1]),
                        waits=([ev_crow, st["pF_read"]] if i == 0 else []),
                        inc=("pe" if i == 3 else None))
                ev_ck[kc] = P.op("vector", lambda e, kc=kc: e.tensor_copy(
                    ckcols.ap()[:, 4 * kc:4 * kc + 4], pF.ap()[:, 0:4]),
                    waits=[evt], inc="dve")
                st["pF_read"] = ev_ck[kc]
                st["rowC_free"] = ev_ck[kc]
            else:
                ev_cv[kc] = P.op("vector", lambda e, kc=kc: e.tensor_copy(
                    cvrow.ap()[0:1, (kc - 2) * 512:(kc - 1) * 512], rowC.ap()[:, :]),
                    waits=[ev_crow], inc="dve")
                st["rowC_free"] = ev_cv[kc]

        st["pF_read"] = ev_pF_read
        emit_kv_c(0, crow)
        for kc in range(4):
            slot = kc % 2
            ev_sign = kvpd[kc]["sign"]
            if kc + 1 < 4:
                kvpd[kc + 1] = wprep_load(wkvT, (kc + 1) * 512, (kc + 1) % 2,
                                          ev_wslot_free.get((kc + 1) % 2))
            elif True:
                kvpd[4] = wprep_load(wqT, 0, 0, ev_wslot_free.get(0))
            mm = None
            for tt in range(NT):
                idx = kc * NT + tt
                pk = paks[idx % 4]
                for dc in range(DC):
                    mm = P.op("tensor", lambda e, dc=dc, tt=tt, slot=slot, pk=pk: e.matmul(
                        pk, xnT[:, dc, tt * 128:(tt + 1) * 128], wring[:, slot, dc, :],
                        start=(dc == 0), stop=(dc == DC - 1)),
                        waits=([ev_sign, ev_B1_last, ev_kst_free.get(idx - 4)]
                               if dc == 0 else []),
                        inc=("pe" if dc == DC - 1 else None))
                wprep_smm(kvpd[kc + 1], tt)
                kr = idx % 4
                ev_ev = P.op("vector", lambda e, kr=kr, pk=pk: e.tensor_copy(
                    qstv[:, kr, :], pk), waits=[mm, kst_dma.get(idx - 4)], inc="dve")
                ev_kst_free[idx] = ev_ev
                if kc < 2:
                    ev_d = P.op("sync", lambda e, kr=kr, kc=kc, tt=tt: e.dma_start(
                        out=kp_in.ap()[tt * 128:(tt + 1) * 128, kc * 512:kc * 512 + 512],
                        in_=qstv[:, kr, :]), waits=[ev_ev], inc="ds", by=16)
                    ev_kp_dma.append(ev_d)
                    kst_dma[idx] = ev_d
                    eq = P.op("vector", lambda e, kr=kr: e.tensor_tensor(
                        out=q2v, in0=qstv[:, kr, :], in1=qstv[:, kr, :],
                        op=AluOpType.mult), waits=[ev_ev], inc="dve")
                    er = None
                    for i in range(4):
                        er = P.op("vector", lambda e, i=i, kc=kc: e.tensor_reduce(
                            out=kn2a.ap()[:, 4 * kc + i:4 * kc + i + 1],
                            in_=q2v[:, i * 128:(i + 1) * 128],
                            axis=mybir.AxisListType.X, op=AluOpType.add),
                            waits=([eq] if i == 0 else []), inc="dve")
                    cols = (4 * kc, 4 * kc + 4)
                    if tt == 0:
                        ev_kn_prev = P.op("vector", lambda e, cols=cols: e.tensor_copy(
                            kn2b.ap()[:, cols[0]:cols[1]], kn2a.ap()[:, cols[0]:cols[1]]),
                            waits=[er], inc="dve")
                    else:
                        src_t = kn2b if tt % 2 == 1 else kn2c
                        dst_t = kn2c if tt % 2 == 1 else kn2b
                        ev_kn_prev = P.op("vector",
                                          lambda e, cols=cols, s=src_t, d=dst_t: e.tensor_tensor(
                            out=d.ap()[:, cols[0]:cols[1]], in0=s.ap()[:, cols[0]:cols[1]],
                            in1=kn2a.ap()[:, cols[0]:cols[1]], op=AluOpType.max),
                            waits=[er, ev_kn_prev], inc="dve")
                else:
                    ev_d = P.op("sync", lambda e, kr=kr, kc=kc, tt=tt: e.dma_start(
                        out=vp_in.ap()[tt * 128:(tt + 1) * 128,
                                       (kc - 2) * 512:(kc - 2) * 512 + 512],
                        in_=qstv[:, kr, :]), waits=[ev_ev], inc="ds", by=16)
                    ev_vp_dma.append(ev_d)
                    kst_dma[idx] = ev_d
            ev_wslot_free[slot] = mm
            if kc + 1 < 4:
                crow = wprep_finish(kvpd[kc + 1], kvsc_row, (kc + 1) * 512)
                emit_kv_c(kc + 1, crow)
            else:
                q_crow0 = wprep_finish(kvpd[4], qsc_row, 0)
        ev_pF_read = st["pF_read"]

        knfin = kn2c
        et1 = P.op("tensor", lambda e: e.transpose(
            pF.ap()[0:8, 0:128], knfin.ap()[:, 0:8], sb_idf.ap()[:, :]),
            waits=[ev_kn_prev, ev_pF_read], inc="pe")
        et2 = P.op("vector", lambda e: e.tensor_copy(knloc.ap()[:, :], pF.ap()[0:8, 0:128]),
                   waits=[et1], inc="dve")
        et3 = P.op("vector", lambda e: e.tensor_reduce(
            out=knlocm.ap()[:, :], in_=knloc.ap()[:, :], axis=mybir.AxisListType.X,
            op=AluOpType.max), waits=[et2], inc="dve")
        et4 = P.op("tensor", lambda e: e.transpose(
            pF.ap()[0:1, 0:8], knlocm.ap()[:, :], sb_idf.ap()[0:8, 0:8]),
            waits=[et3], inc="pe")
        et5 = P.op("vector", lambda e: e.tensor_copy(knlocb.ap()[:, :], pF.ap()[0:1, 0:8]),
                   waits=[et4], inc="dve")
        ev_pF_read = et5
        ev_knrow_dma = P.op("sync", lambda e: e.dma_start(
            out=kp_in.ap()[T:T + 1, 0:8], in_=knlocb.ap()[:, :]),
            waits=[et5], inc="ds", by=16)

        P.wait("gpsimd", ev_kp_dma[-1], ev_knrow_dma)
        ev_ccK = P.op("gpsimd", lambda e: e.collective_compute(
            "AllGather", mybir.AluOpType.bypass,
            replica_groups=[[0, 1, 2, 3], [4, 5, 6, 7]],
            ins=[kp_in.ap().opt()], outs=[kp_out.ap().opt()]), inc="cc")
        P.wait("gpsimd", ev_vp_dma[-1])
        ev_ccV = P.op("gpsimd", lambda e: e.collective_compute(
            "AllGather", mybir.AluOpType.bypass,
            replica_groups=[[0, 1, 2, 3], [4, 5, 6, 7]],
            ins=[vp_in.ap().opt()], outs=[vp_out.ap().opt()]), inc="cc")

        ev_slot_dma = {}
        c_act_evac = {}
        ev_erm_prev = None
        ev_qT_dma_last = None
        qpd = {0: kvpd[4]}
        ecqk_map = {}

        def emit_q_c(qc, ev_crow):
            evt = None
            for i in range(4):
                evt = P.op("tensor", lambda e, i=i: e.transpose(
                    pF.ap()[:, i:i + 1], rowC.ap()[0:1, i * 128:(i + 1) * 128],
                    sb_idf.ap()[0:1, 0:1]),
                    waits=([ev_crow, st["pF_read"], ev_erm_prev] if i == 0 else []),
                    inc=("pe" if i == 3 else None))
            ecq = P.op("vector", lambda e, qc=qc: e.tensor_copy(
                cqcols.ap()[:, 4 * qc:4 * qc + 4], pF.ap()[:, 0:4]), waits=[evt], inc="dve")
            st["pF_read"] = ecq
            st["rowC_free"] = ecq
            return P.op("vector", lambda e, qc=qc: e.tensor_scalar(
                out=cqk.ap()[:, 4 * qc:4 * qc + 4], in0=cqcols.ap()[:, 4 * qc:4 * qc + 4],
                scalar1=ckcols.ap()[:, qc:qc + 1], scalar2=None, op0=AluOpType.mult),
                waits=[ecq, ev_ck.get(0), ev_ck.get(1)], inc="dve")

        st["pF_read"] = ev_pF_read
        ecqk_map[0] = emit_q_c(0, q_crow0)
        for qc in range(8):
            slot = qc % 2
            ev_sign = qpd[qc]["sign"]
            ecqk = ecqk_map[qc]
            if qc + 1 < 8:
                qpd[qc + 1] = wprep_load(wqT, (qc + 1) * 512, (qc + 1) % 2,
                                         ev_wslot_free.get((qc + 1) % 2))
            mm = None
            for ms in range(4):
                m = 4 * qc + ms
                for th in range(2):
                    idx = m * 2 + th
                    pk = paks[idx % 4]
                    for dc in range(DC):
                        mm = P.op("tensor", lambda e, dc=dc, slot=slot, ms=ms, th=th, pk=pk:
                                  e.matmul(pk, wring[:, slot, dc, ms * 128:(ms + 1) * 128],
                                           xnT[:, dc, th * 512:(th + 1) * 512],
                                           start=(dc == 0), stop=(dc == DC - 1)),
                                  waits=([ev_sign, ev_slot_dma.get(idx - 4),
                                          ev_kst_free.get(28 + idx) if idx < 4 else None]
                                         if dc == 0 else []),
                                  inc=("pe" if dc == DC - 1 else None))
                    if qc + 1 < 8:
                        wprep_smm(qpd[qc + 1], ms * 2 + th)
                    esq = P.op("scalar", lambda e, m=m, pk=pk, ecqk=ecqk: e.activation(
                        out=q2v, in_=pk, func=AF.Square,
                        scale=cqk.ap()[:, m:m + 1]),
                        waits=[mm, ecqk,
                               et3 if idx == 0 else None,
                               ev_erm_prev if idx > 0 else None], inc="act")
                    kr = idx % 8
                    eev = P.op("scalar", lambda e, kr=kr, m=m, pk=pk: e.activation(
                        out=qstv[:, kr, :], in_=pk, func=AF.Copy,
                        scale=cqk.ap()[:, m:m + 1]),
                        waits=[ev_slot_dma.get(idx - 8),
                               ev_vp_dma[-1] if idx == 0 else None,
                               kst_dma.get(4 * NT - 4 + idx)], inc="act")
                    ed = P.op("sync", lambda e, kr=kr, m=m, th=th: e.dma_start(
                        out=qT_dram.ap()[m, :, th * 512:(th + 1) * 512], in_=qstv[:, kr, :]),
                        waits=[eev], inc="ds", by=16)
                    ev_slot_dma[idx] = ed
                    ev_qT_dma_last = ed
                    c_act_evac[idx] = eev
                    eqm = P.op("tensor", lambda e: e.matmul(
                        pF.ap()[0:1, 0:512], sb_onesc.ap()[:, :], q2v,
                        start=True, stop=True),
                        waits=[esq, st["pF_read"] if idx == 0 else None,
                               ev_erm_prev if idx > 0 else None], inc="pe")
                    dst = qn2A if th == 0 else qn2B
                    ev_erm_prev = P.op("vector", lambda e, m=m, dst=dst: e.tensor_reduce(
                        out=dst.ap()[0:1, m:m + 1], in_=pF.ap()[0:1, 0:512],
                        axis=mybir.AxisListType.X, op=AluOpType.max),
                        waits=[eqm, ev_qn2i], inc="dve")
            ev_wslot_free[slot] = mm
            if qc + 1 < 8:
                crow = wprep_finish(qpd[qc + 1], qsc_row, (qc + 1) * 512)
                ecqk_map[qc + 1] = emit_q_c(qc + 1, crow)
        ev_pF_read = ev_erm_prev
        ev_C_qn2 = ev_erm_prev
        ev_pF_read = ev_erm_prev

        ev_fst_free = {}
        st["scx_free"] = {}
        fst_dma = {}
        ev_bf_prev = None
        ev_ffn_last = None
        for fc in range(8):
            slot = fc % 2
            ev_stt = None
            for g in range(8):
                rr = g % 2
                G = st["graw"]
                st["graw"] += 1
                evc = P.op("gpsimd", lambda e, g=g, rr=rr, fc=fc: e.dma_start(
                    out=rawv[:, rr, :].rearrange("p (c n) -> p c n", c=4),
                    in_=codesT.ap().rearrange("(c p) n -> p c n", p=128)[
                        :, 4 * g:4 * g + 4, fc * 512:fc * 512 + 512]),
                    waits=[st["raw_free"].get(G - 2)], inc="dg", by=16)
                evs = P.op("sync", lambda e, g=g, rr=rr, fc=fc: e.dma_start(
                    out=scxv[:, rr, :].rearrange("p (c n) -> p c n", c=4),
                    in_=scexpT.ap().rearrange("(c p) n -> p c n", p=128)[
                        :, 4 * g:4 * g + 4, fc * 512:fc * 512 + 512]),
                    waits=[st["scx_free"].get((fc, g))], inc="ds", by=16)
                ev_stt = P.op("vector", lambda e, g=g, rr=rr, slot=slot: e.scalar_tensor_tensor(
                    out=wring[:, slot, 4 * g:4 * g + 4, :],
                    in0=rawv[:, rr, :].rearrange("p (c n) -> p c n", c=4), scalar=128.0,
                    in1=scxv[:, rr, :].rearrange("p (c n) -> p c n", c=4),
                    op0=AluOpType.subtract, op1=AluOpType.mult),
                    waits=[evc, evs, ev_wslot_free.get(slot) if g == 0 else None],
                    inc="dve")
                st["raw_free"][G] = ev_stt
                st["scx_free"][(fc, g + 2)] = st["scx_free"][(fc + 1, g - 6)] = ev_stt
            ev_bs = P.op("sync", lambda e, fc=fc: e.dma_start(
                out=rowSc.ap()[:, :], in_=bias_row.ap()[0:1, fc * 512:fc * 512 + 512]),
                waits=[st["rowSc_free"]], inc="ds", by=16)
            ev_bm = P.op("tensor", lambda e: e.matmul(
                pF.ap()[:, 0:512], sb_onesr.ap()[:, :], rowSc.ap()[:, :],
                start=True, stop=True),
                waits=[ev_bs, ev_bf_prev, ev_pF_read if fc == 0 else None], inc="pe")
            st["rowSc_free"] = ev_bm
            ev_bf_prev = P.op("vector", lambda e, slot=slot: e.tensor_copy(
                coF.ap()[:, slot * 512:slot * 512 + 512], pF.ap()[:, 0:512]),
                waits=[ev_bm], inc="dve")
            mm = None
            for tt in range(NT):
                pk = paks[tt % 4]
                for dc in range(DC):
                    mm = P.op("tensor", lambda e, dc=dc, tt=tt, slot=slot, pk=pk: e.matmul(
                        pk, xnT[:, dc, tt * 128:(tt + 1) * 128], wring[:, slot, dc, :],
                        start=(dc == 0), stop=(dc == DC - 1)),
                        waits=([ev_stt, ev_fst_free.get((fc, tt - 4)),
                                c_act_evac.get(60 + tt) if (fc == 0 and tt < 4) else None]
                               if dc == 0 else []),
                        inc=("pe" if dc == DC - 1 else None))
                kr = tt % 4
                eev = P.op("vector", lambda e, kr=kr, slot=slot, pk=pk: e.scalar_tensor_tensor(
                    out=sctF[:, kr, :], in0=pk, scalar=0.0,
                    in1=coF.ap()[:, slot * 512:slot * 512 + 512],
                    op0=AluOpType.add, op1=AluOpType.add),
                    waits=[mm, fst_dma.get((fc, tt - 4)),
                           fst_dma.get((fc - 1, tt + 4))], inc="dve")
                ed = P.op("sync", lambda e, kr=kr, fc=fc, tt=tt: e.dma_start(
                    out=ffn_dram.ap()[tt * 128:(tt + 1) * 128, fc * 512:fc * 512 + 512],
                    in_=sctF[:, kr, :]), waits=[eev], inc="ds", by=16)
                ev_fst_free[(fc, tt)] = ed
                fst_dma[(fc, tt)] = ed
                ev_ffn_last = ed
            ev_wslot_free[slot] = mm
        ev_E1_done = ev_ffn_last
        ev_pF_read = ev_bf_prev

        eg1 = P.op("sync", lambda e: e.dma_start(
            out=kngb.ap()[:, :],
            in_=kp_out.ap().rearrange("(r t) n -> r t n", r=4)[:, T, 0:8]),
            waits=[ev_ccK], inc="ds", by=16)
        eg2 = P.op("tensor", lambda e: e.transpose(
            trb[3][0:8, 0:4], kngb.ap()[:, :], sb_id.ap()[0:4, 0:4]),
            waits=[eg1, ev_pF_read], inc="pe")
        eg3 = P.op("vector", lambda e: e.tensor_copy(
            knloc.ap()[0:8, 0:4], trb[3][0:8, 0:4]), waits=[eg2], inc="dve")
        eg4 = P.op("vector", lambda e: e.tensor_reduce(
            out=knlocm.ap()[:, :], in_=knloc.ap()[0:8, 0:4], axis=mybir.AxisListType.X,
            op=AluOpType.max), waits=[eg3], inc="dve")
        eg5 = P.op("tensor", lambda e: e.transpose(
            pF.ap()[0:1, 0:8], knlocm.ap()[:, :], sb_idf.ap()[0:8, 0:8]),
            waits=[eg4, eg3], inc="pe")
        eg6 = P.op("vector", lambda e: e.tensor_copy(knrow8.ap()[:, :], pF.ap()[0:1, 0:8]),
                   waits=[eg5], inc="dve")
        en1 = P.op("vector", lambda e: e.tensor_tensor(
            out=qn2T.ap()[:, :], in0=qn2A.ap()[:, :], in1=qn2B.ap()[:, :],
            op=AluOpType.max), waits=[ev_C_qn2], inc="dve")
        en2 = None
        for r in range(4):
            en2 = P.op("vector", lambda e, r=r: e.tensor_tensor(
                out=qn2T.ap().rearrange("o (j r) -> o j r", r=4)[:, :, r],
                in0=qn2T.ap().rearrange("o (j r) -> o j r", r=4)[:, :, r],
                in1=knrow8.ap()[:, :], op=AluOpType.mult),
                waits=([en1, eg6] if r == 0 else []), inc="dve")
        en3 = P.op("vector", lambda e: e.tensor_scalar(
            out=qn2T.ap()[:, :], in0=qn2T.ap()[:, :], scalar1=1e-30, scalar2=None,
            op0=AluOpType.max), waits=[en2], inc="dve")
        en4 = P.op("scalar", lambda e: e.activation(
            out=qn2T.ap()[:, :], in_=qn2T.ap()[:, :], func=AF.Sqrt),
            waits=[en3], inc="act")
        en5 = P.op("vector", lambda e: e.tensor_scalar(
            out=qn2T.ap()[:, :], in0=qn2T.ap()[:, :], scalar1=-RSQ, scalar2=None,
            op0=AluOpType.mult), waits=[en4], inc="dve")
        en6 = P.op("tensor", lambda e: e.matmul(
            pF.ap()[:, 0:32], sb_onesr.ap()[:, :], qn2T.ap()[:, :], start=True, stop=True),
            waits=[en5, eg6], inc="pe")
        ev_negC = P.op("vector", lambda e: e.tensor_copy(negC.ap()[:, :], pF.ap()[:, 0:32]),
                       waits=[en6], inc="dve")
        ev_pF_read = ev_negC
        if debug:
            P.op("sync", lambda e: e.dma_start(out=dbg["negc"].ap()[:, :], in_=negC.ap()[:, :]),
                 waits=[ev_negC], inc="ds", by=16)
            P.op("sync", lambda e: e.dma_start(out=dbg["kn2"].ap()[:, :], in_=knrow8.ap()[:, :]),
                 waits=[eg6], inc="ds", by=16)
            P.op("sync", lambda e: e.dma_start(out=dbg["qn2"].ap()[:, :], in_=qn2T.ap()[:, :]),
                 waits=[en5], inc="ds", by=16)
            P.op("sync", lambda e: e.dma_start(out=dbg["cvrow"].ap()[:, :], in_=cvrow.ap()[:, :]),
                 waits=[ev_cv.get(2), ev_cv.get(3)], inc="ds", by=16)
            P.op("sync", lambda e: e.dma_start(out=dbg["cqk"].ap()[:, :], in_=cqk.ap()[:, :]),
                 waits=[ev_C_qn2], inc="ds", by=16)

        ev_kj = {}
        ev_vj = {}
        ev_qh = {}
        ev_kj_free = {}
        ev_vj_free = {}
        ev_qh_free = {}
        ev_pt_free = {}
        st["pts_free"] = {}
        st["ptsum"] = {}
        ev_exp = {}
        ev_expA = {}
        ev_tt = {0: None, 1: None}
        ev_ez = {0: None, 1: None}
        ev_eo2 = {0: None, 1: None}
        gsc = 0
        for j in range(KVH):
            rj = j % 2
            for r4 in range(4):
                ev_kj[j] = P.op("sync", lambda e, j=j, rj=rj, r4=r4: e.dma_start_transpose(
                    out=kjT[:, rj, r4 * 1024:(r4 + 1) * 1024],
                    in_=kp_out.ap().rearrange("(r t) n -> r t n", r=4)[
                        r4, 0:1024, j * 128:(j + 1) * 128]),
                    waits=[ev_ccK if (j == 0 and r4 == 0) else None,
                           ev_E1_done if (j < 2 and r4 == 0) else None,
                           ev_kj_free.get(j - 2) if r4 == 0 else None],
                    inc="ds", by=16)
            ev_vj[j] = P.op("sync", lambda e, j=j, rj=rj: e.dma_start(
                out=vjv[:, rj, :, :],
                in_=vp_out.ap().rearrange("(s p) o -> p s o", p=128)[
                    :, :, j * 128:(j + 1) * 128]),
                waits=[ev_ccV if j == 0 else None,
                       ev_E1_done if j < 2 else None,
                       ev_vj_free.get(j - 2)], inc="ds", by=16)
            for h4 in range(4):
                h = 4 * j + h4
                rh = h % 2
                ev_qh[h] = P.op("sync", lambda e, h=h, rh=rh: e.dma_start(
                    out=qthv[:, rh, :], in_=qT_dram.ap()[h, :, :]),
                    waits=[ev_qh_free.get(h - 2),
                           ev_E1_done if h < 2 else None,
                           ev_qT_dma_last if h == 0 else None], inc="ds", by=16)
                def emit_scores(sc):
                    par = sc % 2
                    pX = pA if par == 0 else pB
                    emm0 = P.op("tensor", lambda e, sc=sc, pX=pX, rj=rj, rh=rh: e.matmul(
                        pX.ap()[:, 0:512], kjT[:, rj, sc * 128:(sc + 1) * 128],
                        qthv[:, rh, 0:512], start=True, stop=True),
                        waits=([ev_expA.get(gsc + sc - 2),
                                ev_qh[h] if sc == 0 else None,
                                ev_kj[j] if (h4 == 0 and sc == 0) else None,
                                ev_negC if (j == 0 and h4 == 0 and sc == 0) else None,
                                ev_eo2[par] if sc < 2 else None]), inc="pe")
                    ev_expA[gsc + sc] = P.op("scalar", lambda e, sc=sc, pX=pX, h=h:
                                             e.activation(
                        out=ptv[:, sc % 4, 0:512], in_=pX.ap()[:, 0:512], func=AF.Exp,
                        scale=RSQ, bias=negC.ap()[:, h:h + 1]),
                        waits=[emm0] + list(ev_pt_free.get(gsc + sc - 4) or ()), inc="act")
                    emm1 = P.op("tensor", lambda e, sc=sc, pX=pX, rj=rj, rh=rh: e.matmul(
                        pX.ap()[:, 512:1024], kjT[:, rj, sc * 128:(sc + 1) * 128],
                        qthv[:, rh, 512:1024], start=True, stop=True),
                        waits=[ev_exp.get(gsc + sc - 2)], inc="pe")
                    ev_exp[gsc + sc] = P.op("scalar", lambda e, sc=sc, pX=pX, h=h:
                                            e.activation(
                        out=ptv[:, sc % 4, 512:1024], in_=pX.ap()[:, 512:1024],
                        func=AF.Exp, scale=RSQ, bias=negC.ap()[:, h:h + 1]),
                        waits=[emm1], inc="act")

                def emit_av(sc):
                    P.op("tensor", lambda e, sc=sc, rj=rj: e.matmul(
                        pC.ap()[:, :], vjv[:, rj, sc, :], ptv[:, sc % 4, 0:512],
                        start=(sc == 0), stop=(sc == DC - 1)),
                        waits=[ev_expA[gsc + sc],
                               ev_tt[0] if sc == 0 else None,
                               ev_vj[j] if (h4 == 0 and sc == 0) else None])
                    ev_av = P.op("tensor", lambda e, sc=sc, rj=rj: e.matmul(
                        pD.ap()[:, :], vjv[:, rj, sc, :], ptv[:, sc % 4, 512:1024],
                        start=(sc == 0), stop=(sc == DC - 1)),
                        waits=[ev_tt[1] if sc == 0 else None], inc="pe")
                    return ev_av

                def emit_ptsum(p):
                    sc = 2 * p + 1
                    pr = p % 2
                    ev_pts = P.op("vector", lambda e, sc=sc, pr=pr: e.tensor_tensor(
                        out=ptsv[:, pr, :], in0=ptv[:, (sc - 1) % 4, :],
                        in1=ptv[:, sc % 4, :], op=AluOpType.add),
                        waits=[ev_exp[gsc + sc], st["pts_free"].get(gsc // 2 + p - 2)],
                        inc="dve")
                    st["ptsum"][gsc // 2 + p] = ev_pts
                    ev_pt_free[gsc + sc - 1] = ev_pt_free[gsc + sc] = (avs[sc], ev_pts)

                def emit_zpair(p):
                    pr = p % 2
                    P.op("tensor", lambda e, p=p, pr=pr: e.matmul(
                        pE.ap()[0:1, 0:512], sb_onesc.ap()[:, :], ptsv[:, pr, 0:512],
                        start=(p == 0), stop=(p == 15)),
                        waits=[st["ptsum"][gsc // 2 + p],
                               ev_ez[0] if p == 0 else None])
                    ev_z1 = P.op("tensor", lambda e, p=p, pr=pr: e.matmul(
                        pF.ap()[0:1, 0:512], sb_onesc.ap()[:, :], ptsv[:, pr, 512:1024],
                        start=(p == 0), stop=(p == 15)),
                        waits=[ev_ez[1] if p == 0 else None], inc="pe")
                    st["pts_free"][gsc // 2 + p] = ev_z1
                    return ev_z1

                avs = {}
                for sc in range(DC):
                    emit_scores(sc)
                    if sc >= 1:
                        avs[sc - 1] = emit_av(sc - 1)
                        if (sc - 1) % 2 == 1:
                            p = (sc - 1) // 2
                            emit_ptsum(p)
                            if p >= 1:
                                emit_zpair(p - 1)
                avs[DC - 1] = emit_av(DC - 1)
                emit_zpair(DC // 2 - 2)
                emit_ptsum(DC // 2 - 1)
                ev_avz = emit_zpair(DC // 2 - 1)
                ev_avz = st["pts_free"][(gsc + DC - 1) // 2]
                gsc += DC
                for thh, (pZ, zr, rzr, pO, pOut, sl) in enumerate(
                        ((pE, zrow, rzrow, pC, pA, 0), (pF, zrow2, rzrow2, pD, pB, 1))):
                    ez1 = P.op("vector", lambda e, pZ=pZ, zr=zr: e.tensor_copy(
                        zr.ap()[:, :], pZ.ap()[0:1, 0:512]), waits=[ev_avz], inc="dve")
                    ev_ez[thh] = ez1
                    el1 = P.op("scalar", lambda e, zr=zr, rzr=rzr: e.activation(
                        out=rzr.ap()[:, :], in_=zr.ap()[:, :], func=AF.Ln),
                        waits=[ez1], inc="act")
                    el2 = P.op("scalar", lambda e, zr=zr, rzr=rzr: e.activation(
                        out=zr.ap()[:, :], in_=rzr.ap()[:, :], func=AF.Exp, scale=-1.0),
                        waits=[el1], inc="act")
                    eo1 = P.op("tensor", lambda e, j=j, zr=zr, pOut=pOut: e.matmul(
                        pOut.ap()[:, 0:512], cvrow.ap()[0:1, j * 128:(j + 1) * 128],
                        zr.ap()[:, :], start=True, stop=True),
                        waits=[el2, ev_expA[gsc - 2 + thh],
                               ev_cv.get(2) if (j == 0 and h4 == 0) else None,
                               ev_cv.get(3) if (j == 0 and h4 == 0) else None], inc="pe")
                    eo2 = P.op("vector", lambda e, pOut=pOut, sl=sl: e.tensor_copy(
                        sctF[:, sl, :], pOut.ap()[:, 0:512]), waits=[eo1], inc="dve")
                    ev_eo2[thh] = eo2
                    ev_tt[thh] = P.op("vector", lambda e, h=h, thh=thh, pO=pO, sl=sl:
                                      e.tensor_tensor(
                        out=attnT[:, h, thh * 512:(thh + 1) * 512], in0=pO.ap()[:, :],
                        in1=sctF[:, sl, :], op=AluOpType.mult), waits=[eo2], inc="dve")
                ev_qh_free[h] = ev_tt[1]
            ev_kj_free[j] = ev_tt[1]
            ev_vj_free[j] = ev_tt[1]
        ev_D_done = ev_tt[1]
        if debug:
            P.op("sync", lambda e: e.dma_start(out=dbg["attnT"].ap()[:, :], in_=arenaA.ap()[:, :]),
                 waits=[ev_D_done], inc="ds", by=16)

        seed_ring_barrier(ev_D_done)
        ev_x2 = {}
        ev_f2 = {}
        ev_x2_free = {}
        ev_out_dma = {}
        ev_cof_prev = None
        opd = {0: wprep_load(woT, 0, 0, ev_wslot_free.get(0))}
        for g in range(8):
            wprep_smm(opd[0], g)
        o_crow = {0: wprep_finish(opd[0], osc_row, 0)}

        def emit_cof(oc, ev_crow):
            nonlocal ev_cof_prev
            ecm = P.op("tensor", lambda e: e.matmul(
                pF.ap()[:, 0:512], sb_onesr.ap()[:, :], rowC.ap()[:, :],
                start=True, stop=True),
                waits=[ev_crow, ev_cof_prev, ev_pF_read if oc == 0 else None], inc="pe")
            ecf = P.op("vector", lambda e, oc=oc: e.tensor_copy(
                coF.ap()[:, (oc % 2) * 512:(oc % 2) * 512 + 512], pF.ap()[:, 0:512]),
                waits=[ecm], inc="dve")
            ev_cof_prev = ecf
            st["rowC_free"] = ecm
            return ecf

        cof_map = {0: emit_cof(0, o_crow[0])}
        for oc in range(8):
            slot = oc % 2
            ev_sign = opd[oc]["sign"]
            ecf = cof_map[oc]
            if oc + 1 < 8:
                opd[oc + 1] = wprep_load(woT, (oc + 1) * 512, (oc + 1) % 2,
                                         ev_wslot_free.get((oc + 1) % 2))
            mm = None
            for tt in range(NT):
                idx = oc * NT + tt
                rx = idx % 2
                ev_x2[idx] = P.op("sync", lambda e, rx=rx, oc=oc, tt=tt: e.dma_start(
                    out=xeE2[:, rx, :],
                    in_=x_loc.ap()[tt * 128:(tt + 1) * 128, oc * 512:oc * 512 + 512]),
                    waits=[ev_out_dma.get(idx - 2),
                           ev_E1_done if idx < 2 else None], inc="ds", by=16)
                ev_f2[idx] = P.op("sync", lambda e, rx=rx, oc=oc, tt=tt: e.dma_start(
                    out=xeE2[:, 2 + rx, :],
                    in_=ffn_dram.ap()[tt * 128:(tt + 1) * 128, oc * 512:oc * 512 + 512]),
                    waits=[ev_out_dma.get(idx - 2),
                           ev_E1_done if idx == 0 else None], inc="ds", by=16)
                pk = paks[tt % 4]
                for ac in range(H):
                    mm = P.op("tensor", lambda e, ac=ac, tt=tt, slot=slot, pk=pk: e.matmul(
                        pk, attnT[:, ac, tt * 128:(tt + 1) * 128], wring[:, slot, ac, :],
                        start=(ac == 0), stop=(ac == H - 1)),
                        waits=([ev_sign, ev_D_done if idx == 0 else None,
                                ev_out_dma.get(idx - 4)] if ac == 0 else []),
                        inc=("pe" if ac == H - 1 else None))
                if oc + 1 < 8:
                    wprep_smm(opd[oc + 1], tt)
                kr = tt % 2
                ecb1 = P.op("vector", lambda e, kr=kr, oc=oc, pk=pk: e.tensor_tensor(
                    out=sctF[:, 2 * kr, :], in0=pk,
                    in1=coF.ap()[:, (oc % 2) * 512:(oc % 2) * 512 + 512],
                    op=AluOpType.mult), waits=[mm, ecf], inc="dve")
                ecb2 = P.op("vector", lambda e, kr=kr, rx=rx: e.tensor_tensor(
                    out=sctF[:, 2 * kr + 1, :], in0=sctF[:, 2 * kr, :], in1=xeE2[:, rx, :],
                    op=AluOpType.add), waits=[ecb1, ev_x2[idx]], inc="dve")
                ev_x2_free[idx] = ecb2
                ecb3 = P.op("vector", lambda e, kr=kr, rx=rx: e.tensor_tensor(
                    out=xeE2[:, 4 + rx, :], in0=sctF[:, 2 * kr + 1, :],
                    in1=xeE2[:, 2 + rx, :], op=AluOpType.add),
                    waits=[ecb2, ev_f2[idx], ev_out_dma.get(idx - 2)], inc="dve")
                last_out = P.op("sync", lambda e, rx=rx, oc=oc, tt=tt: e.dma_start(
                    out=out_d.ap()[tt * 128:(tt + 1) * 128, oc * 512:oc * 512 + 512],
                    in_=xeE2[:, 4 + rx, :]), waits=[ecb3], inc="ds", by=16)
                ev_out_dma[idx] = last_out
            ev_wslot_free[slot] = mm
            if oc + 1 < 8:
                crow = wprep_finish(opd[oc + 1], osc_row, (oc + 1) * 512)
                cof_map[oc + 1] = emit_cof(oc + 1, crow)

        P.wait("sync", ("ds", P.cnt["ds"]))
        P.wait("gpsimd", ("dg", P.cnt["dg"]))

        @block.tensor
        def _(t):
            for f in P.prog["tensor"]:
                f(t, sems)

        @block.vector
        def _(v):
            for f in P.prog["vector"]:
                f(v, sems)

        @block.scalar
        def _(s):
            for f in P.prog["scalar"]:
                f(s, sems)

        @block.sync
        def _(s):
            for f in P.prog["sync"]:
                f(s, sems)

        @block.gpsimd
        def _(g):
            for f in P.prog["gpsimd"]:
                f(g, sems)

    return nc, P


def host_inputs(inputs):
    x = np.ascontiguousarray(np.asarray(inputs["L0_attn_hidden_states"], np.float32))
    wq = np.asarray(inputs["L0_attn_q_weight"], np.float32)
    wk = np.asarray(inputs["L0_attn_k_weight"], np.float32)
    wv = np.asarray(inputs["L0_attn_v_weight"], np.float32)
    wo = np.asarray(inputs["L0_attn_o_weight"], np.float32)
    qs = np.asarray(inputs["L0_attn_q_scale"], np.float32)
    ks = np.asarray(inputs["L0_attn_k_scale"], np.float32)
    vs = np.asarray(inputs["L0_attn_v_scale"], np.float32)
    osc = np.asarray(inputs["L0_attn_o_scale"], np.float32)
    q_scales = np.asarray(inputs["L0_ffn_q_scales"], np.float32)
    q_vals = np.asarray(inputs["L0_ffn_q_vals"], np.int32)
    fb = np.asarray(inputs["L0_ffn_bias"], np.float32)

    com = dict(
        wqT=np.ascontiguousarray(wq.T),
        wkvT=np.ascontiguousarray(np.concatenate([wk.T, wv.T], axis=1)),
        woT=np.ascontiguousarray(wo.T),
        codesT=np.ascontiguousarray(q_vals.reshape(4096, 4096).T),
        scexpT=np.ascontiguousarray(np.repeat(q_scales.T, 32, axis=0)),
        qsc_row=np.ascontiguousarray(qs[None, :]),
        kvsc_row=np.ascontiguousarray(np.concatenate([ks, vs])[None, :]),
        osc_row=np.ascontiguousarray(osc[None, :]),
        bias_row=np.ascontiguousarray(fb[None, :]),
        identity=np.eye(128, dtype=NB),
        identity_f=np.eye(128, dtype=np.float32),
        onesc_d=np.ones((128, 1), NB),
        onesr_d=np.ones((1, 128), np.float32),
    )
    in_maps = []
    for c in range(8):
        b, q = c // 4, c % 4
        m = dict(com)
        m["x_loc"] = np.ascontiguousarray(x[b, q * T:(q + 1) * T, :])
        in_maps.append(m)
    return in_maps


def kernel(**inputs):
    in_maps = host_inputs(inputs)
    nc, _ = build_nc(debug=False)
    res = run_bass_kernel_spmd(nc, in_maps, core_ids=list(range(8)))
    out = np.empty((2, 4096, 4096), np.float32)
    for c in range(8):
        b, q = c // 4, c % 4
        out[b, q * T:(q + 1) * T, :] = res.results[c]["out"]
    return out


if __name__ == "__main__":
    nc, P = build_nc(debug=False)
    print("instructions (incl waits):", P.ni, "total:", sum(P.ni.values()))
    print("sem totals:", P.cnt)


# version 11
# speedup vs baseline: 1.1016x; 1.0277x over previous
import contextlib
import numpy as np
import ml_dtypes

import concourse.bass as bass
import concourse.mybir as mybir
from concourse.bass_utils import run_bass_kernel_spmd
from concourse.alu_op_type import AluOpType

F32 = mybir.dt.float32
BF16 = mybir.dt.bfloat16
I32 = mybir.dt.int32
AF = mybir.ActivationFunctionType
NB = np.dtype(ml_dtypes.bfloat16)

D = 4096
T = 1024
S = 4096
NT = 8
DC = 32
H = 32
KVH = 8
RSQ = float(1.0 / np.sqrt(np.float32(128.0)))

ENGS = ("tensor", "vector", "scalar", "sync", "gpsimd")


class Prog:

    def __init__(self):
        self.prog = {e: [] for e in ENGS}
        self.cnt = {}
        self.ni = {e: 0 for e in ENGS}

    def ev(self, sem, by):
        self.cnt[sem] = self.cnt.get(sem, 0) + by
        return (sem, self.cnt[sem])

    def op(self, eng, make, waits=(), inc=None, by=1):
        waits = tuple(w for w in waits if w is not None)
        e = self.ev(inc, by) if inc else None
        self.ni[eng] += 1 + len(waits)

        def f(engobj, sems, _w=waits, _m=make, _e=e, _by=by):
            for (s, v) in _w:
                engobj.wait_ge(sems[s], v)
            ins = _m(engobj)
            if _e is not None:
                ins.then_inc(sems[_e[0]], _by)
        self.prog[eng].append(f)
        return e

    def wait(self, eng, *evs):
        evs = tuple(e for e in evs if e is not None)
        if not evs:
            return
        self.ni[eng] += len(evs)

        def f(engobj, sems, _w=evs):
            for (s, v) in _w:
                engobj.wait_ge(sems[s], v)
        self.prog[eng].append(f)


def build_nc(debug=False):
    nc = bass.Bass("TRN2", target_bir_lowering=False)

    x_loc = nc.dram_tensor("x_loc", [T, D], F32, kind="ExternalInput")
    wqT = nc.dram_tensor("wqT", [D, D], F32, kind="ExternalInput")
    wkvT = nc.dram_tensor("wkvT", [D, 2048], F32, kind="ExternalInput")
    woT = nc.dram_tensor("woT", [D, D], F32, kind="ExternalInput")
    codesT = nc.dram_tensor("codesT", [D, D], I32, kind="ExternalInput")
    scexpT = nc.dram_tensor("scexpT", [D, D], F32, kind="ExternalInput")
    qsc_row = nc.dram_tensor("qsc_row", [1, D], F32, kind="ExternalInput")
    kvsc_row = nc.dram_tensor("kvsc_row", [1, 2048], F32, kind="ExternalInput")
    osc_row = nc.dram_tensor("osc_row", [1, D], F32, kind="ExternalInput")
    bias_row = nc.dram_tensor("bias_row", [1, D], F32, kind="ExternalInput")
    identity = nc.dram_tensor("identity", [128, 128], BF16, kind="ExternalInput")
    identity_f = nc.dram_tensor("identity_f", [128, 128], F32, kind="ExternalInput")
    onesc_d = nc.dram_tensor("onesc_d", [128, 1], BF16, kind="ExternalInput")
    onesr_d = nc.dram_tensor("onesr_d", [1, 128], F32, kind="ExternalInput")

    out_d = nc.dram_tensor("out", [T, D], F32, kind="ExternalOutput")

    qT_dram = nc.dram_tensor("qT_dram", [H, 128, T], BF16)
    KP_COLS = 1040
    kp_in = nc.dram_tensor("kp_in", [T + 1, KP_COLS], BF16)
    kp_out = nc.dram_tensor("kp_out", [4 * (T + 1), KP_COLS], BF16)
    vp_in = nc.dram_tensor("vp_in", [T, 1024], BF16)
    vp_out = nc.dram_tensor("vp_out", [S, 1024], BF16)
    ffn_dram = nc.dram_tensor("ffn_dram", [T, D], F32)

    dbg = {}
    if debug:
        for nm, sh, dt in (("xnT", [128, DC * T], BF16), ("attnT", [128, H * T], BF16),
                           ("negc", [128, 32], F32), ("kn2", [1, 8], F32),
                           ("qn2", [1, 32], F32), ("cvrow", [1, 1024], F32),
                           ("cqk", [128, 32], F32), ("qT", [H * 128, T], BF16),
                           ("ffn", [T, D], F32),
                           ("kp", [4 * (T + 1), KP_COLS], BF16),
                           ("vp", [S, 1024], BF16)):
            dbg[nm] = nc.dram_tensor("dbg_" + nm, sh, dt, kind="ExternalOutput")

    P = Prog()
    es = contextlib.ExitStack()
    with es:
        t_ = lambda *a: es.enter_context(nc.sbuf_tensor(*a))
        arenaA = t_("arenaA", [128, DC * 1024], BF16)
        wbuf = t_("wbuf", [128, 32768], BF16)
        scrF = t_("scrF", [128, 4096], F32)
        scrB = t_("scrB", [128, 4096], BF16)
        stageB = t_("stageB", [128, 4096], BF16)
        stageF = t_("stageF", [128, 2048], F32)
        kn2a = t_("kn2a", [128, 8], F32)
        kn2b = t_("kn2b", [128, 8], F32)
        kn2c = t_("kn2c", [128, 8], F32)
        coF = t_("coF", [128, 1024], F32)
        sb_id = t_("sb_id", [128, 128], BF16)
        sb_idf = t_("sb_idf", [128, 128], F32)
        sb_onesc = t_("sb_onesc", [128, 1], BF16)
        sb_onesr = t_("sb_onesr", [1, 128], F32)
        epsv = t_("epsv", [128, 1], F32)
        lstat = [t_("lstatA", [128, 8], F32), t_("lstatB", [128, 8], F32)]
        lsq = [t_("lsqA", [128, 4], F32), t_("lsqB", [128, 4], F32)]
        rowS = t_("rowS", [1, 512], F32)
        rowC = t_("rowC", [1, 512], F32)
        rowSc = t_("rowSc", [1, 512], F32)
        rowB = t_("rowB", [1, 512], F32)
        rowB2 = t_("rowB2", [1, 512], F32)
        cvrow = t_("cvrow", [1, 1024], F32)
        zrow = t_("zrow", [1, 512], F32)
        rzrow = t_("rzrow", [1, 512], F32)
        zrow2 = t_("zrow2", [1, 512], F32)
        rzrow2 = t_("rzrow2", [1, 512], F32)
        qn2A = t_("qn2A", [1, 32], F32)
        qn2B = t_("qn2B", [1, 32], F32)
        qn2T = t_("qn2T", [1, 32], F32)
        kngb = t_("kngb", [4, 8], BF16)
        knrow8 = t_("knrow8", [1, 8], F32)
        knloc = t_("knloc", [8, 128], F32)
        knlocm = t_("knlocm", [8, 1], F32)
        knlocb = t_("knlocb", [1, 8], BF16)
        negC = t_("negC", [128, 32], F32)
        ckcols = t_("ckcols", [128, 8], F32)
        cqcols = t_("cqcols", [128, 32], F32)
        cqk = t_("cqk", [128, 32], F32)

        pA = es.enter_context(nc.psum_tensor("pA", [128, 1024], F32))
        pB = es.enter_context(nc.psum_tensor("pB", [128, 1024], F32))
        pC = es.enter_context(nc.psum_tensor("pC", [128, 512], F32))
        pD = es.enter_context(nc.psum_tensor("pD", [128, 512], F32))
        pE = es.enter_context(nc.psum_tensor("pE", [128, 512], F32))
        pF = es.enter_context(nc.psum_tensor("pF", [128, 512], F32))

        semnames = ["pe", "act", "dve", "ds", "dg", "cc"]
        sems = {n: es.enter_context(nc.semaphore(n)) for n in semnames}
        block = es.enter_context(nc.Block())

        xnT = arenaA.ap().rearrange("p (c t) -> p c t", c=DC)
        attnT = arenaA.ap().rearrange("p (c t) -> p c t", c=H)
        wring = wbuf.ap().rearrange("p (r c n) -> p r c n", r=2, c=DC)
        rawv = scrB.ap().rearrange("p (r n) -> p r n", r=2)
        junkv = wbuf.ap()[:, 0:4096]
        kjT = wbuf.ap()[:, 0:8192].rearrange("p (r s) -> p r s", r=2)
        vjv = wbuf.ap()[:, 8192:16384].rearrange("p (r s c) -> p r s c", r=2, s=32)
        qthv = wbuf.ap()[:, 16384:18432].rearrange("p (r n) -> p r n", r=2)
        ptv = wbuf.ap()[:, 18432:22528].rearrange("p (r n) -> p r n", r=4)
        qstv = stageB.ap().rearrange("p (r n) -> p r n", r=8)
        xbuf = scrF.ap()
        xeE2 = scrF.ap().rearrange("p (r n) -> p r n", r=8)
        scxv = scrF.ap().rearrange("p (r n) -> p r n", r=2)
        sctF = stageF.ap().rearrange("p (r n) -> p r n", r=4)
        q2v = stageF.ap().bitcast(BF16)[:, 0:512]
        xnb = stageB.ap()
        ptsv = stageB.ap()[:, 0:2048].rearrange("p (r n) -> p r n", r=2)
        paks = [pA.ap()[:, 0:512], pA.ap()[:, 512:1024],
                pB.ap()[:, 0:512], pB.ap()[:, 512:1024]]
        trb = [p.ap().bitcast(BF16) for p in (pC, pD, pE, pF)]

        for dst, src in ((sb_id, identity), (sb_idf, identity_f),
                         (sb_onesc, onesc_d), (sb_onesr, onesr_d)):
            P.op("sync", lambda e, d=dst, s=src: e.dma_start(out=d.ap()[:, :], in_=s.ap()[:, :]),
                 inc="ds", by=16)
        evA = ("ds", P.cnt["ds"])
        ev_eps = P.op("vector", lambda e: e.memset(epsv.ap()[:, :], 1e-5), inc="dve")
        P.op("vector", lambda e: e.memset(qn2A.ap()[:, :], -3.0e38), inc="dve")
        ev_qn2i = P.op("vector", lambda e: e.memset(qn2B.ap()[:, :], -3.0e38), inc="dve")

        ev_xn = [None] * NT
        ev_junk = [None] * NT
        ev_evac = {}
        ev_tp = {}
        for tt in range(NT):
            r = tt % 2
            ev_x = P.op(
                "sync",
                lambda e, tt=tt: e.dma_start(
                    out=xbuf[:, :], in_=x_loc.ap()[tt * 128:(tt + 1) * 128, :]),
                waits=[ev_xn[tt - 1] if tt >= 1 else None,
                       ev_junk[tt - 1] if tt >= 1 else None], inc="ds", by=16)
            ls, lq = lstat[r], lsq[r]
            e1 = P.op("vector", lambda e, ls=ls: e.tensor_reduce(
                out=ls.ap()[:, 0:1], in_=xbuf[:, 0:2048], axis=mybir.AxisListType.X,
                op=AluOpType.add), waits=[ev_x], inc="dve")
            e2 = P.op("vector", lambda e, ls=ls: e.tensor_reduce(
                out=ls.ap()[:, 1:2], in_=xbuf[:, 2048:4096], axis=mybir.AxisListType.X,
                op=AluOpType.add), inc="dve")
            ea1 = P.op("scalar", lambda e, lq=lq: e.activation(
                out=junkv[:, 0:2048], in_=xbuf[:, 0:2048], func=AF.Square,
                accum_out=lq.ap()[:, 0:1]), waits=[ev_x], inc="act")
            ev_junk[tt] = ea2 = P.op("scalar", lambda e, lq=lq: e.activation(
                out=junkv[:, 2048:4096], in_=xbuf[:, 2048:4096], func=AF.Square,
                accum_out=lq.ap()[:, 1:2]), inc="act")
            e3 = P.op("vector", lambda e, ls=ls: e.tensor_tensor(
                out=ls.ap()[:, 2:3], in0=ls.ap()[:, 0:1], in1=ls.ap()[:, 1:2],
                op=AluOpType.add), waits=[e1, e2], inc="dve")
            e4 = P.op("vector", lambda e, ls=ls: e.tensor_scalar(
                out=ls.ap()[:, 3:4], in0=ls.ap()[:, 2:3], scalar1=1.0 / D, scalar2=None,
                op0=AluOpType.mult), waits=[e3], inc="dve")
            e5 = P.op("vector", lambda e, ls=ls: e.tensor_tensor(
                out=ls.ap()[:, 4:5], in0=ls.ap()[:, 3:4], in1=ls.ap()[:, 3:4],
                op=AluOpType.mult), waits=[e4], inc="dve")
            e6 = P.op("vector", lambda e, lq=lq: e.tensor_tensor(
                out=lq.ap()[:, 2:3], in0=lq.ap()[:, 0:1], in1=lq.ap()[:, 1:2],
                op=AluOpType.add), waits=[ea1, ea2, e5], inc="dve")
            e7 = P.op("vector", lambda e, ls=ls, lq=lq: e.tensor_scalar(
                out=ls.ap()[:, 5:6], in0=lq.ap()[:, 2:3], scalar1=1.0 / D,
                scalar2=ls.ap()[:, 4:5], op0=AluOpType.mult, op1=AluOpType.subtract),
                waits=[e6], inc="dve")
            e8 = P.op("scalar", lambda e, ls=ls: e.activation(
                out=ls.ap()[:, 6:7], in_=ls.ap()[:, 5:6], func=AF.Sqrt,
                bias=epsv.ap()[:, :]), waits=[e7, ev_eps], inc="act")
            e9 = P.op("vector", lambda e, ls=ls: e.reciprocal(
                out=ls.ap()[:, 7:8], in_=ls.ap()[:, 6:7]), waits=[e8], inc="dve")
            P.op("vector", lambda e, ls=ls: e.tensor_scalar(
                out=xnb[:, 0:2048], in0=xbuf[:, 0:2048], scalar1=ls.ap()[:, 3:4],
                scalar2=ls.ap()[:, 7:8], op0=AluOpType.subtract, op1=AluOpType.mult),
                waits=[e9, ev_tp.get((tt - 1, 3))], inc="dve")
            ev_xn[tt] = P.op("vector", lambda e, ls=ls: e.tensor_scalar(
                out=xnb[:, 2048:4096], in0=xbuf[:, 2048:4096], scalar1=ls.ap()[:, 3:4],
                scalar2=ls.ap()[:, 7:8], op0=AluOpType.subtract, op1=AluOpType.mult),
                inc="dve")
            for g in range(4):
                for i in range(8):
                    dc = 8 * g + i
                    ee = P.op("tensor", lambda e, g=g, i=i, dc=dc: e.transpose(
                        trb[g][:, i * 128:(i + 1) * 128],
                        xnb[:, dc * 128:(dc + 1) * 128], sb_id.ap()[:, :]),
                        waits=([ev_xn[tt], ev_evac.get((tt - 1, g)),
                                evA if (tt == 0 and g == 0) else None] if i == 0 else []),
                        inc=("pe" if i == 7 else None))
                    if i == 7:
                        ev_tp[(tt, g)] = ee
                ev_evac[(tt, g)] = P.op("vector", lambda e, tt=tt, g=g: e.tensor_copy(
                    xnT[:, 8 * g:8 * g + 8, tt * 128:(tt + 1) * 128],
                    trb[g].rearrange("p (c t) -> p c t", c=8)),
                    waits=[ev_tp[(tt, g)]], inc="dve")
        ev_B1_last = ev_evac[(NT - 1, 3)]
        ev_B1_junk = ev_junk[NT - 1]

        if debug:
            P.op("sync", lambda e: e.dma_start(out=dbg["xnT"].ap()[:, :], in_=arenaA.ap()[:, :]),
                 waits=[ev_B1_last], inc="ds", by=16)

        st = {
            "raw_free": {}, "abs_free": {}, "graw": 0, "gabs": 0,
            "rowSc_free": None, "pE_free": None,
        }
        st["raw_free"][-1] = st["raw_free"][-2] = None
        st["abs_free"][-1] = st["abs_free"][-2] = None

        def seed_ring_barrier(ev):
            g = st["graw"]
            st["raw_free"][g - 1] = st["raw_free"][g - 2] = ev

        def wprep_new(src, col0, slot, wslot_free_ev):
            return {"src": src, "col0": col0, "slot": slot, "wslot": wslot_free_ev,
                    "dma": [None] * 8, "abs": [None] * 8, "smm": [None] * 8,
                    "sign": None}

        def wprep_p1(pd, g):
            rr = g % 2
            G = st["graw"]
            st["graw"] += 1
            slot = pd["slot"]
            pd["dma"][g] = P.op("gpsimd", lambda e, g=g, rr=rr, pd=pd: e.dma_start(
                out=rawv[:, rr, :].rearrange("p (c n) -> p c n", c=4),
                in_=pd["src"].ap().rearrange("(c p) n -> p c n", p=128)[
                    :, 4 * g:4 * g + 4, pd["col0"]:pd["col0"] + 512]),
                waits=[st["raw_free"].get(G - 2)], inc="dg", by=16)
            pd["abs"][g] = P.op("vector", lambda e, g=g, rr=rr, slot=slot: e.scalar_tensor_tensor(
                out=wring[:, slot, 4 * g:4 * g + 4, :],
                in0=rawv[:, rr, :].rearrange("p (c n) -> p c n", c=4), scalar=-1.0,
                in1=rawv[:, rr, :].rearrange("p (c n) -> p c n", c=4),
                op0=AluOpType.mult, op1=AluOpType.max),
                waits=[pd["dma"][g], pd["wslot"] if g == 0 else None], inc="dve")
            pd["G" + str(g)] = G

        def wprep_p2(pd, g):
            rr = g % 2
            slot = pd["slot"]
            ev_s = None
            for i in range(4):
                ev_s = P.op("tensor", lambda e, slot=slot, g=g, i=i: e.matmul(
                    pE.ap()[0:1, 0:512], sb_onesc.ap()[:, :], wring[:, slot, 4 * g + i, :],
                    start=(g == 0 and i == 0), stop=(g == 7 and i == 3)),
                    waits=([pd["abs"][g]] if i == 0 else []) +
                          ([st["pE_free"]] if (g == 0 and i == 0) else []),
                    inc="pe")
            pd["smm"][g] = ev_s
            pd["sign"] = P.op("scalar", lambda e, g=g, rr=rr, slot=slot: e.activation(
                out=wring[:, slot, 4 * g:4 * g + 4, :],
                in_=rawv[:, rr, :].rearrange("p (c n) -> p c n", c=4), func=AF.Sign),
                waits=[pd["dma"][g], ev_s], inc="act")
            pd.setdefault("sign_g", [None] * 8)[g] = pd["sign"]
            st["raw_free"][pd["G" + str(g)]] = pd["sign"]


        def wprep_sched(pd, tt):
            if tt == 0:
                wprep_p1(pd, 0)
                wprep_p1(pd, 1)
                wprep_p2(pd, 0)
            elif tt < 7:
                wprep_p1(pd, tt + 1)
                wprep_p2(pd, tt)
            else:
                wprep_p2(pd, 7)

        def wprep_all(pd):
            for g in range(8):
                wprep_p1(pd, g)
                wprep_p2(pd, g)

        def wprep_finish(pd, scale_src, scale_off):
            ev_sc = P.op("sync", lambda e, pd=pd: e.dma_start(
                out=rowSc.ap()[:, :],
                in_=scale_src.ap()[0:1, scale_off:scale_off + 512]),
                waits=[st["rowSc_free"]], inc="ds", by=16)
            ec1 = P.op("vector", lambda e: e.tensor_copy(rowS.ap()[:, :], pE.ap()[0:1, 0:512]),
                       waits=[pd["smm"][7]], inc="dve")
            st["pE_free"] = ec1
            ec2 = P.op("vector", lambda e: e.tensor_scalar(
                out=rowS.ap()[:, :], in0=rowS.ap()[:, :], scalar1=1.0 / D, scalar2=1e-8,
                op0=AluOpType.mult, op1=AluOpType.max), waits=[ec1], inc="dve")
            ev_crow = P.op("vector", lambda e: e.tensor_tensor(
                out=rowC.ap()[:, :], in0=rowS.ap()[:, :], in1=rowSc.ap()[:, :],
                op=AluOpType.mult), waits=[ec2, ev_sc, st.get("rowC_free")], inc="dve")
            st["rowSc_free"] = ev_crow
            return ev_crow

        ev_wslot_free = {}
        ev_kst_free = {}
        kst_dma = {}
        ev_kp_dma = []
        ev_vp_dma = []
        ev_pF_read = ev_B1_last
        ev_kn_prev = None
        ev_ck = {}
        ev_cv = {}
        seed_ring_barrier(ev_B1_junk)

        kvpd = {}
        kvpd[0] = wprep_new(wkvT, 0, 0, ev_B1_junk)
        wprep_all(kvpd[0])
        crow = wprep_finish(kvpd[0], kvsc_row, 0)
        kv_crow = {0: crow}

        def emit_kv_c(kc, ev_crow):
            nonlocal_state = {}
            if kc < 2:
                evt = None
                for i in range(4):
                    evt = P.op("tensor", lambda e, i=i: e.transpose(
                        pF.ap()[:, i:i + 1], rowC.ap()[0:1, i * 128:(i + 1) * 128],
                        sb_idf.ap()[0:1, 0:1]),
                        waits=([ev_crow, st["pF_read"]] if i == 0 else []),
                        inc=("pe" if i == 3 else None))
                ev_ck[kc] = P.op("vector", lambda e, kc=kc: e.tensor_copy(
                    ckcols.ap()[:, 4 * kc:4 * kc + 4], pF.ap()[:, 0:4]),
                    waits=[evt], inc="dve")
                st["pF_read"] = ev_ck[kc]
                st["rowC_free"] = ev_ck[kc]
            else:
                ev_cv[kc] = P.op("vector", lambda e, kc=kc: e.tensor_copy(
                    cvrow.ap()[0:1, (kc - 2) * 512:(kc - 1) * 512], rowC.ap()[:, :]),
                    waits=[ev_crow], inc="dve")
                st["rowC_free"] = ev_cv[kc]

        st["pF_read"] = ev_pF_read
        emit_kv_c(0, crow)
        for kc in range(4):
            slot = kc % 2
            ev_sign = kvpd[kc]["sign"]
            if kc + 1 < 4:
                kvpd[kc + 1] = wprep_new(wkvT, (kc + 1) * 512, (kc + 1) % 2,
                                         ev_wslot_free.get((kc + 1) % 2))
            else:
                kvpd[4] = wprep_new(wqT, 0, 0, ev_wslot_free.get(0))
            mm = None
            for tt in range(NT):
                idx = kc * NT + tt
                pk = paks[idx % 4]
                for dc in range(DC):
                    mm = P.op("tensor", lambda e, dc=dc, tt=tt, slot=slot, pk=pk: e.matmul(
                        pk, xnT[:, dc, tt * 128:(tt + 1) * 128], wring[:, slot, dc, :],
                        start=(dc == 0), stop=(dc == DC - 1)),
                        waits=([kvpd[kc]["sign_g"][dc // 4]] if (dc % 4 == 0 and tt == 0)
                               else []) +
                              ([ev_B1_last, ev_kst_free.get(idx - 4)] if dc == 0 else []),
                        inc=("pe" if dc == DC - 1 else None))
                nxt = kvpd[kc + 1]
                wprep_sched(nxt, tt)
                kr = idx % 4
                ev_ev = P.op("vector", lambda e, kr=kr, pk=pk: e.tensor_copy(
                    qstv[:, kr, :], pk), waits=[mm, kst_dma.get(idx - 4)], inc="dve")
                ev_kst_free[idx] = ev_ev
                if kc < 2:
                    ev_d = P.op("sync", lambda e, kr=kr, kc=kc, tt=tt: e.dma_start(
                        out=kp_in.ap()[tt * 128:(tt + 1) * 128, kc * 512:kc * 512 + 512],
                        in_=qstv[:, kr, :]), waits=[ev_ev], inc="ds", by=16)
                    ev_kp_dma.append(ev_d)
                    kst_dma[idx] = ev_d
                    eq = P.op("vector", lambda e, kr=kr: e.tensor_tensor(
                        out=q2v, in0=qstv[:, kr, :], in1=qstv[:, kr, :],
                        op=AluOpType.mult), waits=[ev_ev], inc="dve")
                    er = None
                    for i in range(4):
                        er = P.op("vector", lambda e, i=i, kc=kc: e.tensor_reduce(
                            out=kn2a.ap()[:, 4 * kc + i:4 * kc + i + 1],
                            in_=q2v[:, i * 128:(i + 1) * 128],
                            axis=mybir.AxisListType.X, op=AluOpType.add),
                            waits=([eq] if i == 0 else []), inc="dve")
                    cols = (4 * kc, 4 * kc + 4)
                    if tt == 0:
                        ev_kn_prev = P.op("vector", lambda e, cols=cols: e.tensor_copy(
                            kn2b.ap()[:, cols[0]:cols[1]], kn2a.ap()[:, cols[0]:cols[1]]),
                            waits=[er], inc="dve")
                    else:
                        src_t = kn2b if tt % 2 == 1 else kn2c
                        dst_t = kn2c if tt % 2 == 1 else kn2b
                        ev_kn_prev = P.op("vector",
                                          lambda e, cols=cols, s=src_t, d=dst_t: e.tensor_tensor(
                            out=d.ap()[:, cols[0]:cols[1]], in0=s.ap()[:, cols[0]:cols[1]],
                            in1=kn2a.ap()[:, cols[0]:cols[1]], op=AluOpType.max),
                            waits=[er, ev_kn_prev], inc="dve")
                else:
                    ev_d = P.op("sync", lambda e, kr=kr, kc=kc, tt=tt: e.dma_start(
                        out=vp_in.ap()[tt * 128:(tt + 1) * 128,
                                       (kc - 2) * 512:(kc - 2) * 512 + 512],
                        in_=qstv[:, kr, :]), waits=[ev_ev], inc="ds", by=16)
                    ev_vp_dma.append(ev_d)
                    kst_dma[idx] = ev_d
            ev_wslot_free[slot] = mm
            if kc + 1 < 4:
                crow = wprep_finish(nxt, kvsc_row, (kc + 1) * 512)
                emit_kv_c(kc + 1, crow)
            else:
                q_crow0 = wprep_finish(nxt, qsc_row, 0)
        ev_pF_read = st["pF_read"]

        knfin = kn2c
        et1 = P.op("tensor", lambda e: e.transpose(
            pF.ap()[0:8, 0:128], knfin.ap()[:, 0:8], sb_idf.ap()[:, :]),
            waits=[ev_kn_prev, ev_pF_read], inc="pe")
        et2 = P.op("vector", lambda e: e.tensor_copy(knloc.ap()[:, :], pF.ap()[0:8, 0:128]),
                   waits=[et1], inc="dve")
        et3 = P.op("vector", lambda e: e.tensor_reduce(
            out=knlocm.ap()[:, :], in_=knloc.ap()[:, :], axis=mybir.AxisListType.X,
            op=AluOpType.max), waits=[et2], inc="dve")
        et4 = P.op("tensor", lambda e: e.transpose(
            pF.ap()[0:1, 0:8], knlocm.ap()[:, :], sb_idf.ap()[0:8, 0:8]),
            waits=[et3], inc="pe")
        et5 = P.op("vector", lambda e: e.tensor_copy(knlocb.ap()[:, :], pF.ap()[0:1, 0:8]),
                   waits=[et4], inc="dve")
        ev_pF_read = et5
        ev_knrow_dma = P.op("sync", lambda e: e.dma_start(
            out=kp_in.ap()[T:T + 1, 0:8], in_=knlocb.ap()[:, :]),
            waits=[et5], inc="ds", by=16)

        P.wait("gpsimd", ev_kp_dma[-1], ev_knrow_dma)
        ev_ccK = P.op("gpsimd", lambda e: e.collective_compute(
            "AllGather", mybir.AluOpType.bypass,
            replica_groups=[[0, 1, 2, 3], [4, 5, 6, 7]],
            ins=[kp_in.ap().opt()], outs=[kp_out.ap().opt()]), inc="cc")
        P.wait("gpsimd", ev_vp_dma[-1])
        ev_ccV = P.op("gpsimd", lambda e: e.collective_compute(
            "AllGather", mybir.AluOpType.bypass,
            replica_groups=[[0, 1, 2, 3], [4, 5, 6, 7]],
            ins=[vp_in.ap().opt()], outs=[vp_out.ap().opt()]), inc="cc")

        ev_slot_dma = {}
        c_act_evac = {}
        ev_erm_prev = None
        ev_qT_dma_last = None
        qpd = {0: kvpd[4]}
        ecqk_map = {}

        def emit_q_c(qc, ev_crow):
            evt = None
            for i in range(4):
                evt = P.op("tensor", lambda e, i=i: e.transpose(
                    pF.ap()[:, i:i + 1], rowC.ap()[0:1, i * 128:(i + 1) * 128],
                    sb_idf.ap()[0:1, 0:1]),
                    waits=([ev_crow, st["pF_read"], ev_erm_prev] if i == 0 else []),
                    inc=("pe" if i == 3 else None))
            ecq = P.op("vector", lambda e, qc=qc: e.tensor_copy(
                cqcols.ap()[:, 4 * qc:4 * qc + 4], pF.ap()[:, 0:4]), waits=[evt], inc="dve")
            st["pF_read"] = ecq
            st["rowC_free"] = ecq
            return P.op("vector", lambda e, qc=qc: e.tensor_scalar(
                out=cqk.ap()[:, 4 * qc:4 * qc + 4], in0=cqcols.ap()[:, 4 * qc:4 * qc + 4],
                scalar1=ckcols.ap()[:, qc:qc + 1], scalar2=None, op0=AluOpType.mult),
                waits=[ecq, ev_ck.get(0), ev_ck.get(1)], inc="dve")

        st["pF_read"] = ev_pF_read
        ecqk_map[0] = emit_q_c(0, q_crow0)
        for qc in range(8):
            slot = qc % 2
            ev_sign = qpd[qc]["sign"]
            ecqk = ecqk_map[qc]
            if qc + 1 < 8:
                qpd[qc + 1] = wprep_new(wqT, (qc + 1) * 512, (qc + 1) % 2,
                                        ev_wslot_free.get((qc + 1) % 2))
            mm = None
            for ms in range(4):
                m = 4 * qc + ms
                for th in range(2):
                    idx = m * 2 + th
                    g8 = ms * 2 + th
                    pk = paks[idx % 4]
                    for dc in range(DC):
                        mm = P.op("tensor", lambda e, dc=dc, slot=slot, ms=ms, th=th, pk=pk:
                                  e.matmul(pk, wring[:, slot, dc, ms * 128:(ms + 1) * 128],
                                           xnT[:, dc, th * 512:(th + 1) * 512],
                                           start=(dc == 0), stop=(dc == DC - 1)),
                                  waits=([qpd[qc]["sign_g"][dc // 4]]
                                         if (dc % 4 == 0 and g8 == 0) else []) +
                                  ([ev_slot_dma.get(idx - 4),
                                    ev_kst_free.get(28 + idx) if idx < 4 else None]
                                   if dc == 0 else []),
                                  inc=("pe" if dc == DC - 1 else None))
                    if qc + 1 < 8:
                        wprep_sched(qpd[qc + 1], g8)
                    esq = P.op("scalar", lambda e, m=m, pk=pk, ecqk=ecqk: e.activation(
                        out=q2v, in_=pk, func=AF.Square,
                        scale=cqk.ap()[:, m:m + 1]),
                        waits=[mm, ecqk,
                               et3 if idx == 0 else None,
                               ev_erm_prev if idx > 0 else None], inc="act")
                    kr = idx % 8
                    eev = P.op("scalar", lambda e, kr=kr, m=m, pk=pk: e.activation(
                        out=qstv[:, kr, :], in_=pk, func=AF.Copy,
                        scale=cqk.ap()[:, m:m + 1]),
                        waits=[ev_slot_dma.get(idx - 8),
                               ev_vp_dma[-1] if idx == 0 else None,
                               kst_dma.get(4 * NT - 4 + idx)], inc="act")
                    ed = P.op("sync", lambda e, kr=kr, m=m, th=th: e.dma_start(
                        out=qT_dram.ap()[m, :, th * 512:(th + 1) * 512], in_=qstv[:, kr, :]),
                        waits=[eev], inc="ds", by=16)
                    ev_slot_dma[idx] = ed
                    ev_qT_dma_last = ed
                    c_act_evac[idx] = eev
                    eqm = P.op("tensor", lambda e: e.matmul(
                        pF.ap()[0:1, 0:512], sb_onesc.ap()[:, :], q2v,
                        start=True, stop=True),
                        waits=[esq, st["pF_read"],
                               ev_erm_prev if idx > 0 else None], inc="pe")
                    dst = qn2A if th == 0 else qn2B
                    ev_erm_prev = P.op("vector", lambda e, m=m, dst=dst: e.tensor_reduce(
                        out=dst.ap()[0:1, m:m + 1], in_=pF.ap()[0:1, 0:512],
                        axis=mybir.AxisListType.X, op=AluOpType.max),
                        waits=[eqm, ev_qn2i], inc="dve")
            ev_wslot_free[slot] = mm
            if qc + 1 < 8:
                crow = wprep_finish(qpd[qc + 1], qsc_row, (qc + 1) * 512)
                ecqk_map[qc + 1] = emit_q_c(qc + 1, crow)
        ev_pF_read = ev_erm_prev
        ev_C_qn2 = ev_erm_prev
        ev_pF_read = ev_erm_prev

        ev_fst_free = {}
        st["scx_free"] = {}
        fst_dma = {}
        st["bf_prev"] = None
        ev_ffn_last = None
        brow = [rowB, rowB2]
        bias_dma = {}

        def emit_bias_dma(fc):
            bias_dma[fc] = P.op("sync", lambda e, fc=fc: e.dma_start(
                out=brow[fc % 2].ap()[:, :],
                in_=bias_row.ap()[0:1, fc * 512:fc * 512 + 512]),
                waits=[st.get(("bm", fc - 2))], inc="ds", by=16)

        def emit_bias(fc):
            ev_bm = P.op("tensor", lambda e, fc=fc: e.matmul(
                pF.ap()[:, 0:512], sb_onesr.ap()[:, :], brow[fc % 2].ap()[:, :],
                start=True, stop=True),
                waits=[bias_dma[fc], st.get("bf_prev"),
                       ev_pF_read if fc == 0 else None], inc="pe")
            st[("bm", fc)] = ev_bm
            st["bf_prev"] = P.op("vector", lambda e, fc=fc: e.tensor_copy(
                coF.ap()[:, (fc % 2) * 512:(fc % 2) * 512 + 512], pF.ap()[:, 0:512]),
                waits=[ev_bm], inc="dve")
            return st["bf_prev"]

        bias_map = {0: None}
        for fc in range(8):
            slot = fc % 2
            ev_stt = None
            for g in range(8):
                rr = g % 2
                G = st["graw"]
                st["graw"] += 1
                evc = P.op("gpsimd", lambda e, g=g, rr=rr, fc=fc: e.dma_start(
                    out=rawv[:, rr, :].rearrange("p (c n) -> p c n", c=4),
                    in_=codesT.ap().rearrange("(c p) n -> p c n", p=128)[
                        :, 4 * g:4 * g + 4, fc * 512:fc * 512 + 512]),
                    waits=[st["raw_free"].get(G - 2)], inc="dg", by=16)
                evs = P.op("sync", lambda e, g=g, rr=rr, fc=fc: e.dma_start(
                    out=scxv[:, rr, :].rearrange("p (c n) -> p c n", c=4),
                    in_=scexpT.ap().rearrange("(c p) n -> p c n", p=128)[
                        :, 4 * g:4 * g + 4, fc * 512:fc * 512 + 512]),
                    waits=[st["scx_free"].get((fc, g))], inc="ds", by=16)
                ev_stt = P.op("vector", lambda e, g=g, rr=rr, slot=slot: e.scalar_tensor_tensor(
                    out=wring[:, slot, 4 * g:4 * g + 4, :],
                    in0=rawv[:, rr, :].rearrange("p (c n) -> p c n", c=4), scalar=128.0,
                    in1=scxv[:, rr, :].rearrange("p (c n) -> p c n", c=4),
                    op0=AluOpType.subtract, op1=AluOpType.mult),
                    waits=[evc, evs, ev_wslot_free.get(slot) if g == 0 else None],
                    inc="dve")
                st["raw_free"][G] = ev_stt
                st["scx_free"][(fc, g + 2)] = st["scx_free"][(fc + 1, g - 6)] = ev_stt
            if fc == 0:
                emit_bias_dma(0)
                emit_bias_dma(1)
            elif fc + 1 < 8:
                emit_bias_dma(fc + 1)
            ev_bf = emit_bias(fc)
            mm = None
            for tt in range(NT):
                pk = paks[tt % 4]
                for dc in range(DC):
                    mm = P.op("tensor", lambda e, dc=dc, tt=tt, slot=slot, pk=pk: e.matmul(
                        pk, xnT[:, dc, tt * 128:(tt + 1) * 128], wring[:, slot, dc, :],
                        start=(dc == 0), stop=(dc == DC - 1)),
                        waits=([ev_stt, ev_fst_free.get((fc, tt - 4)),
                                c_act_evac.get(60 + tt) if (fc == 0 and tt < 4) else None]
                               if dc == 0 else []),
                        inc=("pe" if dc == DC - 1 else None))
                kr = tt % 4
                eev = P.op("vector", lambda e, kr=kr, slot=slot, pk=pk: e.scalar_tensor_tensor(
                    out=sctF[:, kr, :], in0=pk, scalar=0.0,
                    in1=coF.ap()[:, slot * 512:slot * 512 + 512],
                    op0=AluOpType.add, op1=AluOpType.add),
                    waits=[mm, ev_bf, fst_dma.get((fc, tt - 4)),
                           fst_dma.get((fc - 1, tt + 4))], inc="dve")
                ed = P.op("sync", lambda e, kr=kr, fc=fc, tt=tt: e.dma_start(
                    out=ffn_dram.ap()[tt * 128:(tt + 1) * 128, fc * 512:fc * 512 + 512],
                    in_=sctF[:, kr, :]), waits=[eev], inc="ds", by=16)
                ev_fst_free[(fc, tt)] = ed
                fst_dma[(fc, tt)] = ed
                ev_ffn_last = ed
            ev_wslot_free[slot] = mm
        ev_E1_done = ev_ffn_last
        ev_pF_read = st["bf_prev"]

        eg1 = P.op("sync", lambda e: e.dma_start(
            out=kngb.ap()[:, :],
            in_=kp_out.ap().rearrange("(r t) n -> r t n", r=4)[:, T, 0:8]),
            waits=[ev_ccK], inc="ds", by=16)
        eg2 = P.op("tensor", lambda e: e.transpose(
            trb[3][0:8, 0:4], kngb.ap()[:, :], sb_id.ap()[0:4, 0:4]),
            waits=[eg1, ev_pF_read], inc="pe")
        eg3 = P.op("vector", lambda e: e.tensor_copy(
            knloc.ap()[0:8, 0:4], trb[3][0:8, 0:4]), waits=[eg2], inc="dve")
        eg4 = P.op("vector", lambda e: e.tensor_reduce(
            out=knlocm.ap()[:, :], in_=knloc.ap()[0:8, 0:4], axis=mybir.AxisListType.X,
            op=AluOpType.max), waits=[eg3], inc="dve")
        eg5 = P.op("tensor", lambda e: e.transpose(
            pF.ap()[0:1, 0:8], knlocm.ap()[:, :], sb_idf.ap()[0:8, 0:8]),
            waits=[eg4, eg3], inc="pe")
        eg6 = P.op("vector", lambda e: e.tensor_copy(knrow8.ap()[:, :], pF.ap()[0:1, 0:8]),
                   waits=[eg5], inc="dve")
        en1 = P.op("vector", lambda e: e.tensor_tensor(
            out=qn2T.ap()[:, :], in0=qn2A.ap()[:, :], in1=qn2B.ap()[:, :],
            op=AluOpType.max), waits=[ev_C_qn2], inc="dve")
        en2 = None
        for r in range(4):
            en2 = P.op("vector", lambda e, r=r: e.tensor_tensor(
                out=qn2T.ap().rearrange("o (j r) -> o j r", r=4)[:, :, r],
                in0=qn2T.ap().rearrange("o (j r) -> o j r", r=4)[:, :, r],
                in1=knrow8.ap()[:, :], op=AluOpType.mult),
                waits=([en1, eg6] if r == 0 else []), inc="dve")
        en3 = P.op("vector", lambda e: e.tensor_scalar(
            out=qn2T.ap()[:, :], in0=qn2T.ap()[:, :], scalar1=1e-30, scalar2=None,
            op0=AluOpType.max), waits=[en2], inc="dve")
        en4 = P.op("scalar", lambda e: e.activation(
            out=qn2T.ap()[:, :], in_=qn2T.ap()[:, :], func=AF.Sqrt),
            waits=[en3], inc="act")
        en5 = P.op("vector", lambda e: e.tensor_scalar(
            out=qn2T.ap()[:, :], in0=qn2T.ap()[:, :], scalar1=-RSQ, scalar2=None,
            op0=AluOpType.mult), waits=[en4], inc="dve")
        en6 = P.op("tensor", lambda e: e.matmul(
            pF.ap()[:, 0:32], sb_onesr.ap()[:, :], qn2T.ap()[:, :], start=True, stop=True),
            waits=[en5, eg6], inc="pe")
        ev_negC = P.op("vector", lambda e: e.tensor_copy(negC.ap()[:, :], pF.ap()[:, 0:32]),
                       waits=[en6], inc="dve")
        ev_pF_read = ev_negC
        if debug:
            P.op("sync", lambda e: e.dma_start(out=dbg["ffn"].ap()[:, :],
                 in_=ffn_dram.ap()[:, :]), waits=[ev_E1_done], inc="ds", by=16)
            P.op("sync", lambda e: e.dma_start(
                out=dbg["qT"].ap().rearrange("(h p) t -> h p t", h=H),
                in_=qT_dram.ap()[:, :, :]), waits=[ev_qT_dma_last], inc="ds", by=16)
            P.op("sync", lambda e: e.dma_start(out=dbg["kp"].ap()[:, :], in_=kp_out.ap()[:, :]),
                 waits=[ev_ccK], inc="ds", by=16)
            P.op("sync", lambda e: e.dma_start(out=dbg["vp"].ap()[:, :], in_=vp_out.ap()[:, :]),
                 waits=[ev_ccV], inc="ds", by=16)
            P.op("sync", lambda e: e.dma_start(out=dbg["negc"].ap()[:, :], in_=negC.ap()[:, :]),
                 waits=[ev_negC], inc="ds", by=16)
            P.op("sync", lambda e: e.dma_start(out=dbg["kn2"].ap()[:, :], in_=knrow8.ap()[:, :]),
                 waits=[eg6], inc="ds", by=16)
            P.op("sync", lambda e: e.dma_start(out=dbg["qn2"].ap()[:, :], in_=qn2T.ap()[:, :]),
                 waits=[en5], inc="ds", by=16)
            P.op("sync", lambda e: e.dma_start(out=dbg["cvrow"].ap()[:, :], in_=cvrow.ap()[:, :]),
                 waits=[ev_cv.get(2), ev_cv.get(3)], inc="ds", by=16)
            P.op("sync", lambda e: e.dma_start(out=dbg["cqk"].ap()[:, :], in_=cqk.ap()[:, :]),
                 waits=[ev_C_qn2], inc="ds", by=16)

        ev_kj = {}
        ev_vj = {}
        ev_qh = {}
        ev_kj_free = {}
        ev_vj_free = {}
        ev_qh_free = {}
        ev_pt_free = {}
        st["pts_free"] = {}
        st["ptsum"] = {}
        ev_exp = {}
        ev_expA = {}
        ev_tt = {0: None, 1: None}
        ev_ez = {0: None, 1: None}
        ev_eo2 = {0: None, 1: None}
        gsc = 0
        for j in range(KVH):
            rj = j % 2
            for r4 in range(4):
                ev_kj[j] = P.op("sync", lambda e, j=j, rj=rj, r4=r4: e.dma_start_transpose(
                    out=kjT[:, rj, r4 * 1024:(r4 + 1) * 1024],
                    in_=kp_out.ap().rearrange("(r t) n -> r t n", r=4)[
                        r4, 0:1024, j * 128:(j + 1) * 128]),
                    waits=[ev_ccK if (j == 0 and r4 == 0) else None,
                           ev_E1_done if (j < 2 and r4 == 0) else None,
                           ev_kj_free.get(j - 2) if r4 == 0 else None],
                    inc="ds", by=16)
            ev_vj[j] = P.op("sync", lambda e, j=j, rj=rj: e.dma_start(
                out=vjv[:, rj, :, :],
                in_=vp_out.ap().rearrange("(s p) o -> p s o", p=128)[
                    :, :, j * 128:(j + 1) * 128]),
                waits=[ev_ccV if j == 0 else None,
                       ev_E1_done if j < 2 else None,
                       ev_vj_free.get(j - 2)], inc="ds", by=16)
            for h4 in range(4):
                h = 4 * j + h4
                rh = h % 2
                ev_qh[h] = P.op("sync", lambda e, h=h, rh=rh: e.dma_start(
                    out=qthv[:, rh, :], in_=qT_dram.ap()[h, :, :]),
                    waits=[ev_qh_free.get(h - 2),
                           ev_E1_done if h < 2 else None,
                           ev_qT_dma_last if h == 0 else None], inc="ds", by=16)
                def emit_scores(sc):
                    par = sc % 2
                    pX = pA if par == 0 else pB
                    emm0 = P.op("tensor", lambda e, sc=sc, pX=pX, rj=rj, rh=rh: e.matmul(
                        pX.ap()[:, 0:512], kjT[:, rj, sc * 128:(sc + 1) * 128],
                        qthv[:, rh, 0:512], start=True, stop=True),
                        waits=([ev_expA.get(gsc + sc - 2),
                                ev_qh[h] if sc == 0 else None,
                                ev_kj[j] if (h4 == 0 and sc == 0) else None,
                                ev_negC if (j == 0 and h4 == 0 and sc == 0) else None,
                                ev_eo2[par] if sc < 2 else None]), inc="pe")
                    ev_expA[gsc + sc] = P.op("scalar", lambda e, sc=sc, pX=pX, h=h:
                                             e.activation(
                        out=ptv[:, sc % 4, 0:512], in_=pX.ap()[:, 0:512], func=AF.Exp,
                        scale=RSQ, bias=negC.ap()[:, h:h + 1]),
                        waits=[emm0] + list(ev_pt_free.get(gsc + sc - 4) or ()), inc="act")
                    emm1 = P.op("tensor", lambda e, sc=sc, pX=pX, rj=rj, rh=rh: e.matmul(
                        pX.ap()[:, 512:1024], kjT[:, rj, sc * 128:(sc + 1) * 128],
                        qthv[:, rh, 512:1024], start=True, stop=True),
                        waits=[ev_exp.get(gsc + sc - 2)], inc="pe")
                    ev_exp[gsc + sc] = P.op("scalar", lambda e, sc=sc, pX=pX, h=h:
                                            e.activation(
                        out=ptv[:, sc % 4, 512:1024], in_=pX.ap()[:, 512:1024],
                        func=AF.Exp, scale=RSQ, bias=negC.ap()[:, h:h + 1]),
                        waits=[emm1], inc="act")

                def emit_av(sc):
                    P.op("tensor", lambda e, sc=sc, rj=rj: e.matmul(
                        pC.ap()[:, :], vjv[:, rj, sc, :], ptv[:, sc % 4, 0:512],
                        start=(sc == 0), stop=(sc == DC - 1)),
                        waits=[ev_expA[gsc + sc],
                               ev_tt[0] if sc == 0 else None,
                               ev_vj[j] if (h4 == 0 and sc == 0) else None])
                    ev_av = P.op("tensor", lambda e, sc=sc, rj=rj: e.matmul(
                        pD.ap()[:, :], vjv[:, rj, sc, :], ptv[:, sc % 4, 512:1024],
                        start=(sc == 0), stop=(sc == DC - 1)),
                        waits=[ev_tt[1] if sc == 0 else None], inc="pe")
                    return ev_av

                def emit_ptsum(p):
                    sc = 2 * p + 1
                    pr = p % 2
                    ev_pts = P.op("vector", lambda e, sc=sc, pr=pr: e.tensor_tensor(
                        out=ptsv[:, pr, :], in0=ptv[:, (sc - 1) % 4, :],
                        in1=ptv[:, sc % 4, :], op=AluOpType.add),
                        waits=[ev_exp[gsc + sc], st["pts_free"].get(gsc // 2 + p - 2)],
                        inc="dve")
                    st["ptsum"][gsc // 2 + p] = ev_pts
                    ev_pt_free[gsc + sc - 1] = ev_pt_free[gsc + sc] = (avs[sc], ev_pts)

                def emit_zpair(p):
                    pr = p % 2
                    P.op("tensor", lambda e, p=p, pr=pr: e.matmul(
                        pE.ap()[0:1, 0:512], sb_onesc.ap()[:, :], ptsv[:, pr, 0:512],
                        start=(p == 0), stop=(p == 15)),
                        waits=[st["ptsum"][gsc // 2 + p],
                               ev_ez[0] if p == 0 else None])
                    ev_z1 = P.op("tensor", lambda e, p=p, pr=pr: e.matmul(
                        pF.ap()[0:1, 0:512], sb_onesc.ap()[:, :], ptsv[:, pr, 512:1024],
                        start=(p == 0), stop=(p == 15)),
                        waits=[ev_ez[1] if p == 0 else None], inc="pe")
                    st["pts_free"][gsc // 2 + p] = ev_z1
                    return ev_z1

                avs = {}
                for sc in range(DC):
                    emit_scores(sc)
                    if sc >= 1:
                        avs[sc - 1] = emit_av(sc - 1)
                        if (sc - 1) % 2 == 1:
                            p = (sc - 1) // 2
                            emit_ptsum(p)
                            if p >= 1:
                                emit_zpair(p - 1)
                avs[DC - 1] = emit_av(DC - 1)
                emit_zpair(DC // 2 - 2)
                emit_ptsum(DC // 2 - 1)
                ev_avz = emit_zpair(DC // 2 - 1)
                ev_avz = st["pts_free"][(gsc + DC - 1) // 2]
                gsc += DC
                for thh, (pZ, zr, rzr, pO, pOut, sl) in enumerate(
                        ((pE, zrow, rzrow, pC, pA, 0), (pF, zrow2, rzrow2, pD, pB, 1))):
                    ez1 = P.op("vector", lambda e, pZ=pZ, zr=zr: e.tensor_copy(
                        zr.ap()[:, :], pZ.ap()[0:1, 0:512]), waits=[ev_avz], inc="dve")
                    ev_ez[thh] = ez1
                    el1 = P.op("scalar", lambda e, zr=zr, rzr=rzr: e.activation(
                        out=rzr.ap()[:, :], in_=zr.ap()[:, :], func=AF.Ln),
                        waits=[ez1], inc="act")
                    el2 = P.op("scalar", lambda e, zr=zr, rzr=rzr: e.activation(
                        out=zr.ap()[:, :], in_=rzr.ap()[:, :], func=AF.Exp, scale=-1.0),
                        waits=[el1], inc="act")
                    eo1 = P.op("tensor", lambda e, j=j, zr=zr, pOut=pOut: e.matmul(
                        pOut.ap()[:, 0:512], cvrow.ap()[0:1, j * 128:(j + 1) * 128],
                        zr.ap()[:, :], start=True, stop=True),
                        waits=[el2, ev_expA[gsc - 2 + thh],
                               ev_cv.get(2) if (j == 0 and h4 == 0) else None,
                               ev_cv.get(3) if (j == 0 and h4 == 0) else None], inc="pe")
                    eo2 = P.op("vector", lambda e, pOut=pOut, sl=sl: e.tensor_copy(
                        sctF[:, sl, :], pOut.ap()[:, 0:512]), waits=[eo1], inc="dve")
                    ev_eo2[thh] = eo2
                    ev_tt[thh] = P.op("vector", lambda e, h=h, thh=thh, pO=pO, sl=sl:
                                      e.tensor_tensor(
                        out=attnT[:, h, thh * 512:(thh + 1) * 512], in0=pO.ap()[:, :],
                        in1=sctF[:, sl, :], op=AluOpType.mult), waits=[eo2], inc="dve")
                ev_qh_free[h] = ev_tt[1]
            ev_kj_free[j] = ev_tt[1]
            ev_vj_free[j] = ev_tt[1]
        ev_D_done = ev_tt[1]
        if debug:
            P.op("sync", lambda e: e.dma_start(out=dbg["attnT"].ap()[:, :], in_=arenaA.ap()[:, :]),
                 waits=[ev_D_done], inc="ds", by=16)

        seed_ring_barrier(ev_D_done)
        ev_x2 = {}
        ev_f2 = {}
        ev_x2_free = {}
        ev_out_dma = {}
        ev_cof_prev = None
        opd = {0: wprep_new(woT, 0, 0, ev_D_done)}
        wprep_all(opd[0])
        o_crow = {0: wprep_finish(opd[0], osc_row, 0)}

        def emit_cof(oc, ev_crow):
            nonlocal ev_cof_prev
            ecm = P.op("tensor", lambda e: e.matmul(
                pF.ap()[:, 0:512], sb_onesr.ap()[:, :], rowC.ap()[:, :],
                start=True, stop=True),
                waits=[ev_crow, ev_cof_prev, ev_pF_read if oc == 0 else None], inc="pe")
            ecf = P.op("vector", lambda e, oc=oc: e.tensor_copy(
                coF.ap()[:, (oc % 2) * 512:(oc % 2) * 512 + 512], pF.ap()[:, 0:512]),
                waits=[ecm], inc="dve")
            ev_cof_prev = ecf
            st["rowC_free"] = ecm
            return ecf

        cof_map = {0: emit_cof(0, o_crow[0])}
        for oc in range(8):
            slot = oc % 2
            ev_sign = opd[oc]["sign"]
            ecf = cof_map[oc]
            if oc + 1 < 8:
                opd[oc + 1] = wprep_new(woT, (oc + 1) * 512, (oc + 1) % 2,
                                        ev_wslot_free.get((oc + 1) % 2))
            mm = None
            for tt in range(NT):
                idx = oc * NT + tt
                rx = idx % 2
                ev_x2[idx] = P.op("sync", lambda e, rx=rx, oc=oc, tt=tt: e.dma_start(
                    out=xeE2[:, rx, :],
                    in_=x_loc.ap()[tt * 128:(tt + 1) * 128, oc * 512:oc * 512 + 512]),
                    waits=[ev_out_dma.get(idx - 2),
                           ev_E1_done if idx < 2 else None], inc="ds", by=16)
                ev_f2[idx] = P.op("sync", lambda e, rx=rx, oc=oc, tt=tt: e.dma_start(
                    out=xeE2[:, 2 + rx, :],
                    in_=ffn_dram.ap()[tt * 128:(tt + 1) * 128, oc * 512:oc * 512 + 512]),
                    waits=[ev_out_dma.get(idx - 2),
                           ev_E1_done if idx == 0 else None], inc="ds", by=16)
                pk = paks[tt % 4]
                for ac in range(H):
                    mm = P.op("tensor", lambda e, ac=ac, tt=tt, slot=slot, pk=pk: e.matmul(
                        pk, attnT[:, ac, tt * 128:(tt + 1) * 128], wring[:, slot, ac, :],
                        start=(ac == 0), stop=(ac == H - 1)),
                        waits=([opd[oc]["sign_g"][ac // 4]] if (ac % 4 == 0 and tt == 0)
                               else []) +
                              ([ev_D_done if idx == 0 else None,
                                ev_out_dma.get(idx - 4)] if ac == 0 else []),
                        inc=("pe" if ac == H - 1 else None))
                if oc + 1 < 8:
                    wprep_sched(opd[oc + 1], tt)
                kr = tt % 2
                ecb1 = P.op("vector", lambda e, kr=kr, oc=oc, pk=pk: e.tensor_tensor(
                    out=sctF[:, 2 * kr, :], in0=pk,
                    in1=coF.ap()[:, (oc % 2) * 512:(oc % 2) * 512 + 512],
                    op=AluOpType.mult), waits=[mm, ecf], inc="dve")
                ecb2 = P.op("vector", lambda e, kr=kr, rx=rx: e.tensor_tensor(
                    out=sctF[:, 2 * kr + 1, :], in0=sctF[:, 2 * kr, :], in1=xeE2[:, rx, :],
                    op=AluOpType.add), waits=[ecb1, ev_x2[idx]], inc="dve")
                ev_x2_free[idx] = ecb2
                ecb3 = P.op("vector", lambda e, kr=kr, rx=rx: e.tensor_tensor(
                    out=xeE2[:, 4 + rx, :], in0=sctF[:, 2 * kr + 1, :],
                    in1=xeE2[:, 2 + rx, :], op=AluOpType.add),
                    waits=[ecb2, ev_f2[idx], ev_out_dma.get(idx - 2)], inc="dve")
                last_out = P.op("sync", lambda e, rx=rx, oc=oc, tt=tt: e.dma_start(
                    out=out_d.ap()[tt * 128:(tt + 1) * 128, oc * 512:oc * 512 + 512],
                    in_=xeE2[:, 4 + rx, :]), waits=[ecb3], inc="ds", by=16)
                ev_out_dma[idx] = last_out
            ev_wslot_free[slot] = mm
            if oc + 1 < 8:
                crow = wprep_finish(opd[oc + 1], osc_row, (oc + 1) * 512)
                cof_map[oc + 1] = emit_cof(oc + 1, crow)

        P.wait("sync", ("ds", P.cnt["ds"]))
        P.wait("gpsimd", ("dg", P.cnt["dg"]))

        @block.tensor
        def _(t):
            for f in P.prog["tensor"]:
                f(t, sems)

        @block.vector
        def _(v):
            for f in P.prog["vector"]:
                f(v, sems)

        @block.scalar
        def _(s):
            for f in P.prog["scalar"]:
                f(s, sems)

        @block.sync
        def _(s):
            for f in P.prog["sync"]:
                f(s, sems)

        @block.gpsimd
        def _(g):
            for f in P.prog["gpsimd"]:
                f(g, sems)

    return nc, P


def host_inputs(inputs):
    x = np.ascontiguousarray(np.asarray(inputs["L0_attn_hidden_states"], np.float32))
    wq = np.asarray(inputs["L0_attn_q_weight"], np.float32)
    wk = np.asarray(inputs["L0_attn_k_weight"], np.float32)
    wv = np.asarray(inputs["L0_attn_v_weight"], np.float32)
    wo = np.asarray(inputs["L0_attn_o_weight"], np.float32)
    qs = np.asarray(inputs["L0_attn_q_scale"], np.float32)
    ks = np.asarray(inputs["L0_attn_k_scale"], np.float32)
    vs = np.asarray(inputs["L0_attn_v_scale"], np.float32)
    osc = np.asarray(inputs["L0_attn_o_scale"], np.float32)
    q_scales = np.asarray(inputs["L0_ffn_q_scales"], np.float32)
    q_vals = np.asarray(inputs["L0_ffn_q_vals"], np.int32)
    fb = np.asarray(inputs["L0_ffn_bias"], np.float32)

    com = dict(
        wqT=np.ascontiguousarray(wq.T),
        wkvT=np.ascontiguousarray(np.concatenate([wk.T, wv.T], axis=1)),
        woT=np.ascontiguousarray(wo.T),
        codesT=np.ascontiguousarray(q_vals.reshape(4096, 4096).T),
        scexpT=np.ascontiguousarray(np.repeat(q_scales.T, 32, axis=0)),
        qsc_row=np.ascontiguousarray(qs[None, :]),
        kvsc_row=np.ascontiguousarray(np.concatenate([ks, vs])[None, :]),
        osc_row=np.ascontiguousarray(osc[None, :]),
        bias_row=np.ascontiguousarray(fb[None, :]),
        identity=np.eye(128, dtype=NB),
        identity_f=np.eye(128, dtype=np.float32),
        onesc_d=np.ones((128, 1), NB),
        onesr_d=np.ones((1, 128), np.float32),
    )
    in_maps = []
    for c in range(8):
        b, q = c // 4, c % 4
        m = dict(com)
        m["x_loc"] = np.ascontiguousarray(x[b, q * T:(q + 1) * T, :])
        in_maps.append(m)
    return in_maps


def kernel(**inputs):
    in_maps = host_inputs(inputs)
    nc, _ = build_nc(debug=False)
    res = run_bass_kernel_spmd(nc, in_maps, core_ids=list(range(8)))
    out = np.empty((2, 4096, 4096), np.float32)
    for c in range(8):
        b, q = c // 4, c % 4
        out[b, q * T:(q + 1) * T, :] = res.results[c]["out"]
    return out


if __name__ == "__main__":
    nc, P = build_nc(debug=False)
    print("instructions (incl waits):", P.ni, "total:", sum(P.ni.values()))
    print("sem totals:", P.cnt)


# version 15
# speedup vs baseline: 1.1286x; 1.0245x over previous
import contextlib
import numpy as np
import ml_dtypes

import concourse.bass as bass
import concourse.mybir as mybir
from concourse.bass_utils import run_bass_kernel_spmd
from concourse.alu_op_type import AluOpType

F32 = mybir.dt.float32
BF16 = mybir.dt.bfloat16
I32 = mybir.dt.int32
AF = mybir.ActivationFunctionType
NB = np.dtype(ml_dtypes.bfloat16)

D = 4096
T = 1024
S = 4096
NT = 8
DC = 32
H = 32
KVH = 8
RSQ = float(1.0 / np.sqrt(np.float32(128.0)))

ENGS = ("tensor", "vector", "scalar", "sync", "gpsimd")


class Prog:

    def __init__(self):
        self.prog = {e: [] for e in ENGS}
        self.cnt = {}
        self.ni = {e: 0 for e in ENGS}

    def ev(self, sem, by):
        self.cnt[sem] = self.cnt.get(sem, 0) + by
        return (sem, self.cnt[sem])

    def op(self, eng, make, waits=(), inc=None, by=1):
        waits = tuple(w for w in waits if w is not None)
        e = self.ev(inc, by) if inc else None
        self.ni[eng] += 1 + len(waits)

        def f(engobj, sems, _w=waits, _m=make, _e=e, _by=by):
            for (s, v) in _w:
                engobj.wait_ge(sems[s], v)
            ins = _m(engobj)
            if _e is not None:
                ins.then_inc(sems[_e[0]], _by)
        self.prog[eng].append(f)
        return e

    def wait(self, eng, *evs):
        evs = tuple(e for e in evs if e is not None)
        if not evs:
            return
        self.ni[eng] += len(evs)

        def f(engobj, sems, _w=evs):
            for (s, v) in _w:
                engobj.wait_ge(sems[s], v)
        self.prog[eng].append(f)


def build_nc(debug=False):
    nc = bass.Bass("TRN2", target_bir_lowering=False)

    x_loc = nc.dram_tensor("x_loc", [T, D], F32, kind="ExternalInput")
    wqT = nc.dram_tensor("wqT", [D, D], F32, kind="ExternalInput")
    wkvT = nc.dram_tensor("wkvT", [D, 2048], F32, kind="ExternalInput")
    woT = nc.dram_tensor("woT", [D, D], F32, kind="ExternalInput")
    codesT = nc.dram_tensor("codesT", [D, D], I32, kind="ExternalInput")
    scexpT = nc.dram_tensor("scexpT", [D, D], F32, kind="ExternalInput")
    qsc_row = nc.dram_tensor("qsc_row", [1, D], F32, kind="ExternalInput")
    kvsc_row = nc.dram_tensor("kvsc_row", [1, 2048], F32, kind="ExternalInput")
    osc_row = nc.dram_tensor("osc_row", [1, D], F32, kind="ExternalInput")
    bias_row = nc.dram_tensor("bias_row", [1, D], F32, kind="ExternalInput")
    identity = nc.dram_tensor("identity", [128, 128], BF16, kind="ExternalInput")
    identity_f = nc.dram_tensor("identity_f", [128, 128], F32, kind="ExternalInput")
    onesc_d = nc.dram_tensor("onesc_d", [128, 1], BF16, kind="ExternalInput")
    onesr_d = nc.dram_tensor("onesr_d", [1, 128], F32, kind="ExternalInput")

    out_d = nc.dram_tensor("out", [T, D], F32, kind="ExternalOutput")

    qT_dram = nc.dram_tensor("qT_dram", [H, 128, T], BF16)
    KP_COLS = 1040
    kp_in = nc.dram_tensor("kp_in", [T + 1, KP_COLS], BF16)
    kp_out = nc.dram_tensor("kp_out", [4 * (T + 1), KP_COLS], BF16)
    vp_in = nc.dram_tensor("vp_in", [T, 1024], BF16)
    vp_out = nc.dram_tensor("vp_out", [S, 1024], BF16)
    ffn_dram = nc.dram_tensor("ffn_dram", [T, D], F32)

    dbg = {}
    if debug:
        for nm, sh, dt in (("xnT", [128, DC * T], BF16), ("attnT", [128, H * T], BF16),
                           ("negc", [128, 32], F32), ("kn2", [1, 8], F32),
                           ("qn2", [1, 32], F32), ("cvrow", [1, 1024], F32),
                           ("cqk", [128, 32], F32), ("qT", [H * 128, T], BF16),
                           ("ffn", [T, D], F32),
                           ("kp", [4 * (T + 1), KP_COLS], BF16),
                           ("vp", [S, 1024], BF16)):
            dbg[nm] = nc.dram_tensor("dbg_" + nm, sh, dt, kind="ExternalOutput")

    P = Prog()
    es = contextlib.ExitStack()
    with es:
        t_ = lambda *a: es.enter_context(nc.sbuf_tensor(*a))
        arenaA = t_("arenaA", [128, DC * 1024], BF16)
        wbuf = t_("wbuf", [128, 32768], BF16)
        scrF = t_("scrF", [128, 4096], F32)
        scrB = t_("scrB", [128, 4096], BF16)
        stageB = t_("stageB", [128, 4096], BF16)
        stageF = t_("stageF", [128, 2048], F32)
        kn2a = t_("kn2a", [128, 8], F32)
        kn2b = t_("kn2b", [128, 8], F32)
        kn2c = t_("kn2c", [128, 8], F32)
        coF = t_("coF", [128, 1024], F32)
        sb_id = t_("sb_id", [128, 128], BF16)
        sb_idf = t_("sb_idf", [128, 128], F32)
        sb_onesc = t_("sb_onesc", [128, 1], BF16)
        sb_onesr = t_("sb_onesr", [1, 128], F32)
        epsv = t_("epsv", [128, 1], F32)
        lstat = [t_("lstatA", [128, 8], F32), t_("lstatB", [128, 8], F32)]
        lsq = [t_("lsqA", [128, 4], F32), t_("lsqB", [128, 4], F32)]
        rowS = t_("rowS", [1, 512], F32)
        rowC = t_("rowC", [1, 512], F32)
        rowSc = t_("rowSc", [1, 512], F32)
        rowB = t_("rowB", [1, 512], F32)
        rowB2 = t_("rowB2", [1, 512], F32)
        cvrow = t_("cvrow", [1, 1024], F32)
        zrow = t_("zrow", [1, 512], F32)
        rzrow = t_("rzrow", [1, 512], F32)
        zrow2 = t_("zrow2", [1, 512], F32)
        rzrow2 = t_("rzrow2", [1, 512], F32)
        qn2A = t_("qn2A", [1, 32], F32)
        qn2B = t_("qn2B", [1, 32], F32)
        qn2T = t_("qn2T", [1, 32], F32)
        kngb = t_("kngb", [4, 8], BF16)
        knrow8 = t_("knrow8", [1, 8], F32)
        knloc = t_("knloc", [8, 128], F32)
        knlocm = t_("knlocm", [8, 1], F32)
        knlocb = t_("knlocb", [1, 8], BF16)
        negC = t_("negC", [128, 32], F32)
        ckcols = t_("ckcols", [128, 8], F32)
        cqcols = t_("cqcols", [128, 32], F32)
        cqk = t_("cqk", [128, 32], F32)

        pA = es.enter_context(nc.psum_tensor("pA", [128, 1024], F32))
        pB = es.enter_context(nc.psum_tensor("pB", [128, 1024], F32))
        pC = es.enter_context(nc.psum_tensor("pC", [128, 512], F32))
        pD = es.enter_context(nc.psum_tensor("pD", [128, 512], F32))
        pE = es.enter_context(nc.psum_tensor("pE", [128, 512], F32))
        pF = es.enter_context(nc.psum_tensor("pF", [128, 512], F32))

        semnames = ["pe", "act", "dve", "ds", "dg", "cc"]
        sems = {n: es.enter_context(nc.semaphore(n)) for n in semnames}
        block = es.enter_context(nc.Block())

        xnT = arenaA.ap().rearrange("p (c t) -> p c t", c=DC)
        attnT = arenaA.ap().rearrange("p (c t) -> p c t", c=H)
        wring = wbuf.ap().rearrange("p (r c n) -> p r c n", r=2, c=DC)
        rawv = scrB.ap().rearrange("p (r n) -> p r n", r=2)
        junkv = wbuf.ap()[:, 0:4096]
        kjT = wbuf.ap()[:, 0:8192].rearrange("p (r s) -> p r s", r=2)
        vjv = wbuf.ap()[:, 8192:16384].rearrange("p (r s c) -> p r s c", r=2, s=32)
        qthv = wbuf.ap()[:, 16384:18432].rearrange("p (r n) -> p r n", r=2)
        ptv = wbuf.ap()[:, 18432:22528].rearrange("p (r n) -> p r n", r=4)
        qstv = stageB.ap().rearrange("p (r n) -> p r n", r=8)
        xbuf = scrF.ap()
        xeE2 = scrF.ap().rearrange("p (r n) -> p r n", r=8)
        scxv = scrF.ap().rearrange("p (r n) -> p r n", r=2)
        sctF = stageF.ap().rearrange("p (r n) -> p r n", r=4)
        q2v = stageF.ap().bitcast(BF16)[:, 0:512]
        xnb = stageB.ap()
        ptsv = stageB.ap()[:, 0:2048].rearrange("p (r n) -> p r n", r=2)
        paks = [pA.ap()[:, 0:512], pA.ap()[:, 512:1024],
                pB.ap()[:, 0:512], pB.ap()[:, 512:1024]]
        trb = [p.ap().bitcast(BF16) for p in (pC, pD, pE, pF)]

        for dst, src in ((sb_id, identity), (sb_idf, identity_f),
                         (sb_onesc, onesc_d), (sb_onesr, onesr_d)):
            P.op("sync", lambda e, d=dst, s=src: e.dma_start(out=d.ap()[:, :], in_=s.ap()[:, :]),
                 inc="ds", by=16)
        evA = ("ds", P.cnt["ds"])
        ev_eps = P.op("vector", lambda e: e.memset(epsv.ap()[:, :], 1e-5), inc="dve")
        P.op("vector", lambda e: e.memset(qn2A.ap()[:, :], -3.0e38), inc="dve")
        ev_qn2i = P.op("vector", lambda e: e.memset(qn2B.ap()[:, :], -3.0e38), inc="dve")

        ev_xn = [None] * NT
        ev_junk = [None] * NT
        ev_evac = {}
        ev_tp = {}
        for tt in range(NT):
            r = tt % 2
            ev_x = P.op(
                "sync",
                lambda e, tt=tt: e.dma_start(
                    out=xbuf[:, :], in_=x_loc.ap()[tt * 128:(tt + 1) * 128, :]),
                waits=[ev_xn[tt - 1] if tt >= 1 else None,
                       ev_junk[tt - 1] if tt >= 1 else None], inc="ds", by=16)
            ls, lq = lstat[r], lsq[r]
            e1 = P.op("vector", lambda e, ls=ls: e.tensor_reduce(
                out=ls.ap()[:, 0:1], in_=xbuf[:, 0:2048], axis=mybir.AxisListType.X,
                op=AluOpType.add), waits=[ev_x], inc="dve")
            e2 = P.op("vector", lambda e, ls=ls: e.tensor_reduce(
                out=ls.ap()[:, 1:2], in_=xbuf[:, 2048:4096], axis=mybir.AxisListType.X,
                op=AluOpType.add), inc="dve")
            ea1 = P.op("scalar", lambda e, lq=lq: e.activation(
                out=junkv[:, 0:2048], in_=xbuf[:, 0:2048], func=AF.Square,
                accum_out=lq.ap()[:, 0:1]), waits=[ev_x], inc="act")
            ev_junk[tt] = ea2 = P.op("scalar", lambda e, lq=lq: e.activation(
                out=junkv[:, 2048:4096], in_=xbuf[:, 2048:4096], func=AF.Square,
                accum_out=lq.ap()[:, 1:2]), inc="act")
            e3 = P.op("vector", lambda e, ls=ls: e.tensor_tensor(
                out=ls.ap()[:, 2:3], in0=ls.ap()[:, 0:1], in1=ls.ap()[:, 1:2],
                op=AluOpType.add), waits=[e1, e2], inc="dve")
            e4 = P.op("vector", lambda e, ls=ls: e.tensor_scalar(
                out=ls.ap()[:, 3:4], in0=ls.ap()[:, 2:3], scalar1=1.0 / D, scalar2=None,
                op0=AluOpType.mult), waits=[e3], inc="dve")
            e5 = P.op("vector", lambda e, ls=ls: e.tensor_tensor(
                out=ls.ap()[:, 4:5], in0=ls.ap()[:, 3:4], in1=ls.ap()[:, 3:4],
                op=AluOpType.mult), waits=[e4], inc="dve")
            e6 = P.op("vector", lambda e, lq=lq: e.tensor_tensor(
                out=lq.ap()[:, 2:3], in0=lq.ap()[:, 0:1], in1=lq.ap()[:, 1:2],
                op=AluOpType.add), waits=[ea1, ea2, e5], inc="dve")
            e7 = P.op("vector", lambda e, ls=ls, lq=lq: e.tensor_scalar(
                out=ls.ap()[:, 5:6], in0=lq.ap()[:, 2:3], scalar1=1.0 / D,
                scalar2=ls.ap()[:, 4:5], op0=AluOpType.mult, op1=AluOpType.subtract),
                waits=[e6], inc="dve")
            e8 = P.op("scalar", lambda e, ls=ls: e.activation(
                out=ls.ap()[:, 6:7], in_=ls.ap()[:, 5:6], func=AF.Sqrt,
                bias=epsv.ap()[:, :]), waits=[e7, ev_eps], inc="act")
            e9 = P.op("vector", lambda e, ls=ls: e.reciprocal(
                out=ls.ap()[:, 7:8], in_=ls.ap()[:, 6:7]), waits=[e8], inc="dve")
            P.op("vector", lambda e, ls=ls: e.tensor_scalar(
                out=xnb[:, 0:2048], in0=xbuf[:, 0:2048], scalar1=ls.ap()[:, 3:4],
                scalar2=ls.ap()[:, 7:8], op0=AluOpType.subtract, op1=AluOpType.mult),
                waits=[e9, ev_tp.get((tt - 1, 3))], inc="dve")
            ev_xn[tt] = P.op("vector", lambda e, ls=ls: e.tensor_scalar(
                out=xnb[:, 2048:4096], in0=xbuf[:, 2048:4096], scalar1=ls.ap()[:, 3:4],
                scalar2=ls.ap()[:, 7:8], op0=AluOpType.subtract, op1=AluOpType.mult),
                inc="dve")
            for g in range(4):
                for i in range(8):
                    dc = 8 * g + i
                    ee = P.op("tensor", lambda e, g=g, i=i, dc=dc: e.transpose(
                        trb[g][:, i * 128:(i + 1) * 128],
                        xnb[:, dc * 128:(dc + 1) * 128], sb_id.ap()[:, :]),
                        waits=([ev_xn[tt], ev_evac.get((tt - 1, g)),
                                evA if (tt == 0 and g == 0) else None] if i == 0 else []),
                        inc=("pe" if i == 7 else None))
                    if i == 7:
                        ev_tp[(tt, g)] = ee
                ev_evac[(tt, g)] = P.op("vector", lambda e, tt=tt, g=g: e.tensor_copy(
                    xnT[:, 8 * g:8 * g + 8, tt * 128:(tt + 1) * 128],
                    trb[g].rearrange("p (c t) -> p c t", c=8)),
                    waits=[ev_tp[(tt, g)]], inc="dve")
        ev_B1_last = ev_evac[(NT - 1, 3)]
        ev_B1_junk = ev_junk[NT - 1]

        if debug:
            P.op("sync", lambda e: e.dma_start(out=dbg["xnT"].ap()[:, :], in_=arenaA.ap()[:, :]),
                 waits=[ev_B1_last], inc="ds", by=16)

        st = {
            "raw_free": {}, "abs_free": {}, "graw": 0, "gabs": 0,
            "rowSc_free": None, "pE_free": None,
        }
        st["raw_free"][-1] = st["raw_free"][-2] = None
        st["abs_free"][-1] = st["abs_free"][-2] = None

        def seed_ring_barrier(ev):
            g = st["graw"]
            st["raw_free"][g - 1] = st["raw_free"][g - 2] = ev

        def wprep_new(src, col0, slot, wslot_free_ev):
            return {"src": src, "col0": col0, "slot": slot, "wslot": wslot_free_ev,
                    "dma": [None] * 8, "abs": [None] * 8, "smm": [None] * 8,
                    "sign": None}

        def wprep_p1(pd, g):
            rr = g % 2
            G = st["graw"]
            st["graw"] += 1
            slot = pd["slot"]
            pd["dma"][g] = P.op("gpsimd", lambda e, g=g, rr=rr, pd=pd: e.dma_start(
                out=rawv[:, rr, :].rearrange("p (c n) -> p c n", c=4),
                in_=pd["src"].ap().rearrange("(c p) n -> p c n", p=128)[
                    :, 4 * g:4 * g + 4, pd["col0"]:pd["col0"] + 512]),
                waits=[st["raw_free"].get(G - 2)], inc="dg", by=16)
            pd["abs"][g] = P.op("vector", lambda e, g=g, rr=rr, slot=slot: e.scalar_tensor_tensor(
                out=wring[:, slot, 4 * g:4 * g + 4, :],
                in0=rawv[:, rr, :].rearrange("p (c n) -> p c n", c=4), scalar=-1.0,
                in1=rawv[:, rr, :].rearrange("p (c n) -> p c n", c=4),
                op0=AluOpType.mult, op1=AluOpType.max),
                waits=[pd["dma"][g], pd["wslot"] if g == 0 else None], inc="dve")
            pd["G" + str(g)] = G

        def wprep_p2(pd, g):
            rr = g % 2
            slot = pd["slot"]
            ev_s = None
            for i in range(4):
                ev_s = P.op("tensor", lambda e, slot=slot, g=g, i=i: e.matmul(
                    pE.ap()[0:1, 0:512], sb_onesc.ap()[:, :], wring[:, slot, 4 * g + i, :],
                    start=(g == 0 and i == 0), stop=(g == 7 and i == 3)),
                    waits=([pd["abs"][g]] if i == 0 else []) +
                          ([st["pE_free"]] if (g == 0 and i == 0) else []),
                    inc="pe")
            pd["smm"][g] = ev_s
            pd["sign"] = P.op("scalar", lambda e, g=g, rr=rr, slot=slot: e.activation(
                out=wring[:, slot, 4 * g:4 * g + 4, :],
                in_=rawv[:, rr, :].rearrange("p (c n) -> p c n", c=4), func=AF.Sign),
                waits=[pd["dma"][g], ev_s], inc="act")
            pd.setdefault("sign_g", [None] * 8)[g] = pd["sign"]
            st["raw_free"][pd["G" + str(g)]] = pd["sign"]


        def wprep_sched(pd, tt):
            if tt == 0:
                wprep_p1(pd, 0)
                wprep_p1(pd, 1)
                wprep_p2(pd, 0)
            elif tt < 7:
                wprep_p1(pd, tt + 1)
                wprep_p2(pd, tt)
            else:
                wprep_p2(pd, 7)

        def wprep_all(pd):
            for g in range(8):
                wprep_p1(pd, g)
                wprep_p2(pd, g)

        def wprep_finish(pd, scale_src, scale_off):
            ev_sc = P.op("sync", lambda e, pd=pd: e.dma_start(
                out=rowSc.ap()[:, :],
                in_=scale_src.ap()[0:1, scale_off:scale_off + 512]),
                waits=[st["rowSc_free"]], inc="ds", by=16)
            ec1 = P.op("vector", lambda e: e.tensor_copy(rowS.ap()[:, :], pE.ap()[0:1, 0:512]),
                       waits=[pd["smm"][7]], inc="dve")
            st["pE_free"] = ec1
            ec2 = P.op("vector", lambda e: e.tensor_scalar(
                out=rowS.ap()[:, :], in0=rowS.ap()[:, :], scalar1=1.0 / D, scalar2=1e-8,
                op0=AluOpType.mult, op1=AluOpType.max), waits=[ec1], inc="dve")
            ev_crow = P.op("vector", lambda e: e.tensor_tensor(
                out=rowC.ap()[:, :], in0=rowS.ap()[:, :], in1=rowSc.ap()[:, :],
                op=AluOpType.mult), waits=[ec2, ev_sc, st.get("rowC_free")], inc="dve")
            st["rowSc_free"] = ev_crow
            return ev_crow


        def phase_bar(*engs):
            now = [(s, P.cnt.get(s, 0)) for s in ("pe", "act", "dve")]
            now = [e for e in now if e[1] > 0]
            for eng in engs:
                P.wait(eng, *now)
        ev_wslot_free = {}
        ev_kst_free = {}
        kst_dma = {}
        ev_kp_dma = []
        ev_vp_dma = []
        ev_pF_read = ev_B1_last
        ev_kn_prev = None
        ev_ck = {}
        ev_cv = {}
        seed_ring_barrier(ev_B1_junk)

        kvpd = {}
        kvpd[0] = wprep_new(wkvT, 0, 0, ev_B1_junk)
        wprep_all(kvpd[0])
        crow = wprep_finish(kvpd[0], kvsc_row, 0)
        kv_crow = {0: crow}

        def emit_kv_c(kc, ev_crow):
            nonlocal_state = {}
            if kc < 2:
                evt = None
                for i in range(4):
                    evt = P.op("tensor", lambda e, i=i: e.transpose(
                        pF.ap()[:, i:i + 1], rowC.ap()[0:1, i * 128:(i + 1) * 128],
                        sb_idf.ap()[0:1, 0:1]),
                        waits=([ev_crow, st["pF_read"]] if i == 0 else []),
                        inc=("pe" if i == 3 else None))
                ev_ck[kc] = P.op("vector", lambda e, kc=kc: e.tensor_copy(
                    ckcols.ap()[:, 4 * kc:4 * kc + 4], pF.ap()[:, 0:4]),
                    waits=[evt], inc="dve")
                st["pF_read"] = ev_ck[kc]
                st["rowC_free"] = ev_ck[kc]
            else:
                ev_cv[kc] = P.op("vector", lambda e, kc=kc: e.tensor_copy(
                    cvrow.ap()[0:1, (kc - 2) * 512:(kc - 1) * 512], rowC.ap()[:, :]),
                    waits=[ev_crow], inc="dve")
                st["rowC_free"] = ev_cv[kc]

        st["pF_read"] = ev_pF_read
        emit_kv_c(0, crow)
        for kc in range(4):
            slot = kc % 2
            ev_sign = kvpd[kc]["sign"]
            if kc + 1 < 4:
                kvpd[kc + 1] = wprep_new(wkvT, (kc + 1) * 512, (kc + 1) % 2,
                                         ev_wslot_free.get((kc + 1) % 2))
            else:
                kvpd[4] = wprep_new(wqT, 0, 0, ev_wslot_free.get(0))
            mm = None
            for tt in range(NT):
                idx = kc * NT + tt
                pk = paks[idx % 4]
                for dc in range(DC):
                    mm = P.op("tensor", lambda e, dc=dc, tt=tt, slot=slot, pk=pk: e.matmul(
                        pk, xnT[:, dc, tt * 128:(tt + 1) * 128], wring[:, slot, dc, :],
                        start=(dc == 0), stop=(dc == DC - 1)),
                        waits=([kvpd[kc]["sign"], ev_B1_last,
                                ev_kst_free.get(idx - 4)] if dc == 0 else []),
                        inc=("pe" if dc == DC - 1 else None))
                nxt = kvpd[kc + 1]
                wprep_sched(nxt, tt)
                kr = idx % 4
                ev_ev = P.op("vector", lambda e, kr=kr, pk=pk: e.tensor_copy(
                    qstv[:, kr, :], pk), waits=[mm, kst_dma.get(idx - 4)], inc="dve")
                ev_kst_free[idx] = ev_ev
                if kc < 2:
                    ev_d = P.op("sync", lambda e, kr=kr, kc=kc, tt=tt: e.dma_start(
                        out=kp_in.ap()[tt * 128:(tt + 1) * 128, kc * 512:kc * 512 + 512],
                        in_=qstv[:, kr, :]), waits=[ev_ev], inc="ds", by=16)
                    ev_kp_dma.append(ev_d)
                    kst_dma[idx] = ev_d
                    eq = P.op("vector", lambda e, kr=kr: e.tensor_tensor(
                        out=q2v, in0=qstv[:, kr, :], in1=qstv[:, kr, :],
                        op=AluOpType.mult), waits=[ev_ev], inc="dve")
                    er = None
                    for i in range(4):
                        er = P.op("vector", lambda e, i=i, kc=kc: e.tensor_reduce(
                            out=kn2a.ap()[:, 4 * kc + i:4 * kc + i + 1],
                            in_=q2v[:, i * 128:(i + 1) * 128],
                            axis=mybir.AxisListType.X, op=AluOpType.add),
                            waits=([eq] if i == 0 else []), inc="dve")
                    cols = (4 * kc, 4 * kc + 4)
                    if tt == 0:
                        ev_kn_prev = P.op("vector", lambda e, cols=cols: e.tensor_copy(
                            kn2b.ap()[:, cols[0]:cols[1]], kn2a.ap()[:, cols[0]:cols[1]]),
                            waits=[er], inc="dve")
                    else:
                        src_t = kn2b if tt % 2 == 1 else kn2c
                        dst_t = kn2c if tt % 2 == 1 else kn2b
                        ev_kn_prev = P.op("vector",
                                          lambda e, cols=cols, s=src_t, d=dst_t: e.tensor_tensor(
                            out=d.ap()[:, cols[0]:cols[1]], in0=s.ap()[:, cols[0]:cols[1]],
                            in1=kn2a.ap()[:, cols[0]:cols[1]], op=AluOpType.max),
                            waits=[er, ev_kn_prev], inc="dve")
                else:
                    ev_d = P.op("sync", lambda e, kr=kr, kc=kc, tt=tt: e.dma_start(
                        out=vp_in.ap()[tt * 128:(tt + 1) * 128,
                                       (kc - 2) * 512:(kc - 2) * 512 + 512],
                        in_=qstv[:, kr, :]), waits=[ev_ev], inc="ds", by=16)
                    ev_vp_dma.append(ev_d)
                    kst_dma[idx] = ev_d
            ev_wslot_free[slot] = mm
            if kc + 1 < 4:
                crow = wprep_finish(nxt, kvsc_row, (kc + 1) * 512)
                emit_kv_c(kc + 1, crow)
            else:
                q_crow0 = wprep_finish(nxt, qsc_row, 0)
        ev_pF_read = st["pF_read"]

        knfin = kn2c
        et1 = P.op("tensor", lambda e: e.transpose(
            pF.ap()[0:8, 0:128], knfin.ap()[:, 0:8], sb_idf.ap()[:, :]),
            waits=[ev_kn_prev, ev_pF_read], inc="pe")
        et2 = P.op("vector", lambda e: e.tensor_copy(knloc.ap()[:, :], pF.ap()[0:8, 0:128]),
                   waits=[et1], inc="dve")
        et3 = P.op("vector", lambda e: e.tensor_reduce(
            out=knlocm.ap()[:, :], in_=knloc.ap()[:, :], axis=mybir.AxisListType.X,
            op=AluOpType.max), waits=[et2], inc="dve")
        et4 = P.op("tensor", lambda e: e.transpose(
            pF.ap()[0:1, 0:8], knlocm.ap()[:, :], sb_idf.ap()[0:8, 0:8]),
            waits=[et3], inc="pe")
        et5 = P.op("vector", lambda e: e.tensor_copy(knlocb.ap()[:, :], pF.ap()[0:1, 0:8]),
                   waits=[et4], inc="dve")
        ev_pF_read = et5
        ev_knrow_dma = P.op("sync", lambda e: e.dma_start(
            out=kp_in.ap()[T:T + 1, 0:8], in_=knlocb.ap()[:, :]),
            waits=[et5], inc="ds", by=16)

        P.wait("gpsimd", ev_kp_dma[-1], ev_knrow_dma)
        ev_ccK = P.op("gpsimd", lambda e: e.collective_compute(
            "AllGather", mybir.AluOpType.bypass,
            replica_groups=[[0, 1, 2, 3], [4, 5, 6, 7]],
            ins=[kp_in.ap().opt()], outs=[kp_out.ap().opt()]), inc="cc")
        P.wait("gpsimd", ev_vp_dma[-1])
        ev_ccV = P.op("gpsimd", lambda e: e.collective_compute(
            "AllGather", mybir.AluOpType.bypass,
            replica_groups=[[0, 1, 2, 3], [4, 5, 6, 7]],
            ins=[vp_in.ap().opt()], outs=[vp_out.ap().opt()]), inc="cc")

        phase_bar("tensor", "scalar", "vector")
        ev_slot_dma = {}
        c_act_evac = {}
        ev_erm_prev = None
        ev_qT_dma_last = None
        qpd = {0: kvpd[4]}
        ecqk_map = {}

        def emit_q_c(qc, ev_crow):
            evt = None
            for i in range(4):
                evt = P.op("tensor", lambda e, i=i: e.transpose(
                    pF.ap()[:, i:i + 1], rowC.ap()[0:1, i * 128:(i + 1) * 128],
                    sb_idf.ap()[0:1, 0:1]),
                    waits=([ev_crow, st["pF_read"], ev_erm_prev] if i == 0 else []),
                    inc=("pe" if i == 3 else None))
            ecq = P.op("vector", lambda e, qc=qc: e.tensor_copy(
                cqcols.ap()[:, 4 * qc:4 * qc + 4], pF.ap()[:, 0:4]), waits=[evt], inc="dve")
            st["pF_read"] = ecq
            st["rowC_free"] = ecq
            return P.op("vector", lambda e, qc=qc: e.tensor_scalar(
                out=cqk.ap()[:, 4 * qc:4 * qc + 4], in0=cqcols.ap()[:, 4 * qc:4 * qc + 4],
                scalar1=ckcols.ap()[:, qc:qc + 1], scalar2=None, op0=AluOpType.mult),
                waits=[ecq, ev_ck.get(0), ev_ck.get(1)], inc="dve")

        st["pF_read"] = ev_pF_read
        ecqk_map[0] = emit_q_c(0, q_crow0)
        for qc in range(8):
            slot = qc % 2
            ev_sign = qpd[qc]["sign"]
            ecqk = ecqk_map[qc]
            if qc + 1 < 8:
                qpd[qc + 1] = wprep_new(wqT, (qc + 1) * 512, (qc + 1) % 2,
                                        ev_wslot_free.get((qc + 1) % 2))
            mm = None
            for ms in range(4):
                m = 4 * qc + ms
                for th in range(2):
                    idx = m * 2 + th
                    g8 = ms * 2 + th
                    pk = paks[idx % 4]
                    for dc in range(DC):
                        mm = P.op("tensor", lambda e, dc=dc, slot=slot, ms=ms, th=th, pk=pk:
                                  e.matmul(pk, wring[:, slot, dc, ms * 128:(ms + 1) * 128],
                                           xnT[:, dc, th * 512:(th + 1) * 512],
                                           start=(dc == 0), stop=(dc == DC - 1)),
                                  waits=([qpd[qc]["sign"] if g8 == 0 else None,
                                          ev_slot_dma.get(idx - 4),
                                          ev_kst_free.get(28 + idx) if idx < 4 else None]
                                         if dc == 0 else []),
                                  inc=("pe" if dc == DC - 1 else None))
                    if qc + 1 < 8:
                        wprep_sched(qpd[qc + 1], g8)
                    esq = P.op("scalar", lambda e, m=m, pk=pk, ecqk=ecqk: e.activation(
                        out=q2v, in_=pk, func=AF.Square,
                        scale=cqk.ap()[:, m:m + 1]),
                        waits=[mm, ecqk,
                               et3 if idx == 0 else None,
                               ev_erm_prev if idx > 0 else None], inc="act")
                    kr = idx % 8
                    eev = P.op("scalar", lambda e, kr=kr, m=m, pk=pk: e.activation(
                        out=qstv[:, kr, :], in_=pk, func=AF.Copy,
                        scale=cqk.ap()[:, m:m + 1]),
                        waits=[ev_slot_dma.get(idx - 8),
                               ev_vp_dma[-1] if idx == 0 else None,
                               kst_dma.get(4 * NT - 4 + idx)], inc="act")
                    ed = P.op("sync", lambda e, kr=kr, m=m, th=th: e.dma_start(
                        out=qT_dram.ap()[m, :, th * 512:(th + 1) * 512], in_=qstv[:, kr, :]),
                        waits=[eev], inc="ds", by=16)
                    ev_slot_dma[idx] = ed
                    ev_qT_dma_last = ed
                    c_act_evac[idx] = eev
                    eqm = P.op("tensor", lambda e: e.matmul(
                        pF.ap()[0:1, 0:512], sb_onesc.ap()[:, :], q2v,
                        start=True, stop=True),
                        waits=[esq, st["pF_read"],
                               ev_erm_prev if idx > 0 else None], inc="pe")
                    dst = qn2A if th == 0 else qn2B
                    ev_erm_prev = P.op("vector", lambda e, m=m, dst=dst: e.tensor_reduce(
                        out=dst.ap()[0:1, m:m + 1], in_=pF.ap()[0:1, 0:512],
                        axis=mybir.AxisListType.X, op=AluOpType.max),
                        waits=[eqm, ev_qn2i], inc="dve")
            ev_wslot_free[slot] = mm
            if qc + 1 < 8:
                crow = wprep_finish(qpd[qc + 1], qsc_row, (qc + 1) * 512)
                ecqk_map[qc + 1] = emit_q_c(qc + 1, crow)
        ev_pF_read = ev_erm_prev
        ev_C_qn2 = ev_erm_prev
        ev_pF_read = ev_erm_prev

        phase_bar("tensor", "scalar", "vector")
        ev_fst_free = {}
        st["scx_free"] = {}
        fst_dma = {}
        st["bf_prev"] = None
        ev_ffn_last = None
        brow = [rowB, rowB2]
        bias_dma = {}

        def emit_bias_dma(fc):
            bias_dma[fc] = P.op("sync", lambda e, fc=fc: e.dma_start(
                out=brow[fc % 2].ap()[:, :],
                in_=bias_row.ap()[0:1, fc * 512:fc * 512 + 512]),
                waits=[st.get(("bm", fc - 2))], inc="ds", by=16)

        def emit_bias(fc):
            ev_bm = P.op("tensor", lambda e, fc=fc: e.matmul(
                pF.ap()[:, 0:512], sb_onesr.ap()[:, :], brow[fc % 2].ap()[:, :],
                start=True, stop=True),
                waits=[bias_dma[fc], st.get("bf_prev"),
                       ev_pF_read if fc == 0 else None], inc="pe")
            st[("bm", fc)] = ev_bm
            st["bf_prev"] = P.op("vector", lambda e, fc=fc: e.tensor_copy(
                coF.ap()[:, (fc % 2) * 512:(fc % 2) * 512 + 512], pF.ap()[:, 0:512]),
                waits=[ev_bm], inc="dve")
            return st["bf_prev"]

        bias_map = {0: None}
        for fc in range(8):
            slot = fc % 2
            ev_stt = None
            for g in range(8):
                rr = g % 2
                G = st["graw"]
                st["graw"] += 1
                evc = P.op("gpsimd", lambda e, g=g, rr=rr, fc=fc: e.dma_start(
                    out=rawv[:, rr, :].rearrange("p (c n) -> p c n", c=4),
                    in_=codesT.ap().rearrange("(c p) n -> p c n", p=128)[
                        :, 4 * g:4 * g + 4, fc * 512:fc * 512 + 512]),
                    waits=[st["raw_free"].get(G - 2)], inc="dg", by=16)
                evs = P.op("sync", lambda e, g=g, rr=rr, fc=fc: e.dma_start(
                    out=scxv[:, rr, :].rearrange("p (c n) -> p c n", c=4),
                    in_=scexpT.ap().rearrange("(c p) n -> p c n", p=128)[
                        :, 4 * g:4 * g + 4, fc * 512:fc * 512 + 512]),
                    waits=[st["scx_free"].get((fc, g))], inc="ds", by=16)
                ev_stt = P.op("vector", lambda e, g=g, rr=rr, slot=slot: e.scalar_tensor_tensor(
                    out=wring[:, slot, 4 * g:4 * g + 4, :],
                    in0=rawv[:, rr, :].rearrange("p (c n) -> p c n", c=4), scalar=128.0,
                    in1=scxv[:, rr, :].rearrange("p (c n) -> p c n", c=4),
                    op0=AluOpType.subtract, op1=AluOpType.mult),
                    waits=[evc, evs, ev_wslot_free.get(slot) if g == 0 else None],
                    inc="dve")
                st["raw_free"][G] = ev_stt
                st["scx_free"][(fc, g + 2)] = st["scx_free"][(fc + 1, g - 6)] = ev_stt
            if fc == 0:
                emit_bias_dma(0)
                emit_bias_dma(1)
            elif fc + 1 < 8:
                emit_bias_dma(fc + 1)
            ev_bf = emit_bias(fc)
            mm = None
            for tt in range(NT):
                pk = paks[tt % 4]
                for dc in range(DC):
                    mm = P.op("tensor", lambda e, dc=dc, tt=tt, slot=slot, pk=pk: e.matmul(
                        pk, xnT[:, dc, tt * 128:(tt + 1) * 128], wring[:, slot, dc, :],
                        start=(dc == 0), stop=(dc == DC - 1)),
                        waits=([ev_stt, ev_fst_free.get((fc, tt - 4)),
                                c_act_evac.get(60 + tt) if (fc == 0 and tt < 4) else None]
                               if dc == 0 else []),
                        inc=("pe" if dc == DC - 1 else None))
                kr = tt % 4
                eev = P.op("vector", lambda e, kr=kr, slot=slot, pk=pk: e.scalar_tensor_tensor(
                    out=sctF[:, kr, :], in0=pk, scalar=0.0,
                    in1=coF.ap()[:, slot * 512:slot * 512 + 512],
                    op0=AluOpType.add, op1=AluOpType.add),
                    waits=[mm, ev_bf, fst_dma.get((fc, tt - 4)),
                           fst_dma.get((fc - 1, tt + 4))], inc="dve")
                ed = P.op("sync", lambda e, kr=kr, fc=fc, tt=tt: e.dma_start(
                    out=ffn_dram.ap()[tt * 128:(tt + 1) * 128, fc * 512:fc * 512 + 512],
                    in_=sctF[:, kr, :]), waits=[eev], inc="ds", by=16)
                ev_fst_free[(fc, tt)] = ed
                fst_dma[(fc, tt)] = ed
                ev_ffn_last = ed
            ev_wslot_free[slot] = mm
        ev_E1_done = ev_ffn_last
        ev_pF_read = st["bf_prev"]

        eg1 = P.op("sync", lambda e: e.dma_start(
            out=kngb.ap()[:, :],
            in_=kp_out.ap().rearrange("(r t) n -> r t n", r=4)[:, T, 0:8]),
            waits=[ev_ccK], inc="ds", by=16)
        eg2 = P.op("tensor", lambda e: e.transpose(
            trb[3][0:8, 0:4], kngb.ap()[:, :], sb_id.ap()[0:4, 0:4]),
            waits=[eg1, ev_pF_read], inc="pe")
        eg3 = P.op("vector", lambda e: e.tensor_copy(
            knloc.ap()[0:8, 0:4], trb[3][0:8, 0:4]), waits=[eg2], inc="dve")
        eg4 = P.op("vector", lambda e: e.tensor_reduce(
            out=knlocm.ap()[:, :], in_=knloc.ap()[0:8, 0:4], axis=mybir.AxisListType.X,
            op=AluOpType.max), waits=[eg3], inc="dve")
        eg5 = P.op("tensor", lambda e: e.transpose(
            pF.ap()[0:1, 0:8], knlocm.ap()[:, :], sb_idf.ap()[0:8, 0:8]),
            waits=[eg4, eg3], inc="pe")
        eg6 = P.op("vector", lambda e: e.tensor_copy(knrow8.ap()[:, :], pF.ap()[0:1, 0:8]),
                   waits=[eg5], inc="dve")
        en1 = P.op("vector", lambda e: e.tensor_tensor(
            out=qn2T.ap()[:, :], in0=qn2A.ap()[:, :], in1=qn2B.ap()[:, :],
            op=AluOpType.max), waits=[ev_C_qn2], inc="dve")
        en2 = None
        for r in range(4):
            en2 = P.op("vector", lambda e, r=r: e.tensor_tensor(
                out=qn2T.ap().rearrange("o (j r) -> o j r", r=4)[:, :, r],
                in0=qn2T.ap().rearrange("o (j r) -> o j r", r=4)[:, :, r],
                in1=knrow8.ap()[:, :], op=AluOpType.mult),
                waits=([en1, eg6] if r == 0 else []), inc="dve")
        en3 = P.op("vector", lambda e: e.tensor_scalar(
            out=qn2T.ap()[:, :], in0=qn2T.ap()[:, :], scalar1=1e-30, scalar2=None,
            op0=AluOpType.max), waits=[en2], inc="dve")
        en4 = P.op("scalar", lambda e: e.activation(
            out=qn2T.ap()[:, :], in_=qn2T.ap()[:, :], func=AF.Sqrt),
            waits=[en3], inc="act")
        en5 = P.op("vector", lambda e: e.tensor_scalar(
            out=qn2T.ap()[:, :], in0=qn2T.ap()[:, :], scalar1=-RSQ, scalar2=None,
            op0=AluOpType.mult), waits=[en4], inc="dve")
        en6 = P.op("tensor", lambda e: e.matmul(
            pF.ap()[:, 0:32], sb_onesr.ap()[:, :], qn2T.ap()[:, :], start=True, stop=True),
            waits=[en5, eg6], inc="pe")
        ev_negC = P.op("vector", lambda e: e.tensor_copy(negC.ap()[:, :], pF.ap()[:, 0:32]),
                       waits=[en6], inc="dve")
        ev_pF_read = ev_negC
        if debug:
            P.op("sync", lambda e: e.dma_start(out=dbg["ffn"].ap()[:, :],
                 in_=ffn_dram.ap()[:, :]), waits=[ev_E1_done], inc="ds", by=16)
            P.op("sync", lambda e: e.dma_start(
                out=dbg["qT"].ap().rearrange("(h p) t -> h p t", h=H),
                in_=qT_dram.ap()[:, :, :]), waits=[ev_qT_dma_last], inc="ds", by=16)
            P.op("sync", lambda e: e.dma_start(out=dbg["kp"].ap()[:, :], in_=kp_out.ap()[:, :]),
                 waits=[ev_ccK], inc="ds", by=16)
            P.op("sync", lambda e: e.dma_start(out=dbg["vp"].ap()[:, :], in_=vp_out.ap()[:, :]),
                 waits=[ev_ccV], inc="ds", by=16)
            P.op("sync", lambda e: e.dma_start(out=dbg["negc"].ap()[:, :], in_=negC.ap()[:, :]),
                 waits=[ev_negC], inc="ds", by=16)
            P.op("sync", lambda e: e.dma_start(out=dbg["kn2"].ap()[:, :], in_=knrow8.ap()[:, :]),
                 waits=[eg6], inc="ds", by=16)
            P.op("sync", lambda e: e.dma_start(out=dbg["qn2"].ap()[:, :], in_=qn2T.ap()[:, :]),
                 waits=[en5], inc="ds", by=16)
            P.op("sync", lambda e: e.dma_start(out=dbg["cvrow"].ap()[:, :], in_=cvrow.ap()[:, :]),
                 waits=[ev_cv.get(2), ev_cv.get(3)], inc="ds", by=16)
            P.op("sync", lambda e: e.dma_start(out=dbg["cqk"].ap()[:, :], in_=cqk.ap()[:, :]),
                 waits=[ev_C_qn2], inc="ds", by=16)

        phase_bar("tensor", "scalar", "vector")
        ev_kj = {}
        ev_vj = {}
        ev_qh = {}
        ev_kj_free = {}
        ev_vj_free = {}
        ev_qh_free = {}
        ev_pt_free = {}
        st["pts_free"] = {}
        st["ptsum"] = {}
        ev_exp = {}
        ev_expA = {}
        ev_tt = {0: None, 1: None}
        ev_ez = {0: None, 1: None}
        ev_eo2 = {0: None, 1: None}
        gsc = 0
        for j in range(KVH):
            rj = j % 2
            for r4 in range(4):
                ev_kj[j] = P.op("sync", lambda e, j=j, rj=rj, r4=r4: e.dma_start_transpose(
                    out=kjT[:, rj, r4 * 1024:(r4 + 1) * 1024],
                    in_=kp_out.ap().rearrange("(r t) n -> r t n", r=4)[
                        r4, 0:1024, j * 128:(j + 1) * 128]),
                    waits=[ev_ccK if (j == 0 and r4 == 0) else None,
                           ev_E1_done if (j < 2 and r4 == 0) else None,
                           ev_kj_free.get(j - 2) if r4 == 0 else None],
                    inc="ds", by=16)
            ev_vj[j] = P.op("sync", lambda e, j=j, rj=rj: e.dma_start(
                out=vjv[:, rj, :, :],
                in_=vp_out.ap().rearrange("(s p) o -> p s o", p=128)[
                    :, :, j * 128:(j + 1) * 128]),
                waits=[ev_ccV if j == 0 else None,
                       ev_E1_done if j < 2 else None,
                       ev_vj_free.get(j - 2)], inc="ds", by=16)
            for h4 in range(4):
                h = 4 * j + h4
                rh = h % 2
                ev_qh[h] = P.op("sync", lambda e, h=h, rh=rh: e.dma_start(
                    out=qthv[:, rh, :], in_=qT_dram.ap()[h, :, :]),
                    waits=[ev_qh_free.get(h - 2),
                           ev_E1_done if h < 2 else None,
                           ev_qT_dma_last if h == 0 else None], inc="ds", by=16)
                def emit_scores(sc):
                    par = sc % 2
                    pX = pA if par == 0 else pB
                    emm0 = P.op("tensor", lambda e, sc=sc, pX=pX, rj=rj, rh=rh: e.matmul(
                        pX.ap()[:, 0:512], kjT[:, rj, sc * 128:(sc + 1) * 128],
                        qthv[:, rh, 0:512], start=True, stop=True),
                        waits=([ev_expA.get(gsc + sc - 2),
                                ev_qh[h] if sc == 0 else None,
                                ev_kj[j] if (h4 == 0 and sc == 0) else None,
                                ev_negC if (j == 0 and h4 == 0 and sc == 0) else None,
                                ev_eo2[par] if sc < 2 else None]), inc="pe")
                    ev_expA[gsc + sc] = P.op("scalar", lambda e, sc=sc, pX=pX, h=h:
                                             e.activation(
                        out=ptv[:, sc % 4, 0:512], in_=pX.ap()[:, 0:512], func=AF.Exp,
                        scale=RSQ, bias=negC.ap()[:, h:h + 1]),
                        waits=[emm0] + list(ev_pt_free.get(gsc + sc - 4) or ()), inc="act")
                    emm1 = P.op("tensor", lambda e, sc=sc, pX=pX, rj=rj, rh=rh: e.matmul(
                        pX.ap()[:, 512:1024], kjT[:, rj, sc * 128:(sc + 1) * 128],
                        qthv[:, rh, 512:1024], start=True, stop=True),
                        waits=[ev_exp.get(gsc + sc - 2)], inc="pe")
                    ev_exp[gsc + sc] = P.op("scalar", lambda e, sc=sc, pX=pX, h=h:
                                            e.activation(
                        out=ptv[:, sc % 4, 512:1024], in_=pX.ap()[:, 512:1024],
                        func=AF.Exp, scale=RSQ, bias=negC.ap()[:, h:h + 1]),
                        waits=[emm1], inc="act")

                def emit_av(sc):
                    P.op("tensor", lambda e, sc=sc, rj=rj: e.matmul(
                        pC.ap()[:, :], vjv[:, rj, sc, :], ptv[:, sc % 4, 0:512],
                        start=(sc == 0), stop=(sc == DC - 1)),
                        waits=[ev_expA[gsc + sc],
                               ev_tt[0] if sc == 0 else None,
                               ev_vj[j] if (h4 == 0 and sc == 0) else None])
                    ev_av = P.op("tensor", lambda e, sc=sc, rj=rj: e.matmul(
                        pD.ap()[:, :], vjv[:, rj, sc, :], ptv[:, sc % 4, 512:1024],
                        start=(sc == 0), stop=(sc == DC - 1)),
                        waits=[ev_tt[1] if sc == 0 else None], inc="pe")
                    return ev_av

                def emit_ptsum(p):
                    sc = 2 * p + 1
                    pr = p % 2
                    ev_pts = P.op("vector", lambda e, sc=sc, pr=pr: e.tensor_tensor(
                        out=ptsv[:, pr, :], in0=ptv[:, (sc - 1) % 4, :],
                        in1=ptv[:, sc % 4, :], op=AluOpType.add),
                        waits=[ev_exp[gsc + sc], st["pts_free"].get(gsc // 2 + p - 2)],
                        inc="dve")
                    st["ptsum"][gsc // 2 + p] = ev_pts
                    ev_pt_free[gsc + sc - 1] = ev_pt_free[gsc + sc] = (avs[sc], ev_pts)

                def emit_zpair(p):
                    pr = p % 2
                    P.op("tensor", lambda e, p=p, pr=pr: e.matmul(
                        pE.ap()[0:1, 0:512], sb_onesc.ap()[:, :], ptsv[:, pr, 0:512],
                        start=(p == 0), stop=(p == 15)),
                        waits=[st["ptsum"][gsc // 2 + p],
                               ev_ez[0] if p == 0 else None])
                    ev_z1 = P.op("tensor", lambda e, p=p, pr=pr: e.matmul(
                        pF.ap()[0:1, 0:512], sb_onesc.ap()[:, :], ptsv[:, pr, 512:1024],
                        start=(p == 0), stop=(p == 15)),
                        waits=[ev_ez[1] if p == 0 else None], inc="pe")
                    st["pts_free"][gsc // 2 + p] = ev_z1
                    return ev_z1

                avs = {}
                for sc in range(DC):
                    emit_scores(sc)
                    if sc >= 1:
                        avs[sc - 1] = emit_av(sc - 1)
                        if (sc - 1) % 2 == 1:
                            p = (sc - 1) // 2
                            emit_ptsum(p)
                            if p >= 1:
                                emit_zpair(p - 1)
                avs[DC - 1] = emit_av(DC - 1)
                emit_zpair(DC // 2 - 2)
                emit_ptsum(DC // 2 - 1)
                ev_avz = emit_zpair(DC // 2 - 1)
                ev_avz = st["pts_free"][(gsc + DC - 1) // 2]
                gsc += DC
                for thh, (pZ, zr, rzr, pO, pOut, sl) in enumerate(
                        ((pE, zrow, rzrow, pC, pA, 0), (pF, zrow2, rzrow2, pD, pB, 1))):
                    ez1 = P.op("vector", lambda e, pZ=pZ, zr=zr: e.tensor_copy(
                        zr.ap()[:, :], pZ.ap()[0:1, 0:512]), waits=[ev_avz], inc="dve")
                    ev_ez[thh] = ez1
                    el1 = P.op("scalar", lambda e, zr=zr, rzr=rzr: e.activation(
                        out=rzr.ap()[:, :], in_=zr.ap()[:, :], func=AF.Ln),
                        waits=[ez1], inc="act")
                    el2 = P.op("scalar", lambda e, zr=zr, rzr=rzr: e.activation(
                        out=zr.ap()[:, :], in_=rzr.ap()[:, :], func=AF.Exp, scale=-1.0),
                        waits=[el1], inc="act")
                    eo1 = P.op("tensor", lambda e, j=j, zr=zr, pOut=pOut: e.matmul(
                        pOut.ap()[:, 0:512], cvrow.ap()[0:1, j * 128:(j + 1) * 128],
                        zr.ap()[:, :], start=True, stop=True),
                        waits=[el2, ev_expA[gsc - 2 + thh],
                               ev_cv.get(2) if (j == 0 and h4 == 0) else None,
                               ev_cv.get(3) if (j == 0 and h4 == 0) else None], inc="pe")
                    eo2 = P.op("vector", lambda e, pOut=pOut, sl=sl: e.tensor_copy(
                        sctF[:, sl, :], pOut.ap()[:, 0:512]), waits=[eo1], inc="dve")
                    ev_eo2[thh] = eo2
                    ev_tt[thh] = P.op("vector", lambda e, h=h, thh=thh, pO=pO, sl=sl:
                                      e.tensor_tensor(
                        out=attnT[:, h, thh * 512:(thh + 1) * 512], in0=pO.ap()[:, :],
                        in1=sctF[:, sl, :], op=AluOpType.mult), waits=[eo2], inc="dve")
                ev_qh_free[h] = ev_tt[1]
            ev_kj_free[j] = ev_tt[1]
            ev_vj_free[j] = ev_tt[1]
        ev_D_done = ev_tt[1]
        if debug:
            P.op("sync", lambda e: e.dma_start(out=dbg["attnT"].ap()[:, :], in_=arenaA.ap()[:, :]),
                 waits=[ev_D_done], inc="ds", by=16)

        phase_bar("tensor", "scalar", "vector")
        seed_ring_barrier(ev_D_done)
        ev_x2 = {}
        ev_f2 = {}
        ev_x2_free = {}
        ev_out_dma = {}
        ev_cof_prev = None
        opd = {0: wprep_new(woT, 0, 0, ev_D_done)}
        wprep_all(opd[0])
        o_crow = {0: wprep_finish(opd[0], osc_row, 0)}

        def emit_cof(oc, ev_crow):
            nonlocal ev_cof_prev
            ecm = P.op("tensor", lambda e: e.matmul(
                pF.ap()[:, 0:512], sb_onesr.ap()[:, :], rowC.ap()[:, :],
                start=True, stop=True),
                waits=[ev_crow, ev_cof_prev, ev_pF_read if oc == 0 else None], inc="pe")
            ecf = P.op("vector", lambda e, oc=oc: e.tensor_copy(
                coF.ap()[:, (oc % 2) * 512:(oc % 2) * 512 + 512], pF.ap()[:, 0:512]),
                waits=[ecm], inc="dve")
            ev_cof_prev = ecf
            st["rowC_free"] = ecm
            return ecf

        cof_map = {0: emit_cof(0, o_crow[0])}
        for oc in range(8):
            slot = oc % 2
            ev_sign = opd[oc]["sign"]
            ecf = cof_map[oc]
            if oc + 1 < 8:
                opd[oc + 1] = wprep_new(woT, (oc + 1) * 512, (oc + 1) % 2,
                                        ev_wslot_free.get((oc + 1) % 2))
            mm = None
            for tt in range(NT):
                idx = oc * NT + tt
                rx = idx % 2
                ev_x2[idx] = P.op("sync", lambda e, rx=rx, oc=oc, tt=tt: e.dma_start(
                    out=xeE2[:, rx, :],
                    in_=x_loc.ap()[tt * 128:(tt + 1) * 128, oc * 512:oc * 512 + 512]),
                    waits=[ev_out_dma.get(idx - 2),
                           ev_E1_done if idx < 2 else None], inc="ds", by=16)
                ev_f2[idx] = P.op("sync", lambda e, rx=rx, oc=oc, tt=tt: e.dma_start(
                    out=xeE2[:, 2 + rx, :],
                    in_=ffn_dram.ap()[tt * 128:(tt + 1) * 128, oc * 512:oc * 512 + 512]),
                    waits=[ev_out_dma.get(idx - 2),
                           ev_E1_done if idx == 0 else None], inc="ds", by=16)
                pk = paks[tt % 4]
                for ac in range(H):
                    mm = P.op("tensor", lambda e, ac=ac, tt=tt, slot=slot, pk=pk: e.matmul(
                        pk, attnT[:, ac, tt * 128:(tt + 1) * 128], wring[:, slot, ac, :],
                        start=(ac == 0), stop=(ac == H - 1)),
                        waits=([opd[oc]["sign"] if tt == 0 else None,
                                ev_D_done if idx == 0 else None,
                                ev_out_dma.get(idx - 4)] if ac == 0 else []),
                        inc=("pe" if ac == H - 1 else None))
                if oc + 1 < 8:
                    wprep_sched(opd[oc + 1], tt)
                kr = tt % 2
                ecb1 = P.op("vector", lambda e, kr=kr, oc=oc, pk=pk: e.tensor_tensor(
                    out=sctF[:, 2 * kr, :], in0=pk,
                    in1=coF.ap()[:, (oc % 2) * 512:(oc % 2) * 512 + 512],
                    op=AluOpType.mult), waits=[mm, ecf], inc="dve")
                ecb2 = P.op("vector", lambda e, kr=kr, rx=rx: e.tensor_tensor(
                    out=sctF[:, 2 * kr + 1, :], in0=sctF[:, 2 * kr, :], in1=xeE2[:, rx, :],
                    op=AluOpType.add), waits=[ecb1, ev_x2[idx]], inc="dve")
                ev_x2_free[idx] = ecb2
                ecb3 = P.op("vector", lambda e, kr=kr, rx=rx: e.tensor_tensor(
                    out=xeE2[:, 4 + rx, :], in0=sctF[:, 2 * kr + 1, :],
                    in1=xeE2[:, 2 + rx, :], op=AluOpType.add),
                    waits=[ecb2, ev_f2[idx], ev_out_dma.get(idx - 2)], inc="dve")
                last_out = P.op("sync", lambda e, rx=rx, oc=oc, tt=tt: e.dma_start(
                    out=out_d.ap()[tt * 128:(tt + 1) * 128, oc * 512:oc * 512 + 512],
                    in_=xeE2[:, 4 + rx, :]), waits=[ecb3], inc="ds", by=16)
                ev_out_dma[idx] = last_out
            ev_wslot_free[slot] = mm
            if oc + 1 < 8:
                crow = wprep_finish(opd[oc + 1], osc_row, (oc + 1) * 512)
                cof_map[oc + 1] = emit_cof(oc + 1, crow)

        P.wait("sync", ("ds", P.cnt["ds"]))
        P.wait("gpsimd", ("dg", P.cnt["dg"]))

        @block.tensor
        def _(t):
            for f in P.prog["tensor"]:
                f(t, sems)

        @block.vector
        def _(v):
            for f in P.prog["vector"]:
                f(v, sems)

        @block.scalar
        def _(s):
            for f in P.prog["scalar"]:
                f(s, sems)

        @block.sync
        def _(s):
            for f in P.prog["sync"]:
                f(s, sems)

        @block.gpsimd
        def _(g):
            for f in P.prog["gpsimd"]:
                f(g, sems)

    return nc, P


def host_inputs(inputs):
    x = np.ascontiguousarray(np.asarray(inputs["L0_attn_hidden_states"], np.float32))
    wq = np.asarray(inputs["L0_attn_q_weight"], np.float32)
    wk = np.asarray(inputs["L0_attn_k_weight"], np.float32)
    wv = np.asarray(inputs["L0_attn_v_weight"], np.float32)
    wo = np.asarray(inputs["L0_attn_o_weight"], np.float32)
    qs = np.asarray(inputs["L0_attn_q_scale"], np.float32)
    ks = np.asarray(inputs["L0_attn_k_scale"], np.float32)
    vs = np.asarray(inputs["L0_attn_v_scale"], np.float32)
    osc = np.asarray(inputs["L0_attn_o_scale"], np.float32)
    q_scales = np.asarray(inputs["L0_ffn_q_scales"], np.float32)
    q_vals = np.asarray(inputs["L0_ffn_q_vals"], np.int32)
    fb = np.asarray(inputs["L0_ffn_bias"], np.float32)

    com = dict(
        wqT=np.ascontiguousarray(wq.T),
        wkvT=np.ascontiguousarray(np.concatenate([wk.T, wv.T], axis=1)),
        woT=np.ascontiguousarray(wo.T),
        codesT=np.ascontiguousarray(q_vals.reshape(4096, 4096).T),
        scexpT=np.ascontiguousarray(np.repeat(q_scales.T, 32, axis=0)),
        qsc_row=np.ascontiguousarray(qs[None, :]),
        kvsc_row=np.ascontiguousarray(np.concatenate([ks, vs])[None, :]),
        osc_row=np.ascontiguousarray(osc[None, :]),
        bias_row=np.ascontiguousarray(fb[None, :]),
        identity=np.eye(128, dtype=NB),
        identity_f=np.eye(128, dtype=np.float32),
        onesc_d=np.ones((128, 1), NB),
        onesr_d=np.ones((1, 128), np.float32),
    )
    in_maps = []
    for c in range(8):
        b, q = c // 4, c % 4
        m = dict(com)
        m["x_loc"] = np.ascontiguousarray(x[b, q * T:(q + 1) * T, :])
        in_maps.append(m)
    return in_maps


def kernel(**inputs):
    in_maps = host_inputs(inputs)
    nc, _ = build_nc(debug=False)
    res = run_bass_kernel_spmd(nc, in_maps, core_ids=list(range(8)))
    out = np.empty((2, 4096, 4096), np.float32)
    for c in range(8):
        b, q = c // 4, c % 4
        out[b, q * T:(q + 1) * T, :] = res.results[c]["out"]
    return out


if __name__ == "__main__":
    nc, P = build_nc(debug=False)
    print("instructions (incl waits):", P.ni, "total:", sum(P.ni.values()))
    print("sem totals:", P.cnt)
